# version 12
# baseline (speedup 1.0000x reference)
import numpy as np
import concourse.bass as bass
import concourse.mybir as mybir
from concourse.bass_utils import run_bass_kernel_spmd

F32 = mybir.dt.float32
BF16 = mybir.dt.bfloat16
AF = mybir.ActivationFunctionType
OP = mybir.AluOpType
AX = mybir.AxisListType

H = 2048
HB = 16
T = 1024
TH = T + 3
SUB = 256
NSUB = T // SUB
NST = 16
DTR = 64
E = 16
I_FF = 1408
IS = 2816
EPS = 1e-5

RACE_SAFE = False
DEBUG = False
DEBUG_FRONT = False
ENGS = ["sync", "gpsimd", "tensor", "vector", "scalar"]


class Sem:
    __slots__ = ("h", "v", "name")

    def __init__(self, handle):
        self.h = handle
        self.v = 0

    def inc(self, n=1):
        self.v += n
        return self.v


class K:

    def __init__(self, nc, bufs, sem_handles):
        self.nc = nc
        self.b = bufs
        self._sh = sem_handles
        self.cur = None
        self.e = None
        self.nsteps_ref = None
        self.nsteps = 0
        self.mode = "emit"
        self.rec = None
        self.plan_chain = {}
        self.plan_wait = {}

    def start_scan_pass(self):
        self.mode = "scan"
        self.cur = "__scan__"
        self.e = None
        self.nsteps = 0
        self.eidx = {}
        self.rec = {e: [] for e in ENGS}
        for name, h in self._sh.items():
            sem = Sem(h)
            sem.name = name
            setattr(self, name, sem)
        self.chains = {e: Sem(self._sh["chain_" + e])
                       for e in ("vector", "scalar", "gpsimd")}
        for e in ("vector", "scalar", "gpsimd"):
            self.chains[e].name = "chain_" + e
            setattr(self, "chain_" + e, self.chains[e])

    def start_pass(self, eng_name, eng):
        self.mode = "emit"
        self.cur = eng_name
        self.e = eng
        self.nsteps = 0
        self.eidx = {}
        for name, h in self._sh.items():
            sem = Sem(h)
            sem.name = name
            setattr(self, name, sem)
        self.chains = {e: Sem(self._sh["chain_" + e])
                       for e in ("vector", "scalar", "gpsimd")}
        for e in ("vector", "scalar", "gpsimd"):
            self.chains[e].name = "chain_" + e
            setattr(self, "chain_" + e, self.chains[e])

    def end_pass(self):
        if self.nsteps_ref is None:
            self.nsteps_ref = self.nsteps
        else:
            assert self.nsteps == self.nsteps_ref, \
                f"emission diverged on pass {self.cur}: {self.nsteps} != {self.nsteps_ref}"

    def step(self, eng, emit, waits=(), inc=None, inc_n=1, drain=False):
        self.nsteps += 1
        i = self.eidx.get(eng, 0)
        self.eidx[eng] = i + 1
        if self.mode == "scan":
            self.rec[eng].append([bool(drain), inc.name if inc else None,
                                  inc.v + inc_n if inc else None, False])
            v = inc.inc(inc_n) if inc is not None else None
            return v
        chain = self.chains.get(eng)
        needs_chain = self.plan_chain.get(eng, {}).get(i, False)
        v = inc.inc(inc_n) if inc is not None else None
        if needs_chain:
            chain.inc(1)
        if self.cur == eng:
            for s, val in waits:
                if val > 0:
                    self.e.wait_ge(s.h, val)
            if drain:
                dw = self.plan_wait.get(eng, {}).get(i)
                if dw is None:
                    self.e.drain()
                else:
                    sname, sval = dw
                    if sval > 0:
                        self.e.wait_ge(self.sem_by_name(sname).h, sval)
            ins = emit(self.e)
            if inc is not None:
                assert ins is not None, "step with inc= must emit an instruction"
                ins.then_inc(inc.h, inc_n)
            if needs_chain:
                assert ins is not None and inc is None
                ins.then_inc(chain.h, 1)
        return v

    def sem_by_name(self, name):
        return getattr(self, name)

    def plan(self):
        self.plan_chain = {}
        self.plan_wait = {}
        ENGINE_SEMS = {"vec", "scl", "mm", "pool"}
        for eng, recs in self.rec.items():
            chain_marked = {}
            waits = {}
            for i, r in enumerate(recs):
                if r[0] and i > 0:
                    p = i - 1
                    while p >= 0 and recs[p][1] is not None \
                            and recs[p][1] not in ENGINE_SEMS:
                        p -= 1
                    if p < 0:
                        continue
                    if recs[p][1] in ENGINE_SEMS:
                        waits[i] = (recs[p][1], recs[p][2])
                    else:
                        chain_marked[p] = True
            cval = 0
            cvals = {}
            for p in sorted(chain_marked):
                cval += 1
                cvals[p] = cval
            for i, r in enumerate(recs):
                if r[0] and i > 0 and i not in waits:
                    p = i - 1
                    if p in cvals:
                        waits[i] = ("chain_" + eng, cvals[p])
            self.plan_chain[eng] = chain_marked
            self.plan_wait[eng] = waits


def ts(i, n=128):
    return slice(i * n, (i + 1) * n)



def emit_consts(k):
    b = k.b
    k.step("vector", lambda e: e.memset(b.ones[:], 1.0), inc=k.vec)
    b.v_consts_vec = k.vec.v
    k.step("sync", lambda e: e.dma_start(out=b.nw_col[:], in_=b.nw_in),
           inc=k.xload, inc_n=16)
    k.step("sync", lambda e: e.dma_start(out=b.convb_col[:], in_=b.convb_in),
           inc=k.xload, inc_n=16)
    k.step("sync", lambda e: e.dma_start(out=b.dtb_col[:], in_=b.dtb_in),
           inc=k.xload, inc_n=16)
    k.step("sync", lambda e: e.dma_start(out=b.d_col[:], in_=b.d_in),
           inc=k.xload, inc_n=16)
    k.step("sync", lambda e: e.dma_start(out=b.m_col[:], in_=b.mask_in),
           inc=k.xload, inc_n=16)
    k.step("sync", lambda e: e.dma_start(out=b.im_col[:], in_=b.imask_in),
           inc=k.xload, inc_n=16)
    k.step("sync", lambda e: e.dma_start(out=b.A_sbuf[:], in_=b.alog_in),
           inc=k.xload, inc_n=16)
    b.v_consts = k.xload.v
    b.v_dtw = k.step("gpsimd",
                     lambda e: e.dma_start(out=b.dtw_slab[:], in_=b.dtw_t),
                     inc=k.wdtw, inc_n=16)
    v = k.step("scalar",
               lambda e: e.activation(out=b.A_sbuf[:], in_=b.A_sbuf[:], func=AF.Exp),
               waits=[(k.xload, b.v_consts)], inc=k.scl)
    b.v_A = k.step("vector",
                   lambda e: e.tensor_scalar_mul(b.A_sbuf[:], b.A_sbuf[:], -1.0),
                   waits=[(k.scl, v)], inc=k.vec)


def emit_ln(k):
    b = k.b
    tslices = [(0, 512), (512, 512), (1024, 3)]

    v_mmblk = {}
    for f in range(HB):
        par = f % 2
        xl = k.xl0 if par == 0 else k.xl1
        vx = k.step(
            "sync",
            lambda e, f=f, par=par: e.dma_start(out=b.xblk[par][:],
                                                in_=b.x_fm[ts(f), :]),
            waits=[(k.mm, v_mmblk.get(f - 2, 0))], inc=xl, inc_n=16)
        vsq = k.step(
            "scalar",
            lambda e, f=f, par=par: e.activation(out=b.sq[:], in_=b.xblk[par][:],
                                                 func=AF.Square),
            waits=[(xl, vx), (k.mm, v_mmblk.get(f - 1, 0))], inc=k.scl)
        for tsi, (t0, tw) in enumerate(tslices):
            k.step(
                "tensor",
                lambda e, f=f, par=par, tsi=tsi, t0=t0, tw=tw: e.matmul(
                    b.pb[2 * tsi][0:1, 0:tw], b.ones[:, 0:1],
                    b.xblk[par][:, t0:t0 + tw],
                    start=(f == 0), stop=(f == HB - 1)),
                waits=[(xl, vx), (k.vec, b.v_consts_vec)] if tsi == 0 else ())
        for tsi, (t0, tw) in enumerate(tslices):
            vmb = k.step(
                "tensor",
                lambda e, f=f, tsi=tsi, t0=t0, tw=tw: e.matmul(
                    b.pb[2 * tsi + 1][0:1, 0:tw], b.ones[:, 0:1],
                    b.sq[:, t0:t0 + tw],
                    start=(f == 0), stop=(f == HB - 1)),
                waits=[(k.scl, vsq)] if tsi == 0 else (),
                inc=(k.mm if tsi == 2 else None))
        v_mmblk[f] = vmb
    v_sums = k.mm.v

    for tsi, (t0, tw) in enumerate(tslices):
        k.step("vector",
               lambda e, tsi=tsi, t0=t0, tw=tw: e.tensor_scalar_mul(
                   b.row_mean[0:1, t0:t0 + tw], b.pb[2 * tsi][0:1, 0:tw], 1.0 / H),
               waits=[(k.mm, v_sums)])
    k.step("vector",
           lambda e: e.scalar_tensor_tensor(
               out=b.row_var[0:1, :], in0=b.row_mean[0:1, :], scalar=-1.0,
               in1=b.row_mean[0:1, :], op0=OP.mult, op1=OP.mult),
           drain=True)
    for tsi, (t0, tw) in enumerate(tslices):
        k.step("vector",
               lambda e, tsi=tsi, t0=t0, tw=tw: e.scalar_tensor_tensor(
                   out=b.row_var[0:1, t0:t0 + tw], in0=b.pb[2 * tsi + 1][0:1, 0:tw],
                   scalar=1.0 / H, in1=b.row_var[0:1, t0:t0 + tw],
                   op0=OP.mult, op1=OP.add),
               drain=(tsi == 0))
    k.step("vector",
           lambda e: e.tensor_scalar_add(b.row_var[0:1, :], b.row_var[0:1, :], EPS),
           drain=True)
    v = k.step("vector",
               lambda e: e.reciprocal(b.row_rstd[0:1, :], b.row_var[0:1, :]),
               drain=True, inc=k.vec)
    b.v_ln_psum_done = v
    v = k.step("scalar",
               lambda e: e.activation(out=b.row_rstd[0:1, :], in_=b.row_rstd[0:1, :],
                                      func=AF.Sqrt),
               waits=[(k.vec, v)], inc=k.scl)
    v_mr = k.step("vector",
                  lambda e: e.scalar_tensor_tensor(
                      out=b.row_mr[0:1, :], in0=b.row_mean[0:1, :], scalar=-1.0,
                      in1=b.row_rstd[0:1, :], op0=OP.mult, op1=OP.mult),
                  waits=[(k.scl, v)], inc=k.vec)
    k.step("scalar",
           lambda e: e.dma_start(out=b.rows_dram[0:1, :], in_=b.row_rstd[0:1, :]),
           waits=[(k.vec, v_mr)], drain=True, inc=k.dmah, inc_n=16)
    v_r2 = k.step("scalar",
                  lambda e: e.dma_start(out=b.rows_dram[1:2, :], in_=b.row_mr[0:1, :]),
                  inc=k.dmah, inc_n=16)
    k.step("scalar",
           lambda e: e.dma_start(out=b.rstd_b[:],
                                 in_=b.rows_dram[0:1, :].partition_broadcast(128)),
           waits=[(k.dmah, v_r2)], inc=k.dmah, inc_n=16)
    v_bc = k.step("scalar",
                  lambda e: e.dma_start(out=b.mr_b[:],
                                        in_=b.rows_dram[1:2, :].partition_broadcast(128)),
                  inc=k.dmah, inc_n=16)

    v_vblk = {}
    for f in range(HB):
        par = f % 2
        xl = k.xl0 if par == 0 else k.xl1
        vx = k.step(
            "sync",
            lambda e, f=f, par=par: e.dma_start(out=b.xblk[par][:],
                                                in_=b.x_fm[ts(f), :]),
            waits=[(k.vec, v_vblk.get(f - 2, 0)), (k.mm, v_sums)],
            inc=xl, inc_n=16)
        k.step("vector",
               lambda e, f=f, par=par: e.tensor_tensor(
                   out=b.xn[:, f * TH:(f + 1) * TH], in0=b.xblk[par][:],
                   in1=b.rstd_b[:, 0:TH], op=OP.mult),
               waits=[(xl, vx), (k.dmah, v_bc)], drain=True)
        k.step("vector",
               lambda e, f=f: e.tensor_tensor(
                   out=b.xn[:, f * TH:(f + 1) * TH],
                   in0=b.xn[:, f * TH:(f + 1) * TH],
                   in1=b.mr_b[:, 0:TH], op=OP.add),
               drain=True)
        ssem = k.st0
        vadd = k.step("vector",
                      lambda e, f=f, par=par: e.tensor_tensor(
                          out=b.sq[:, 0:T], in0=b.xblk[par][:, 3:TH],
                          in1=b.xn[:, f * TH + 3:(f + 1) * TH], op=OP.add),
                      waits=[(ssem, ssem.v)], drain=True, inc=k.vec)
        v_vblk[f] = vadd
        k.step(
            "sync",
            lambda e, f=f: e.dma_start(out=b.out_fm[ts(f), :], in_=b.sq[:, 0:T]),
            waits=[(k.vec, vadd)], inc=ssem, inc_n=16)
    b.v_xn = k.vec.v
    b.v_out_init_st0 = k.st0.v


def emit_r2(k):
    b = k.b
    tslices = [(0, 512), (512, 512), (1024, 3)]
    prev_mm = 0
    for tsi, (t0, tw) in enumerate(tslices):
        bank = 6 if tsi != 1 else 7
        coff = 0 if tsi != 2 else 256
        for f in range(HB):
            vsq = k.step(
                "scalar",
                lambda e, f=f, t0=t0, tw=tw: e.activation(
                    out=b.sq[:, t0:t0 + tw],
                    in_=b.xn[:, f * TH + t0: f * TH + t0 + tw], func=AF.Square),
                waits=[(k.vec, b.v_xn), (k.mm, prev_mm), (k.st0, k.st0.v)],
                inc=k.scl)
            prev_mm = k.step(
                "tensor",
                lambda e, f=f, t0=t0, tw=tw, bank=bank, coff=coff: e.matmul(
                    b.pb[bank][0:1, coff:coff + tw], b.ones[:, 0:1],
                    b.sq[:, t0:t0 + tw],
                    start=(f == 0), stop=(f == HB - 1)),
                waits=[(k.scl, vsq)], inc=k.mm)
    v_sums = prev_mm
    for tsi, (t0, tw) in enumerate(tslices):
        bank = 6 if tsi != 1 else 7
        coff = 0 if tsi != 2 else 256
        k.step("vector",
               lambda e, tsi=tsi, t0=t0, tw=tw, bank=bank, coff=coff:
               e.tensor_scalar(out=b.row_var[0:1, t0:t0 + tw],
                               in0=b.pb[bank][0:1, coff:coff + tw],
                               scalar1=1.0 / H, scalar2=EPS,
                               op0=OP.mult, op1=OP.add),
               waits=[(k.mm, v_sums)])
    v = k.step("vector",
               lambda e: e.reciprocal(b.row_rstd[0:1, :], b.row_var[0:1, :]),
               drain=True, inc=k.vec)
    b.v_r2_psum_done = v
    v = k.step("scalar",
               lambda e: e.activation(out=b.row_rstd[0:1, :], in_=b.row_rstd[0:1, :],
                                      func=AF.Sqrt),
               waits=[(k.vec, v)], inc=k.scl)
    v_rd = k.step("scalar",
                  lambda e: e.dma_start(out=b.rows_dram[2:3, :], in_=b.row_rstd[0:1, :]),
                  drain=True, inc=k.dmah, inc_n=16)
    b.v_r2b = k.step("scalar",
                     lambda e: e.dma_start(
                         out=b.r2_b[:],
                         in_=b.rows_dram[2:3, :].partition_broadcast(128)),
                     waits=[(k.dmah, v_rd)], inc=k.dmah, inc_n=16)


IP_PHASES = [(i * 128, 128) for i in range(32)]


def emit_inproj(k):
    b = k.b
    tslices = [(0, 343), (343, 342), (685, 342)]
    v_mm_phase = {}
    v_scale_phase = {}
    for ph, (o0, W) in enumerate(IP_PHASES):
        par = ph % 2
        wl = k.wload0 if par == 0 else k.wload1
        grp = b.frq % 2
        b.frq += 1
        base = 4 * grp
        vw = k.step(
            "gpsimd",
            lambda e, o0=o0, W=W, par=par: e.dma_start(
                out=b.wslab[par][:, 0:16 * W].rearrange("p (kb w) -> p kb w", kb=16),
                in_=b.inw_t.rearrange("(kb p) o -> p kb o", p=128)[:, :, o0:o0 + W]),
            waits=[(k.mm, v_mm_phase.get(ph - 2, 0))], inc=wl, inc_n=16)
        for kb in range(16):
            k.step(
                "vector",
                lambda e, kb=kb, W=W, par=par: e.tensor_scalar_mul(
                    b.wslab[par][:, kb * W:(kb + 1) * W],
                    b.wslab[par][:, kb * W:(kb + 1) * W],
                    b.nw_col[:, kb:kb + 1]),
                waits=([(wl, vw), (k.mm, v_mm_phase.get(ph - 2, 0)),
                        (k.xload, b.v_consts)] if kb == 0 else ()),
                drain=(kb == 0), inc=(k.vec if kb == 15 else None))
        v_scale_phase[ph] = k.vec.v
        for tsi, (t0, tw) in enumerate(tslices):
            bank = base + tsi
            for kb in range(16):
                wts = []
                if tsi == 0 and kb == 0:
                    wts = [(k.vec, v_scale_phase[ph]),
                           (k.vec, b.fr_prev_vec[grp]),
                           (k.scl, b.fr_prev_scl[grp])]
                    if ph <= 1:
                        wts.append((k.vec, b.v_r2_psum_done))
                is_last = (tsi == 2 and kb == 15)
                k.step(
                    "tensor",
                    lambda e, tsi=tsi, kb=kb, t0=t0, tw=tw, W=W,
                           par=par, bank=bank: e.matmul(
                        b.pb[bank][0:128, 0:tw],
                        b.wslab[par][:, kb * W: kb * W + 128],
                        b.xn[:, kb * TH + t0: kb * TH + t0 + tw],
                        start=(kb == 0), stop=(kb == 15)),
                    waits=wts, inc=(k.mm if is_last else None))
        v_mm_phase[ph] = k.mm.v
        dest = b.xc_pre if o0 < H else b.z
        f = (o0 % H) // 128
        for tsi, (t0, tw) in enumerate(tslices):
            bank = base + tsi
            wts = [(k.mm, v_mm_phase[ph])]
            if tsi == 0:
                wts.append((k.dmah, b.v_r2b))
            k.step(
                "vector",
                lambda e, dest=dest, f=f, t0=t0, tw=tw, bank=bank:
                e.tensor_tensor(
                    out=dest[:, f * TH + t0: f * TH + t0 + tw],
                    in0=b.pb[bank][0:128, 0:tw],
                    in1=b.r2_b[:, t0:t0 + tw], op=OP.mult),
                waits=wts, drain=(tsi == 0),
                inc=(k.vec if tsi == 2 else None))
        b.fr_prev_vec[grp] = k.vec.v
    b.v_inproj_done = k.vec.v
    b.v_inproj_mm = k.mm.v
    for f in range(HB):
        k.step("scalar",
               lambda e, f=f: e.activation(
                   out=b.z[:, f * TH:(f + 1) * TH],
                   in_=b.z[:, f * TH:(f + 1) * TH], func=AF.Silu),
               waits=[(k.vec, b.v_inproj_done)] if f == 0 else (),
               inc=(k.scl if f == HB - 1 else None))
    b.v_sz = k.scl.v


def emit_conv(k):
    b = k.b
    tslices = [(0, 512), (512, 512)]
    v_mm_half = {}
    for ph in range(HB):
        b.conv_grp = b.frq % 2
        b.frq += 1
        b.conv_base = 4 * b.conv_grp
        for half in range(2):
            q = 2 * ph + half
            par = q % 2
            wl = k.wload0 if par == 0 else k.wload1
            vw = k.step(
                "gpsimd",
                lambda e, ph=ph, half=half, par=par: e.dma_start(
                    out=b.wslab[par][:, 0:4096].rearrange("p (c kb w) -> p c kb w", c=2, kb=16),
                    in_=b.convw.rearrange("c (kb p) o -> p c kb o", p=128)[
                        :, 2 * half:2 * half + 2, :, ph * 128:(ph + 1) * 128]),
                waits=[(k.mm, v_mm_half.get(q - 2, b.v_inproj_mm if q < 2 else 0))],
                inc=wl, inc_n=16)
            first = True
            for tsi, (t0, tw) in enumerate(tslices):
                bank = b.conv_base + tsi
                for cl in range(2):
                    c = 2 * half + cl
                    for kb in range(16):
                        wts = []
                        if first:
                            wts = [(wl, vw)]
                            if half == 0:
                                wts += [(k.scl, b.fr_prev_scl[b.conv_grp]),
                                        (k.vec, b.fr_prev_vec[b.conv_grp])]
                            if q == 0:
                                wts.append((k.vec, b.v_inproj_done))
                            first = False
                        is_last = (tsi == 1 and cl == 1 and kb == 15)
                        k.step(
                            "tensor",
                            lambda e, c=c, cl=cl, kb=kb, t0=t0, tw=tw, par=par,
                                   bank=bank: e.matmul(
                                b.pb[bank][:, 0:tw],
                                b.wslab[par][:, (cl * 16 + kb) * 128:(cl * 16 + kb + 1) * 128],
                                b.xc_pre[:, kb * TH + t0 + c: kb * TH + t0 + c + tw],
                                start=(c == 0 and kb == 0), stop=(c == 3 and kb == 15)),
                            waits=wts, inc=(k.mm if is_last else None))
            v_mm_half[q] = k.mm.v
        for tsi, (t0, tw) in enumerate(tslices):
            bank = b.conv_base + tsi
            k.step(
                "scalar",
                lambda e, ph=ph, t0=t0, tw=tw, bank=bank: e.activation(
                    out=b.xc[:, ph * T + t0: ph * T + t0 + tw],
                    in_=b.pb[bank][:, 0:tw], func=AF.Silu,
                    bias=b.convb_col[:, ph:ph + 1], scale=1.0),
                waits=[(k.mm, v_mm_half[2 * ph + 1])],
                inc=(k.scl if tsi == 1 else None))
        b.fr_prev_scl[b.conv_grp] = k.scl.v
    b.v_conv_mm = k.mm.v
    b.v_conv_done = k.scl.v


def emit_xproj(k):
    b = k.b
    vw = k.step(
        "gpsimd",
        lambda e: e.dma_start(
            out=b.wslab[0][:, 0:16 * 96].rearrange("p (kb j) -> p kb j", kb=16),
            in_=b.xpw_t.rearrange("(kb p) j -> p kb j", p=128)),
        waits=[(k.mm, b.v_conv_mm)], inc=k.wload0, inc_n=16)
    tslices = [(0, 512), (512, 512)]
    for tsi, (t0, tw) in enumerate(tslices):
        for kb in range(16):
            wts = []
            if tsi == 0 and kb == 0:
                wts = [(k.wload0, vw), (k.scl, b.v_conv_done),
                       (k.vec, max(b.fr_prev_vec[0], b.fr_prev_vec[1]))]
            k.step(
                "tensor",
                lambda e, kb=kb, t0=t0, tw=tw, tsi=tsi: e.matmul(
                    b.pb[2 + tsi][0:96, 0:tw],
                    b.wslab[0][:, kb * 96:(kb + 1) * 96],
                    b.xc[:, kb * T + t0: kb * T + t0 + tw],
                    start=(kb == 0), stop=(kb == 15)),
                waits=wts, inc=(k.mm if (tsi == 1 and kb == 15) else None))
    v_mm = k.mm.v
    b.v_xproj_mm = v_mm
    for tsi, (t0, tw) in enumerate(tslices):
        k.step("vector",
               lambda e, t0=t0, tw=tw, tsi=tsi: e.tensor_copy(
                   out=b.dbc[0:96, t0:t0 + tw], in_=b.pb[2 + tsi][0:96, 0:tw]),
               waits=[(k.mm, v_mm)], drain=(tsi == 0),
               inc=(k.vec if tsi == 1 else None))
    b.v_dbc = k.vec.v
    b.v_dbc_dram = k.step(
        "sync",
        lambda e: e.dma_start(out=b.dbc_dram.ap(), in_=b.dbc[0:96, :]),
        waits=[(k.vec, b.v_dbc)], inc=k.dmas, inc_n=16)



def emit_scan_pass(k, is_c):
    b = k.b
    for sub in range(NSUB):
        s0 = sub * SUB
        v_bm = k.step(
            "scalar",
            lambda e, s0=s0: e.dma_start(
                out=b.bm_rep[:].rearrange("p (n t) -> p n t", n=NST),
                in_=b.dbc_dram.ap()[64:80, s0:s0 + SUB].partition_broadcast(128)),
            waits=[(k.dmas, k.dmas.v), (k.vec, b.vv_bmfree)],
            inc=k.dmah, inc_n=16)
        if is_c:
            v_bm = k.step(
                "scalar",
                lambda e, s0=s0: e.dma_start(
                    out=b.c_rep[:].rearrange("p (n t) -> p n t", n=NST),
                    in_=b.dbc_dram.ap()[80:96, s0:s0 + SUB].partition_broadcast(128)),
                waits=[(k.pool, b.vv_bmfree_pool)],
                inc=k.dmah, inc_n=16)
        v_bmc = v_bm

        vdt = {}
        vsp = {}
        for hh in range(HB):
            bank = 6 + hh % 2
            wts = [(k.scl, vsp.get(hh - 2, b.v_dtpb_free)), (k.vec, b.v_dbc)]
            if hh < 2:
                wts.append((k.vec, b.v_oppb_free))
            if sub == 0 and hh == 0:
                wts.append((k.wdtw, k.wdtw.v))
            vdt[hh] = k.step(
                "tensor",
                lambda e, hh=hh, s0=s0, bank=bank: e.matmul(
                    b.pb[bank][:, 0:SUB],
                    b.dtw_slab[0:64, hh * 128:(hh + 1) * 128],
                    b.dbc[0:64, s0:s0 + SUB],
                    start=True, stop=True),
                waits=wts, inc=k.mm)
            t1 = b.tsc1[hh % 2]
            t2 = b.tsc2[hh % 2]
            k.step("scalar",
                   lambda e, hh=hh, bank=bank, t1=t1: e.activation(
                       out=t1, in_=b.pb[bank][:, 0:SUB], func=AF.Abs,
                       bias=b.dtb_col[:, hh:hh + 1]),
                   waits=[(k.mm, vdt[hh]), (k.vec, b.v_tscfree.get(hh % 2, 0))],
                   drain=True)
            k.step("scalar",
                   lambda e, t1=t1: e.activation(out=t1, in_=t1,
                                                 func=AF.Exp, scale=-1.0),
                   drain=True)
            k.step("scalar",
                   lambda e, t1=t1: e.activation(out=t1, in_=t1,
                                                 func=AF.Ln, bias=1.0),
                   drain=True)
            vsp[hh] = k.step("scalar",
                             lambda e, hh=hh, bank=bank, t2=t2: e.activation(
                                 out=t2, in_=b.pb[bank][:, 0:SUB], func=AF.Relu,
                                 bias=b.dtb_col[:, hh:hh + 1]),
                             drain=True, inc=k.scl)
            vd = k.step("vector",
                        lambda e, hh=hh, t1=t1, t2=t2: e.tensor_tensor(
                            out=b.d_sub[:, hh * SUB:(hh + 1) * SUB],
                            in0=t1, in1=t2, op=OP.add),
                        waits=[(k.scl, vsp[hh])], drain=True, inc=k.vec)
            b.v_tscfree[hh % 2] = vd
        b.v_dtpb_free = k.scl.v
        v_delta = k.vec.v

        k.step("vector",
               lambda e, s0=s0: e.tensor_tensor(
                   out=b.utmp_all[:].rearrange("p (h t) -> p h t", h=HB),
                   in0=b.d_sub[:].rearrange("p (h t) -> p h t", h=HB),
                   in1=b.xc[:].rearrange("p (h t) -> p h t", h=HB)[:, :, s0:s0 + SUB],
                   op=OP.mult),
               drain=True, inc=k.vec)
        v_delta = k.vec.v

        if not is_c:
            k.step("vector",
                   lambda e: e.tensor_reduce(
                       out=b.aprod[:, 0:16],
                       in_=b.d_sub[:].rearrange("p (h t) -> p h t", h=HB),
                       axis=AX.X, op=OP.add),
                   waits=[(k.vec, v_delta)], drain=True)
            if sub == 0:
                k.step("vector",
                       lambda e: e.tensor_copy(out=b.sumd[:], in_=b.aprod[:, 0:16]),
                       drain=True, inc=k.vec)
            else:
                k.step("vector",
                       lambda e: e.tensor_tensor(out=b.sumd[:], in0=b.sumd[:],
                                                 in1=b.aprod[:, 0:16], op=OP.add),
                       drain=True, inc=k.vec)

        for hh in range(HB):
            par = hh % 2
            ab = b.ab[par]
            bb = b.bbp[par]
            for n in range(NST):
                b.v_abprod[par] = k.step(
                    "scalar",
                    lambda e, hh=hh, n=n, ab=ab: e.activation(
                        out=ab[:, n * SUB:(n + 1) * SUB],
                        in_=b.d_sub[:, hh * SUB:(hh + 1) * SUB], func=AF.Exp,
                        scale=b.A_sbuf[:, hh * NST + n: hh * NST + n + 1]),
                    waits=([(k.vec, v_delta), (k.vec, b.v_abfree.get(par, 0)),
                            (k.vec, b.v_A)] if n == 0 else ()),
                    inc=(k.scl if n == NST - 1 else None))
            k.step(
                "vector",
                lambda e, hh=hh, bb=bb: e.tensor_tensor(
                    out=bb[:].rearrange("p (n t) -> p n t", n=NST),
                    in0=b.utmp_all[:, hh * SUB:(hh + 1) * SUB]
                        .unsqueeze(1).broadcast_to([128, NST, SUB]),
                    in1=b.bm_rep[:].rearrange("p (n t) -> p n t", n=NST),
                    op=OP.mult),
                waits=[(k.dmah, v_bmc),
                       (k.pool, b.v_bbfree_pool.get(par, 0))], drain=True)
            for n in range(NST):
                v_scans = k.step(
                    "vector",
                    lambda e, hh=hh, n=n, ab=ab, bb=bb: e.tensor_tensor_scan(
                        out=bb[:, n * SUB:(n + 1) * SUB],
                        data0=ab[:, n * SUB:(n + 1) * SUB],
                        data1=bb[:, n * SUB:(n + 1) * SUB],
                        initial=b.state[:, hh * NST + n: hh * NST + n + 1],
                        op0=OP.mult, op1=OP.add),
                    waits=([(k.scl, b.v_abprod[par]),
                            (k.pool, b.v_harv.get(hh, 0))] if n == 0 else ()),
                    drain=(n == 0),
                    inc=(k.vec if n == NST - 1 else None))
            b.v_abfree[par] = v_scans
            vst = k.step("gpsimd",
                         lambda e, hh=hh, bb=bb: e.tensor_copy(
                             out=b.state[:, hh * NST:(hh + 1) * NST],
                             in_=bb[:].rearrange("p (n t) -> p n t",
                                                 n=NST)[:, :, SUB - 1]),
                         waits=[(k.vec, v_scans)], inc=k.pool)
            b.v_harv[hh] = vst
            if not is_c:
                b.v_bbfree_pool[par] = vst
            else:
                k.step("gpsimd",
                       lambda e, bb=bb: e.tensor_tensor(
                           out=bb[:], in0=bb[:], in1=b.c_rep[:], op=OP.mult),
                       waits=[(k.dmah, v_bmc)], drain=True)
                k.step("gpsimd",
                       lambda e, bb=bb: e.tensor_tensor(
                           out=bb[:, 0:8 * SUB], in0=bb[:, 0:8 * SUB],
                           in1=bb[:, 8 * SUB:16 * SUB], op=OP.add),
                       drain=True)
                k.step("gpsimd",
                       lambda e, bb=bb: e.tensor_tensor(
                           out=bb[:, 0:4 * SUB], in0=bb[:, 0:4 * SUB],
                           in1=bb[:, 4 * SUB:8 * SUB], op=OP.add),
                       drain=True)
                k.step("gpsimd",
                       lambda e, bb=bb: e.tensor_tensor(
                           out=bb[:, 0:2 * SUB], in0=bb[:, 0:2 * SUB],
                           in1=bb[:, 2 * SUB:4 * SUB], op=OP.add),
                       drain=True)
                vrhs = k.step("gpsimd",
                              lambda e, hh=hh, bb=bb: e.tensor_tensor(
                                  out=b.rhs[:, hh * SUB:(hh + 1) * SUB],
                                  in0=bb[:, 0:SUB], in1=bb[:, SUB:2 * SUB],
                                  op=OP.add),
                              drain=True, inc=k.pool)
                b.v_bbfree_pool[par] = vrhs
                b.vv_bmfree_pool = vrhs
        b.vv_bmfree = k.vec.v
        if is_c:
            k.step("vector",
                   lambda e, s0=s0: e.tensor_tensor(
                       out=b.dxc[:].rearrange("p (h t) -> p h t", h=HB),
                       in0=b.xc[:].rearrange("p (h t) -> p h t", h=HB)[:, :, s0:s0 + SUB],
                       in1=b.d_col[:].unsqueeze(2).broadcast_to([128, HB, SUB]),
                       op=OP.mult),
                   waits=[(k.pool, k.pool.v)], drain=True)
            k.step("vector",
                   lambda e: e.tensor_tensor(out=b.rhs[:], in0=b.rhs[:],
                                             in1=b.dxc[:], op=OP.add),
                   drain=True)
            vrhs = k.step("vector",
                          lambda e, s0=s0: e.tensor_tensor(
                              out=b.rhs[:].rearrange("p (h t) -> p h t", h=HB),
                              in0=b.rhs[:].rearrange("p (h t) -> p h t", h=HB),
                              in1=b.z[:].rearrange("p (h t) -> p h t", h=HB)[
                                  :, :, 3 + s0:3 + s0 + SUB],
                              op=OP.mult),
                          drain=True, inc=k.vec)
            emit_outproj_sub(k, sub)


def emit_outproj_sub(k, sub):
    b = k.b
    v_rhs = k.vec.v
    for ph in range(8):
        o0 = ph * 256
        q, par, wl, vw = _slab(
            k, b.outw_t.rearrange("(kb p) o -> p kb o", p=128)[:, :, o0:o0 + 256]
                .rearrange("p kb o -> p (kb o)") if False else
            b.outw_t.rearrange("(kb p) o -> p kb o", p=128)[:, :, o0:o0 + 256],
            4096)
        for ob in range(2):
            bank = 6 + ob
            for kb in range(16):
                wts = []
                if ob == 0 and kb == 0:
                    wts = [(wl, vw), (k.vec, v_rhs),
                           (k.vec, b.op_evac.get(q - 1, 0)),
                           (k.scl, b.v_dtpb_free)]
                k.step(
                    "tensor",
                    lambda e, ob=ob, kb=kb, par=par, bank=bank: e.matmul(
                        b.pb[bank][:, 0:SUB],
                        b.wslab[par][:, kb * 256 + ob * 128: kb * 256 + ob * 128 + 128],
                        b.rhs[:, kb * SUB:(kb + 1) * SUB],
                        start=(kb == 0), stop=(kb == 15)),
                    waits=wts,
                    inc=(k.mm if (ob == 1 and kb == 15) else None))
        b.op_mm[q] = k.mm.v
        b.slab_mm[q] = k.mm.v
        for ob in range(2):
            bank = 6 + ob
            o = o0 + ob * 128
            f = o // 128
            ssem = k.acc0 if ob == 0 else k.acc1
            stg = b.sq[:, ob * SUB:(ob + 1) * SUB]
            vev = k.step(
                "vector",
                lambda e, bank=bank, stg=stg: e.tensor_copy(out=stg,
                                                            in_=b.pb[bank][:, 0:SUB]),
                waits=[(k.mm, b.op_mm[q]), (ssem, ssem.v)], drain=True,
                inc=k.vec)
            k.step(
                "gpsimd",
                lambda e, f=f, sub=sub, stg=stg: e.dma_start(
                    out=b.out_fm[ts(f), sub * SUB:(sub + 1) * SUB], in_=stg,
                    accum_op=OP.add),
                waits=[(k.vec, vev), (k.st0, b.v_out_init_st0)],
                inc=ssem, inc_n=16)
        b.op_evac[q] = k.vec.v
    b.v_oppb_free = k.vec.v



def emit_gather(k, n_cores):
    b = k.b
    v = k.step("vector",
               lambda e: e.tensor_tensor(
                   out=b.aprod[:].rearrange("p (h n) -> p h n", n=NST),
                   in0=b.sumd[:].unsqueeze(2).broadcast_to([128, HB, NST]),
                   in1=b.A_sbuf[:].rearrange("p (h n) -> p h n", n=NST),
                   op=OP.mult),
               drain=True, inc=k.vec)
    v = k.step("scalar",
               lambda e: e.activation(out=b.aprod[:], in_=b.aprod[:], func=AF.Exp),
               waits=[(k.vec, v)], drain=True, inc=k.scl)
    k.step("sync",
           lambda e: e.dma_start(out=b.ag_in.ap()[:, 0:256], in_=b.aprod[:]),
           waits=[(k.scl, v)], inc=k.dmas, inc_n=16)
    v_ag = k.step("sync",
                  lambda e: e.dma_start(out=b.ag_in.ap()[:, 256:512], in_=b.state[:]),
                  waits=[(k.vec, k.vec.v), (k.pool, k.pool.v)],
                  inc=k.dmas, inc_n=16)
    v_cc = k.step("gpsimd",
                  lambda e, n_cores=n_cores: e.collective_compute(
                      "AllGather", OP.bypass,
                      replica_groups=[list(range(n_cores))],
                      ins=[b.ag_in.ap().opt()],
                      outs=[b.ag_out.ap().opt()]),
                  waits=[(k.dmas, v_ag)], inc=k.cc)
    v_g = k.step("gpsimd",
                 lambda e, n_cores=n_cores: e.dma_start(
                     out=b.gath[:, 0:n_cores * 512].rearrange("p (r c) -> p r c", r=n_cores),
                     in_=b.ag_out.ap().rearrange("(r p) c -> p r c", p=128)),
                 waits=[(k.cc, v_cc), (k.pool, k.pool.v)],
                 inc=k.wgath, inc_n=16)
    v = k.step("vector", lambda e: e.memset(b.hin[:], 0.0),
               waits=[(k.wgath, v_g)], drain=True, inc=k.vec)
    for j in range(n_cores):
        k.step("vector",
               lambda e, j=j: e.scalar_tensor_tensor(
                   out=b.tsc1[0], in0=b.gath[:, j * 512: j * 512 + 256],
                   scalar=b.m_col[:, j:j + 1],
                   in1=b.im_col[:, j:j + 1].to_broadcast([128, 256]),
                   op0=OP.mult, op1=OP.add),
               drain=True)
        k.step("vector",
               lambda e: e.tensor_tensor(out=b.hin[:], in0=b.hin[:],
                                         in1=b.tsc1[0], op=OP.mult),
               drain=True)
        k.step("vector",
               lambda e, j=j: e.tensor_scalar_mul(
                   b.tsc1[0], b.gath[:, j * 512 + 256:(j + 1) * 512],
                   b.m_col[:, j:j + 1]),
               drain=True)
        k.step("vector",
               lambda e: e.tensor_tensor(out=b.hin[:], in0=b.hin[:],
                                         in1=b.tsc1[0], op=OP.add),
               drain=True)
    b.v_hin = k.step("vector",
                     lambda e: e.tensor_copy(out=b.state[:], in_=b.hin[:]),
                     drain=True, inc=k.vec)
    b.v_tscfree = {0: b.v_hin, 1: b.v_hin}



def _slab(k, src_ap, width):
    b = k.b
    q = b.wq
    b.wq += 1
    par = q % 2
    wl = k.wload0 if par == 0 else k.wload1
    nd = len(src_ap.shape)
    dst = b.wslab[par][:, 0:width]
    if nd == 3:
        dst = dst.rearrange("p (a c) -> p a c", a=src_ap.shape[1])
    vw = k.step(
        "gpsimd",
        lambda e, src_ap=src_ap, dst=dst: e.dma_start(out=dst, in_=src_ap),
        waits=[(k.mm, b.slab_mm.get(q - 2, b.v_xproj_mm if q < 2 else 0))],
        inc=wl, inc_n=16)
    return q, par, wl, vw


def emit_gate(k):
    b = k.b
    ptile = b.tsc1[0][:, 0:16]
    negmax = b.tsc1[0][:, 32:33]
    rcp = b.tsc1[0][:, 34:35]
    m8 = b.tsc1[0][:, 40:48]
    mask = b.tsc1[0][:, 64:80]
    ctile = b.ctile[:]
    cstage = b.tsc2[0][0:16, 0:128]
    k.step("gpsimd",
           lambda e: e.dma_start(
               out=b.gslab[:].rearrange("p (kb e) -> p kb e", kb=16),
               in_=b.gatew_t.rearrange("(kb p) e -> p kb e", p=128)),
           inc=k.wdtw, inc_n=16)
    v_id = k.step("gpsimd",
                  lambda e: e.dma_start(out=b.ident_bf[:], in_=b.ident),
                  inc=k.wdtw, inc_n=16)
    v_gw = v_id
    v_cdram = b.v_cfm_dram = 0
    for tt in range(8):
        t0 = 3 + tt * 128
        vmm = k.step(
            "tensor",
            lambda e, tt=tt, t0=t0: [
                e.matmul(b.pb[4][:, 0:16],
                         b.xn[:, kb * TH + t0: kb * TH + t0 + 128],
                         b.gslab[:, kb * 16:(kb + 1) * 16],
                         start=(kb == 0), stop=(kb == 15))
                for kb in range(16)][-1],
            waits=([(k.wdtw, v_gw), (k.vec, b.v_dbc)]
                   if tt == 0 else [(k.vec, k.vec.v)]),
            inc=k.mm)
        k.step("vector",
               lambda e: e.tensor_reduce(out=negmax[:, :], in_=b.pb[4][:, 0:16],
                                         axis=AX.X, op=OP.max),
               waits=[(k.mm, vmm)], drain=True)
        k.step("vector",
               lambda e: e.tensor_scalar_mul(negmax, negmax, -1.0), drain=True)
        vneg = k.step("vector", lambda e: e.engine_nop(), drain=True, inc=k.vec)
        vexp = k.step("scalar",
                      lambda e: e.activation(out=ptile, in_=b.pb[4][:, 0:16],
                                             func=AF.Exp, bias=negmax),
                      waits=[(k.vec, vneg)], drain=True, inc=k.scl)
        k.step("vector",
               lambda e: e.tensor_reduce(out=rcp, in_=ptile, axis=AX.X, op=OP.add),
               waits=[(k.scl, vexp)], drain=True)
        k.step("vector", lambda e: e.reciprocal(rcp, rcp), drain=True)
        k.step("vector",
               lambda e: e.tensor_scalar_mul(ptile, ptile, rcp), drain=True)
        k.step("vector", lambda e: e.max(out=m8, in_=ptile), drain=True)
        k.step("vector",
               lambda e: e.tensor_scalar(out=mask, in0=ptile, scalar1=m8[:, 5:6],
                                         scalar2=None, op0=OP.is_ge),
               drain=True)
        vct = k.step("vector",
                     lambda e: e.tensor_tensor(out=ctile, in0=ptile, in1=mask,
                                               op=OP.mult),
                     drain=True, inc=k.vec)
        vtr = k.step("tensor",
                     lambda e: e.transpose(b.pb[5][0:16, 0:64].bitcast(BF16), ctile,
                                           b.ident_bf[:]),
                     waits=[(k.vec, vct), (k.wdtw, v_id),
                            (k.vec, b.v_ctr_prev)],
                     inc=k.mm)
        b.v_ctr_prev = k.step(
            "vector",
            lambda e: e.tensor_copy(out=cstage,
                                    in_=b.pb[5][0:16, 0:64].bitcast(BF16)),
            waits=[(k.mm, vtr), (k.dmas, v_cdram)], drain=True, inc=k.vec)
        v_cdram = k.step(
            "sync",
            lambda e, tt=tt: e.dma_start(
                out=b.combine_dram.ap()[:, tt * 128:(tt + 1) * 128], in_=cstage),
            waits=[(k.vec, b.v_ctr_prev)], inc=k.dmas, inc_n=16)
    b.v_cfm_dram = v_cdram


def emit_shared_up_phase(k, ph):
    b = k.b
    o0 = ph * 256
    grp = b.bgrp % 2
    b.bgrp += 1
    base = 4 * grp
    q, par, wl, vw = _slab(
        k, b.shw1_t.rearrange("(kb p) o -> p kb o", p=128)[:, :, o0:o0 + 256], 4096)
    for ob in range(2):
        for tti, tw0 in enumerate((3, 515)):
            bank = base + 2 * ob + tti
            for kb in range(16):
                wts = []
                if ob == 0 and tti == 0 and kb == 0:
                    wts = [(wl, vw),
                           (k.vec, max(b.grp_prev_vec[grp], b.v_acc_zero)),
                           (k.scl, b.grp_prev_scl[grp])]
                k.step(
                    "tensor",
                    lambda e, ob=ob, kb=kb, tw0=tw0, par=par, bank=bank: e.matmul(
                        b.pb[bank][:, 0:512],
                        b.wslab[par][:, kb * 256 + ob * 128: kb * 256 + ob * 128 + 128],
                        b.xn[:, kb * TH + tw0: kb * TH + tw0 + 512],
                        start=(kb == 0), stop=(kb == 15)),
                    waits=wts,
                    inc=(k.mm if (ob == 1 and tti == 1 and kb == 15) else None))
    b.moe_mm[q] = k.mm.v
    b.slab_mm[q] = k.mm.v
    for ob in range(2):
        o = o0 + ob * 128
        for tti in range(2):
            bank = base + 2 * ob + tti
            if o < IS:
                ib = o // 128
                k.step("scalar",
                       lambda e, ib=ib, tti=tti, bank=bank: e.activation(
                           out=b.sg_s[:, ib * 1024 + tti * 512: ib * 1024 + tti * 512 + 512],
                           in_=b.pb[bank][:, 0:512], func=AF.Silu),
                       waits=[(k.mm, b.moe_mm[q])],
                       inc=(k.scl if (tti == 1) else None))
            else:
                ib = (o - IS) // 128
                k.step("vector",
                       lambda e, ib=ib, tti=tti, bank=bank: e.tensor_tensor(
                           out=b.sg_s[:, ib * 1024 + tti * 512: ib * 1024 + tti * 512 + 512],
                           in0=b.sg_s[:, ib * 1024 + tti * 512: ib * 1024 + tti * 512 + 512],
                           in1=b.pb[bank][:, 0:512], op=OP.mult),
                       waits=[(k.mm, b.moe_mm[q]), (k.scl, b.v_shared_gate_done)],
                       drain=True,
                       inc=(k.vec if (tti == 1) else None))
    if o0 + 256 == IS:
        b.v_shared_gate_done = k.scl.v
    b.grp_prev_vec[grp] = k.vec.v
    b.grp_prev_scl[grp] = k.scl.v
    if ph == 2 * IS // 256 - 1:
        b.v_shared_act = k.vec.v


def emit_shared_down(k):
    b = k.b
    ISB = IS // 128
    for ph in range(HB):
        grp = b.bgrp % 2
        b.bgrp += 1
        base = 4 * grp
        q, par, wl, vw = _slab(
            k, b.shw2_t.rearrange("(kb p) o -> p kb o", p=128)[:, :, ph * 128:(ph + 1) * 128],
            ISB * 128)
        for tti in range(2):
            bank = base + tti
            for kb in range(ISB):
                wts = []
                if tti == 0 and kb == 0:
                    wts = [(wl, vw),
                           (k.vec, max(b.v_shared_act, b.grp_prev_vec[grp])),
                           (k.scl, b.grp_prev_scl[grp])]
                k.step(
                    "tensor",
                    lambda e, kb=kb, tti=tti, par=par, bank=bank: e.matmul(
                        b.pb[bank][:, 0:512],
                        b.wslab[par][:, kb * 128:(kb + 1) * 128],
                        b.sg_s[:, kb * 1024 + tti * 512: kb * 1024 + tti * 512 + 512],
                        start=(kb == 0), stop=(kb == ISB - 1)),
                    waits=wts,
                    inc=(k.mm if (tti == 1 and kb == ISB - 1) else None))
        b.moe_mm[q] = k.mm.v
        b.slab_mm[q] = k.mm.v
        for tti in range(2):
            bank = base + tti
            k.step("vector",
                   lambda e, ph=ph, tti=tti, bank=bank: e.tensor_tensor(
                       out=b.acc[:, ph * 1024 + tti * 512: ph * 1024 + tti * 512 + 512],
                       in0=b.acc[:, ph * 1024 + tti * 512: ph * 1024 + tti * 512 + 512],
                       in1=b.pb[bank][:, 0:512], op=OP.add),
                   waits=[(k.mm, b.moe_mm[q]), (k.vec, b.v_acc_zero)],
                   drain=True, inc=(k.vec if tti == 1 else None))
        b.grp_prev_vec[grp] = k.vec.v


def emit_experts(k):
    b = k.b
    IB = I_FF // 128
    for ex in range(E):
        par_e = ex % 2
        act = b.act[par_e]
        cb = b.cb[par_e]
        wcb = k.wcb0 if par_e == 0 else k.wcb1
        v_cb = k.step(
            "gpsimd",
            lambda e, ex=ex, cb=cb: e.dma_start(
                out=cb, in_=b.combine_dram.ap()[ex:ex + 1, :].partition_broadcast(128)),
            waits=[(k.dmas, b.v_cfm_dram), (k.vec, b.cb_free.get(par_e, 0))],
            inc=wcb, inc_n=16)
        for ph in range(IB):
            o0 = ph * 256
            grp = b.bgrp % 2
            b.bgrp += 1
            base = 4 * grp
            q, par, wl, vw = _slab(
                k, b.w1_t.rearrange("e (kb p) o -> e p kb o", p=128)[ex][:, :, o0:o0 + 256],
                4096)
            for ob in range(2):
                for tti, tw0 in enumerate((3, 515)):
                    bank = base + 2 * ob + tti
                    for kb in range(16):
                        wts = []
                        if ob == 0 and tti == 0 and kb == 0:
                            wts = [(wl, vw),
                                   (k.vec, b.grp_prev_vec[grp]),
                                   (k.scl, b.grp_prev_scl[grp])]
                        k.step(
                            "tensor",
                            lambda e, ob=ob, kb=kb, tw0=tw0, par=par, bank=bank: e.matmul(
                                b.pb[bank][:, 0:512],
                                b.wslab[par][:, kb * 256 + ob * 128: kb * 256 + ob * 128 + 128],
                                b.xn[:, kb * TH + tw0: kb * TH + tw0 + 512],
                                start=(kb == 0), stop=(kb == 15)),
                            waits=wts,
                            inc=(k.mm if (ob == 1 and tti == 1 and kb == 15) else None))
            b.moe_mm[q] = k.mm.v
            b.slab_mm[q] = k.mm.v
            for ob in range(2):
                o = o0 + ob * 128
                for tti in range(2):
                    bank = base + 2 * ob + tti
                    if o < I_FF:
                        ib = o // 128
                        k.step("scalar",
                               lambda e, ib=ib, tti=tti, bank=bank, act=act: e.activation(
                                   out=act[:, ib * 1024 + tti * 512: ib * 1024 + tti * 512 + 512],
                                   in_=b.pb[bank][:, 0:512], func=AF.Silu),
                               waits=[(k.mm, b.moe_mm[q]),
                                      (k.vec, b.act_free.get(par_e, 0))],
                               inc=(k.scl if tti == 1 else None))
                    else:
                        ib = (o - I_FF) // 128
                        sl = lambda act, ib, tti: act[:, ib * 1024 + tti * 512: ib * 1024 + tti * 512 + 512]
                        k.step("vector",
                               lambda e, ib=ib, tti=tti, bank=bank, act=act, sl=sl: e.tensor_tensor(
                                   out=sl(act, ib, tti), in0=sl(act, ib, tti),
                                   in1=b.pb[bank][:, 0:512], op=OP.mult),
                               waits=[(k.mm, b.moe_mm[q])],
                               drain=True)
                        k.step("vector",
                               lambda e, ib=ib, tti=tti, act=act, cb=cb, sl=sl: e.tensor_tensor(
                                   out=sl(act, ib, tti), in0=sl(act, ib, tti),
                                   in1=cb[:, tti * 512:(tti + 1) * 512], op=OP.mult),
                               waits=[(wcb, v_cb)], drain=True,
                               inc=(k.vec if (tti == 1 and ob == 1) else None))
            b.grp_prev_vec[grp] = k.vec.v
            b.grp_prev_scl[grp] = k.scl.v
        v_act = k.vec.v
        b.cb_free[par_e] = v_act
        for ph in range(HB):
            grp = b.bgrp % 2
            b.bgrp += 1
            base = 4 * grp
            q, par, wl, vw = _slab(
                k, b.w2_t.rearrange("e (kb p) o -> e p kb o", p=128)[ex][:, :, ph * 128:(ph + 1) * 128],
                IB * 128)
            for tti in range(2):
                bank = base + tti
                for kb in range(IB):
                    wts = []
                    if tti == 0 and kb == 0:
                        wts = [(wl, vw),
                               (k.vec, max(v_act, b.grp_prev_vec[grp])),
                               (k.scl, b.grp_prev_scl[grp])]
                    k.step(
                        "tensor",
                        lambda e, kb=kb, tti=tti, par=par, bank=bank, act=act: e.matmul(
                            b.pb[bank][:, 0:512],
                            b.wslab[par][:, kb * 128:(kb + 1) * 128],
                            act[:, kb * 1024 + tti * 512: kb * 1024 + tti * 512 + 512],
                            start=(kb == 0), stop=(kb == IB - 1)),
                        waits=wts,
                        inc=(k.mm if (tti == 1 and kb == IB - 1) else None))
            b.moe_mm[q] = k.mm.v
            b.slab_mm[q] = k.mm.v
            for tti in range(2):
                bank = base + tti
                k.step("vector",
                       lambda e, ph=ph, tti=tti, bank=bank: e.tensor_tensor(
                           out=b.acc[:, ph * 1024 + tti * 512: ph * 1024 + tti * 512 + 512],
                           in0=b.acc[:, ph * 1024 + tti * 512: ph * 1024 + tti * 512 + 512],
                           in1=b.pb[bank][:, 0:512], op=OP.add),
                       waits=[(k.mm, b.moe_mm[q])], drain=True,
                       inc=(k.vec if tti == 1 else None))
            b.grp_prev_vec[grp] = k.vec.v
        b.act_free[par_e] = k.vec.v


def emit_moe_tail(k):
    b = k.b
    b.v_acc_zero = k.step(
        "vector", lambda e: e.memset(b.acc[:], 0.0),
        waits=[(k.vec, k.vec.v), (k.acc0, k.acc0.v), (k.acc1, k.acc1.v),
               (k.pool, k.pool.v)],
        drain=True, inc=k.vec)
    for ph in range(2 * IS // 256):
        emit_shared_up_phase(k, ph)
    emit_shared_down(k)
    emit_experts(k)
    vflush_gate = [(k.vec, k.vec.v), (k.acc0, k.acc0.v), (k.acc1, k.acc1.v),
                   (k.st0, b.v_out_init_st0)]
    for f in range(HB):
        ssem = k.acc0 if f % 2 == 0 else k.acc1
        k.step("gpsimd",
               lambda e, f=f: e.dma_start(
                   out=b.out_fm[ts(f), :], in_=b.acc[:, f * 1024:(f + 1) * 1024],
                   accum_op=OP.add),
               waits=(vflush_gate if f == 0 else ()), inc=ssem, inc_n=16)


def emit_debug(k):
    if not DEBUG:
        return
    b = k.b
    if not DEBUG_FRONT:
        k.step("sync",
               lambda e: e.dma_start(
                   out=b.xn_dbg.rearrange("(f p) t -> p f t", p=128),
                   in_=b.xn[:].rearrange("p (f t) -> p f t", f=HB)),
               waits=[(k.vec, b.v_xn)], inc=k.st2, inc_n=16)
        return
    k.step("sync",
           lambda e: e.dma_start(
               out=b.xn_dbg.rearrange("(f p) t -> p f t", p=128),
               in_=b.xn[:].rearrange("p (f t) -> p f t", f=HB)),
           waits=[(k.vec, b.v_xn)], inc=k.st2, inc_n=16)
    k.step("sync",
           lambda e: e.dma_start(
               out=b.z_dbg.rearrange("(f p) t -> p f t", p=128),
               in_=b.z[:].rearrange("p (f t) -> p f t", f=HB)),
           waits=[(k.vec, b.v_inproj_done)], inc=k.st2, inc_n=16)
    k.step("sync",
           lambda e: e.dma_start(
               out=b.xc_dbg.rearrange("(f p) t -> p f t", p=128),
               in_=b.xc[:].rearrange("p (f t) -> p f t", f=HB)),
           waits=[(k.scl, b.v_conv_done)], inc=k.st2, inc_n=16)
    k.step("sync",
           lambda e: e.dma_start(out=b.dbc_dbg, in_=b.dbc[0:96, :]),
           waits=[(k.vec, b.v_dbc)], inc=k.st2, inc_n=16)
    k.step("sync",
           lambda e: e.dma_start(out=b.st_dbg[:, 0:512], in_=b.rhs[:, 0:1024]
                                 .bitcast(F32)),
           waits=[(k.vec, k.vec.v)], inc=k.st2, inc_n=16)


def program(k, n_cores=8):
    b = k.b
    b.vscl_bank = {}
    b.vscl_exp = 0
    b.vv_bmfree = 0
    b.v_tscfree = 0
    b.op_mm = {}
    b.op_evac = {}
    b.opq = 0
    b.wq = 0
    b.slab_mm = {}
    b.shared_ph_iter = None
    b.v_ctr_prev = 0
    b.v_shared_act = 0
    b.shared_fed = 0
    b.frq = 0
    b.fr_prev_vec = {0: 0, 1: 0}
    b.fr_prev_scl = {0: 0, 1: 0}
    b.bgrp = 0
    b.grp_prev_vec = {0: 0, 1: 0}
    b.grp_prev_scl = {0: 0, 1: 0}
    b.moe_mm = {}
    b.moe_evac = {}
    b.moe_sevac = {}
    b.cb_free = {}
    b.act_free = {}
    b.v_scans = 0
    b.v_dtpb_free = 0
    b.v_oppb_free = 0
    b.v_abprod = {}
    b.v_abfree = {}
    b.v_bbfree = 0
    b.v_bbfree_pool = {}
    b.vv_bmfree_pool = 0
    b.v_harv = {}
    b.v_tscfree = {}
    b.eg_gate_done = {}
    b.v_shared_gate_done = 0
    emit_consts(k)
    emit_ln(k)
    emit_r2(k)
    emit_inproj(k)
    emit_conv(k)
    emit_xproj(k)
    emit_gate(k)
    emit_scan_pass(k, is_c=False)
    emit_gather(k, n_cores)
    emit_scan_pass(k, is_c=True)
    emit_moe_tail(k)
    emit_debug(k)
    waits = [(s, s.v) for s in (k.st0, k.st1, k.st2, k.st3, k.acc0, k.acc1)]
    k.step("sync", lambda e: e.nop(), waits=waits)



class Bufs:
    pass


def build_nc(n_cores=8):
    nc = bass.Bass("TRN2", target_bir_lowering=False)
    b = Bufs()
    b.x_fm = nc.dram_tensor("x_fm", [H, TH], F32, kind="ExternalInput").ap()
    b.inw_t = nc.dram_tensor("inw_t", [H, 2 * H], F32, kind="ExternalInput").ap()
    b.convw = nc.dram_tensor("convw", [4, H, H], F32, kind="ExternalInput").ap()
    b.xpw_t = nc.dram_tensor("xpw_t", [H, 96], F32, kind="ExternalInput").ap()
    b.dtw_t = nc.dram_tensor("dtw_t", [DTR, H], F32, kind="ExternalInput").ap()
    b.outw_t = nc.dram_tensor("outw_t", [H, H], F32, kind="ExternalInput").ap()
    b.nw_in = nc.dram_tensor("nw_in", [128, HB], F32, kind="ExternalInput").ap()
    b.convb_in = nc.dram_tensor("convb_in", [128, HB], F32, kind="ExternalInput").ap()
    b.dtb_in = nc.dram_tensor("dtb_in", [128, HB], F32, kind="ExternalInput").ap()
    b.d_in = nc.dram_tensor("d_in", [128, HB], F32, kind="ExternalInput").ap()
    b.alog_in = nc.dram_tensor("alog_in", [128, HB * NST], F32, kind="ExternalInput").ap()
    b.mask_in = nc.dram_tensor("mask_in", [128, 8], F32, kind="ExternalInput").ap()
    b.imask_in = nc.dram_tensor("imask_in", [128, 8], F32, kind="ExternalInput").ap()
    b.gatew_t = nc.dram_tensor("gatew_t", [H, E], F32, kind="ExternalInput").ap()
    b.shw1_t = nc.dram_tensor("shw1_t", [H, 2 * IS], F32, kind="ExternalInput").ap()
    b.shw2_t = nc.dram_tensor("shw2_t", [IS, H], F32, kind="ExternalInput").ap()
    b.w1_t = nc.dram_tensor("w1_t", [E, H, 2 * I_FF], F32, kind="ExternalInput").ap()
    b.w2_t = nc.dram_tensor("w2_t", [E, I_FF, H], F32, kind="ExternalInput").ap()
    b.ident = nc.dram_tensor("ident", [128, 128], F32, kind="ExternalInput").ap()
    b.combine_dram = nc.dram_tensor("combine_dram", [E, T], F32)
    b.out_fm = nc.dram_tensor("out_fm", [H, T], F32, kind="ExternalOutput").ap()
    b.rows_dram = nc.dram_tensor("rows_dram", [4, TH], F32).ap()
    b.dbc_dram = nc.dram_tensor("dbc_dram", [96, T], BF16)
    b.ag_in = nc.dram_tensor("ag_in", [128, 512], F32)
    b.ag_out = nc.dram_tensor("ag_out", [n_cores * 128, 512], F32,
                              addr_space="Shared" if n_cores > 1 else "Local")
    if DEBUG:
        b.xn_dbg = nc.dram_tensor("xn_dbg", [H, TH], BF16, kind="ExternalOutput").ap()
        b.z_dbg = nc.dram_tensor("z_dbg", [H, TH], BF16, kind="ExternalOutput").ap()
        b.xc_dbg = nc.dram_tensor("xc_dbg", [H, T], BF16, kind="ExternalOutput").ap()
        b.dbc_dbg = nc.dram_tensor("dbc_dbg", [96, T], BF16, kind="ExternalOutput").ap()
        b.st_dbg = nc.dram_tensor("st_dbg", [128, 1024], F32, kind="ExternalOutput").ap()

    from contextlib import ExitStack
    ctx = ExitStack()
    S = lambda name, shape, dt: ctx.enter_context(nc.sbuf_tensor(name, shape, dt))
    b.xblk = [S(f"xblk{i}", [128, TH], F32) for i in range(2)]
    b.xn = S("xn", [128, HB * TH], BF16)
    b.sq = S("sq", [128, TH], F32)
    b.ones = S("ones", [128, 4], F32)
    b.dbc = S("dbc", [128, T], BF16)
    b.wslab = [S(f"wslab{i}", [128, 4096], BF16) for i in range(2)]
    b.dtw_slab = S("dtw_slab", [64, H], BF16)
    b.A_sbuf = S("A_sbuf", [128, HB * NST], F32)
    b.nw_col = S("nw_col", [128, HB], F32)
    b.convb_col = S("convb_col", [128, HB], F32)
    b.dtb_col = S("dtb_col", [128, HB], F32)
    b.d_col = S("d_col", [128, HB], F32)
    b.m_col = S("m_col", [128, 8], F32)
    b.im_col = S("im_col", [128, 8], F32)

    b.state = b.xblk[0][:, 0:256]
    b.hin = b.xblk[0][:, 256:512]
    b.y_hh = b.xblk[0][:, 512:768]
    b.aprod = b.xblk[0][:, 768:1024]
    b.sumd = b.y_hh[:, 0:16]
    b.tsc1 = [b.xblk[1][:, 0:256], b.xblk[1][:, 512:768]]
    b.tsc2 = [b.xblk[1][:, 256:512], b.xblk[1][:, 768:1024]]

    NA = 35907
    b.arena = S("arena", [128, NA], F32)
    A = b.arena
    b.rstd_b = A[:, 0:1027]
    b.mr_b = A[:, 1027:2054]
    b.row_mean = A[0:1, 2054:3081]
    b.row_var = A[0:1, 3081:4108]
    b.row_rstd = A[0:1, 4108:5135]
    b.row_mr = A[0:1, 5135:6162]
    b.xc_pre = A[:, 0:8224].bitcast(BF16)[:, 0:HB * TH]
    b.z = A[:, 8224:16448].bitcast(BF16)[:, 0:HB * TH]
    b.xc = A[:, 16448:24640].bitcast(BF16)
    b.r2_b = A[:, 24640:25667]
    b.d_sub = A[:, 0:2048].bitcast(BF16)
    b.ab = [A[:, 2048:4096].bitcast(BF16), A[:, 25667:27715].bitcast(BF16)]
    b.dxc = A[:, 4096:6144].bitcast(BF16)
    b.gath = A[:, 2048:6144]
    b.bbp = [A[:, 6144:8192].bitcast(BF16),
             A[:, 4096:6144].bitcast(BF16)]
    b.utmp_all = A[:, 27715:29763].bitcast(BF16)
    b.bm_rep = A[:, 29763:31811].bitcast(BF16)
    b.c_rep = A[:, 31811:33859].bitcast(BF16)
    b.rhs = A[:, 33859:35907].bitcast(BF16)
    b.acc = A[:, 0:16384]
    b.act = [A[:, 16384:22016].bitcast(BF16),
             A[:, 22016:27648].bitcast(BF16)]
    b.sg_s = A[:, 16384:27648].bitcast(BF16)[:, 0:22 * 1024]
    b.cb = [A[:, 27648:28160].bitcast(BF16),
            A[:, 28160:28672].bitcast(BF16)]
    b.gslab = A[:, 28672:28800].bitcast(BF16)
    b.ident_bf = A[:, 28800:28864].bitcast(BF16)
    b.ctile = A[:, 28864:28872].bitcast(BF16)
    b.cfm = A[0:16, 28872:29896]

    b.pb = [ctx.enter_context(nc.psum_tensor(f"pb{i}", [128, 512], F32))
            for i in range(8)]

    sem_names = ["xload", "xl0", "xl1", "wload0", "wload1", "mm", "vec", "scl",
                 "dmah", "dmas", "cc", "st0", "st1", "st2", "st3", "acc0",
                 "acc1", "wdtw", "wgath", "wcb0", "wcb1", "pool",
                 "chain_vector", "chain_scalar", "chain_gpsimd"]
    sems = {}
    for name in sem_names:
        sems[name] = ctx.enter_context(nc.semaphore(name))

    k = K(nc, b, sems)
    k.start_scan_pass()
    program(k, n_cores=n_cores)
    k.end_pass()
    k.plan()
    with ctx, nc.Block() as block:
        for eng_name in ENGS:
            def section(eng, eng_name=eng_name):
                k.start_pass(eng_name, eng)
                program(k, n_cores=n_cores)
                k.end_pass()
            getattr(block, eng_name)(section)
    return nc, k


def _colify(v):
    return np.ascontiguousarray(v.reshape(HB, 128).T)


def host_inputs(inputs):
    w = {}
    w["inw_t"] = np.ascontiguousarray(np.asarray(inputs["L0_attn_in_proj_w"]).T)
    w["convw"] = np.ascontiguousarray(np.asarray(inputs["L0_attn_conv_w"]))
    w["xpw_t"] = np.ascontiguousarray(np.asarray(inputs["L0_attn_x_proj_w"]).T)
    w["nw_in"] = _colify(np.asarray(inputs["L0_attn_norm_w"]))
    w["convb_in"] = _colify(np.asarray(inputs["L0_attn_conv_b"]))
    w["dtw_t"] = np.ascontiguousarray(np.asarray(inputs["L0_attn_dt_proj_w"]).T)
    w["outw_t"] = np.ascontiguousarray(np.asarray(inputs["L0_attn_out_proj_w"]).T)
    w["dtb_in"] = _colify(np.asarray(inputs["L0_attn_dt_proj_b"]))
    w["d_in"] = _colify(np.asarray(inputs["L0_attn_D"]))
    al = np.asarray(inputs["L0_attn_A_log"])
    w["alog_in"] = np.ascontiguousarray(
        al.reshape(HB, 128, NST).transpose(1, 0, 2).reshape(128, HB * NST))
    w["gatew_t"] = np.ascontiguousarray(np.asarray(inputs["L0_ffn_gate_w"]).T)
    w["shw1_t"] = np.ascontiguousarray(np.asarray(inputs["L0_ffn_shared_w1"]).T)
    w["shw2_t"] = np.ascontiguousarray(np.asarray(inputs["L0_ffn_shared_w2"]).T)
    w["w1_t"] = np.ascontiguousarray(np.asarray(inputs["L0_ffn_w1"]).transpose(0, 2, 1))
    w["w2_t"] = np.ascontiguousarray(np.asarray(inputs["L0_ffn_w2"]).transpose(0, 2, 1))
    w["ident"] = np.eye(128, dtype=np.float32)
    return w


def kernel(**inputs):
    x = np.asarray(inputs["L0_attn_x"])
    nc, _k = build_nc()
    shared = host_inputs(inputs)
    in_maps = []
    for c in range(8):
        seq, pos = divmod(c, 4)
        t0 = pos * T
        xc = x[seq, t0:t0 + T, :]
        halo = x[seq, t0 - 3:t0, :] if pos > 0 else np.zeros((3, H), np.float32)
        xf = np.ascontiguousarray(np.concatenate([halo, xc], axis=0).T)
        m = np.zeros(8, np.float32)
        for j in range(8):
            if j // 4 == seq and j % 4 < pos:
                m[j] = 1.0
        mask = np.repeat(m[None, :], 128, axis=0).astype(np.float32)
        in_maps.append({"x_fm": xf, "mask_in": mask,
                        "imask_in": np.ascontiguousarray(1.0 - mask), **shared})
    res = run_bass_kernel_spmd(nc, in_maps, core_ids=list(range(8)))
    outs = [res.results[c]["out_fm"].T for c in range(8)]
    full = np.concatenate(outs, axis=0).reshape(2, 4096, H)
    if DEBUG:
        kernel._dbg = res.results
    return full.astype(np.float32)


if __name__ == "__main__":
    rng = np.random.default_rng(0)
    x = rng.standard_normal((2, 4096, H), dtype=np.float32)
    out = kernel(L0_attn_x=x)
    m = x.mean(-1, keepdims=True)
    v = x.var(-1, keepdims=True)
    xn = (x - m) / np.sqrt(v + EPS)
    ref = x + xn
    err = np.abs(out - ref).max()
    print("out err vs x+xn:", err)
    if DEBUG:
        xn_dbg = kernel._dbg[0]["xn_dbg"].T
        ref_xn = xn[0, :T]
        got = xn_dbg[3:, :]
        print("xn err:", np.abs(got - ref_xn).max())


# version 13
# speedup vs baseline: 1.0537x; 1.0537x over previous
import numpy as np
import concourse.bass as bass
import concourse.mybir as mybir
from concourse.bass_utils import run_bass_kernel_spmd

F32 = mybir.dt.float32
BF16 = mybir.dt.bfloat16
AF = mybir.ActivationFunctionType
OP = mybir.AluOpType
AX = mybir.AxisListType

H = 2048
HB = 16
T = 1024
TH = T + 3
SUB = 256
NSUB = T // SUB
NST = 16
DTR = 64
E = 16
I_FF = 1408
IS = 2816
EPS = 1e-5

RACE_SAFE = False
DEBUG = False
DEBUG_FRONT = False
ENGS = ["sync", "gpsimd", "tensor", "vector", "scalar"]


class Sem:
    __slots__ = ("h", "v", "name")

    def __init__(self, handle):
        self.h = handle
        self.v = 0

    def inc(self, n=1):
        self.v += n
        return self.v


class K:

    def __init__(self, nc, bufs, sem_handles):
        self.nc = nc
        self.b = bufs
        self._sh = sem_handles
        self.cur = None
        self.e = None
        self.nsteps_ref = None
        self.nsteps = 0
        self.mode = "emit"
        self.rec = None
        self.plan_chain = {}
        self.plan_wait = {}

    def start_scan_pass(self):
        self.mode = "scan"
        self.cur = "__scan__"
        self.e = None
        self.nsteps = 0
        self.eidx = {}
        self.rec = {e: [] for e in ENGS}
        for name, h in self._sh.items():
            sem = Sem(h)
            sem.name = name
            setattr(self, name, sem)
        self.chains = {e: Sem(self._sh["chain_" + e]) for e in ("vector", "scalar")}
        for e in ("vector", "scalar"):
            self.chains[e].name = "chain_" + e
            setattr(self, "chain_" + e, self.chains[e])

    def start_pass(self, eng_name, eng):
        self.mode = "emit"
        self.cur = eng_name
        self.e = eng
        self.nsteps = 0
        self.eidx = {}
        for name, h in self._sh.items():
            sem = Sem(h)
            sem.name = name
            setattr(self, name, sem)
        self.chains = {e: Sem(self._sh["chain_" + e]) for e in ("vector", "scalar")}
        for e in ("vector", "scalar"):
            self.chains[e].name = "chain_" + e
            setattr(self, "chain_" + e, self.chains[e])

    def end_pass(self):
        if self.nsteps_ref is None:
            self.nsteps_ref = self.nsteps
        else:
            assert self.nsteps == self.nsteps_ref, \
                f"emission diverged on pass {self.cur}: {self.nsteps} != {self.nsteps_ref}"

    def step(self, eng, emit, waits=(), inc=None, inc_n=1, drain=False):
        self.nsteps += 1
        i = self.eidx.get(eng, 0)
        self.eidx[eng] = i + 1
        if self.mode == "scan":
            self.rec[eng].append([bool(drain), inc.name if inc else None,
                                  inc.v + inc_n if inc else None, False])
            v = inc.inc(inc_n) if inc is not None else None
            return v
        chain = self.chains.get(eng)
        needs_chain = self.plan_chain.get(eng, {}).get(i, False)
        v = inc.inc(inc_n) if inc is not None else None
        if needs_chain:
            chain.inc(1)
        if self.cur == eng:
            for s, val in waits:
                if val > 0:
                    self.e.wait_ge(s.h, val)
            if drain:
                dw = self.plan_wait.get(eng, {}).get(i)
                if dw is None:
                    self.e.drain()
                else:
                    sname, sval = dw
                    if sval > 0:
                        self.e.wait_ge(self.sem_by_name(sname).h, sval)
            ins = emit(self.e)
            if inc is not None:
                assert ins is not None, "step with inc= must emit an instruction"
                ins.then_inc(inc.h, inc_n)
            if needs_chain:
                assert ins is not None and inc is None
                ins.then_inc(chain.h, 1)
        return v

    def sem_by_name(self, name):
        return getattr(self, name)

    def plan(self):
        self.plan_chain = {}
        self.plan_wait = {}
        ENGINE_SEMS = {"vec", "scl", "mm"}
        for eng, recs in self.rec.items():
            chain_marked = {}
            waits = {}
            for i, r in enumerate(recs):
                if r[0] and i > 0:
                    p = i - 1
                    while p >= 0 and recs[p][1] is not None \
                            and recs[p][1] not in ENGINE_SEMS:
                        p -= 1
                    if p < 0:
                        continue
                    if recs[p][1] in ENGINE_SEMS:
                        waits[i] = (recs[p][1], recs[p][2])
                    else:
                        chain_marked[p] = True
            cval = 0
            cvals = {}
            for p in sorted(chain_marked):
                cval += 1
                cvals[p] = cval
            for i, r in enumerate(recs):
                if r[0] and i > 0 and i not in waits:
                    p = i - 1
                    if p in cvals:
                        waits[i] = ("chain_" + eng, cvals[p])
            self.plan_chain[eng] = chain_marked
            self.plan_wait[eng] = waits


def ts(i, n=128):
    return slice(i * n, (i + 1) * n)



def emit_consts(k):
    b = k.b
    k.step("vector", lambda e: e.memset(b.ones[:], 1.0), inc=k.vec)
    b.v_consts_vec = k.vec.v
    k.step("sync", lambda e: e.dma_start(out=b.nw_col[:], in_=b.nw_in),
           inc=k.xload, inc_n=16)
    k.step("sync", lambda e: e.dma_start(out=b.convb_col[:], in_=b.convb_in),
           inc=k.xload, inc_n=16)
    k.step("sync", lambda e: e.dma_start(out=b.dtb_col[:], in_=b.dtb_in),
           inc=k.xload, inc_n=16)
    k.step("sync", lambda e: e.dma_start(out=b.d_col[:], in_=b.d_in),
           inc=k.xload, inc_n=16)
    k.step("sync", lambda e: e.dma_start(out=b.m_col[:], in_=b.mask_in),
           inc=k.xload, inc_n=16)
    k.step("sync", lambda e: e.dma_start(out=b.im_col[:], in_=b.imask_in),
           inc=k.xload, inc_n=16)
    k.step("sync", lambda e: e.dma_start(out=b.A_sbuf[:], in_=b.alog_in),
           inc=k.xload, inc_n=16)
    b.v_consts = k.xload.v
    b.v_dtw = k.step("gpsimd",
                     lambda e: e.dma_start(out=b.dtw_slab[:], in_=b.dtw_t),
                     inc=k.wdtw, inc_n=16)
    v = k.step("scalar",
               lambda e: e.activation(out=b.A_sbuf[:], in_=b.A_sbuf[:], func=AF.Exp),
               waits=[(k.xload, b.v_consts)], inc=k.scl)
    b.v_A = k.step("vector",
                   lambda e: e.tensor_scalar_mul(b.A_sbuf[:], b.A_sbuf[:], -1.0),
                   waits=[(k.scl, v)], inc=k.vec)


def emit_ln(k):
    b = k.b
    tslices = [(0, 512), (512, 512), (1024, 3)]

    v_mmblk = {}
    for f in range(HB):
        par = f % 2
        xl = k.xl0 if par == 0 else k.xl1
        vx = k.step(
            "sync",
            lambda e, f=f, par=par: e.dma_start(out=b.xblk[par][:],
                                                in_=b.x_fm[ts(f), :]),
            waits=[(k.mm, v_mmblk.get(f - 2, 0))], inc=xl, inc_n=16)
        vsq = k.step(
            "scalar",
            lambda e, f=f, par=par: e.activation(out=b.sq[:], in_=b.xblk[par][:],
                                                 func=AF.Square),
            waits=[(xl, vx), (k.mm, v_mmblk.get(f - 1, 0))], inc=k.scl)
        for tsi, (t0, tw) in enumerate(tslices):
            k.step(
                "tensor",
                lambda e, f=f, par=par, tsi=tsi, t0=t0, tw=tw: e.matmul(
                    b.pb[2 * tsi][0:1, 0:tw], b.ones[:, 0:1],
                    b.xblk[par][:, t0:t0 + tw],
                    start=(f == 0), stop=(f == HB - 1)),
                waits=[(xl, vx), (k.vec, b.v_consts_vec)] if tsi == 0 else ())
        for tsi, (t0, tw) in enumerate(tslices):
            vmb = k.step(
                "tensor",
                lambda e, f=f, tsi=tsi, t0=t0, tw=tw: e.matmul(
                    b.pb[2 * tsi + 1][0:1, 0:tw], b.ones[:, 0:1],
                    b.sq[:, t0:t0 + tw],
                    start=(f == 0), stop=(f == HB - 1)),
                waits=[(k.scl, vsq)] if tsi == 0 else (),
                inc=(k.mm if tsi == 2 else None))
        v_mmblk[f] = vmb
    v_sums = k.mm.v

    for tsi, (t0, tw) in enumerate(tslices):
        k.step("vector",
               lambda e, tsi=tsi, t0=t0, tw=tw: e.tensor_scalar_mul(
                   b.row_mean[0:1, t0:t0 + tw], b.pb[2 * tsi][0:1, 0:tw], 1.0 / H),
               waits=[(k.mm, v_sums)])
    k.step("vector",
           lambda e: e.scalar_tensor_tensor(
               out=b.row_var[0:1, :], in0=b.row_mean[0:1, :], scalar=-1.0,
               in1=b.row_mean[0:1, :], op0=OP.mult, op1=OP.mult),
           drain=True)
    for tsi, (t0, tw) in enumerate(tslices):
        k.step("vector",
               lambda e, tsi=tsi, t0=t0, tw=tw: e.scalar_tensor_tensor(
                   out=b.row_var[0:1, t0:t0 + tw], in0=b.pb[2 * tsi + 1][0:1, 0:tw],
                   scalar=1.0 / H, in1=b.row_var[0:1, t0:t0 + tw],
                   op0=OP.mult, op1=OP.add),
               drain=(tsi == 0))
    k.step("vector",
           lambda e: e.tensor_scalar_add(b.row_var[0:1, :], b.row_var[0:1, :], EPS),
           drain=True)
    v = k.step("vector",
               lambda e: e.reciprocal(b.row_rstd[0:1, :], b.row_var[0:1, :]),
               drain=True, inc=k.vec)
    b.v_ln_psum_done = v
    v = k.step("scalar",
               lambda e: e.activation(out=b.row_rstd[0:1, :], in_=b.row_rstd[0:1, :],
                                      func=AF.Sqrt),
               waits=[(k.vec, v)], inc=k.scl)
    v_mr = k.step("vector",
                  lambda e: e.scalar_tensor_tensor(
                      out=b.row_mr[0:1, :], in0=b.row_mean[0:1, :], scalar=-1.0,
                      in1=b.row_rstd[0:1, :], op0=OP.mult, op1=OP.mult),
                  waits=[(k.scl, v)], inc=k.vec)
    k.step("scalar",
           lambda e: e.dma_start(out=b.rows_dram[0:1, :], in_=b.row_rstd[0:1, :]),
           waits=[(k.vec, v_mr)], drain=True, inc=k.dmah, inc_n=16)
    v_r2 = k.step("scalar",
                  lambda e: e.dma_start(out=b.rows_dram[1:2, :], in_=b.row_mr[0:1, :]),
                  inc=k.dmah, inc_n=16)
    k.step("scalar",
           lambda e: e.dma_start(out=b.rstd_b[:],
                                 in_=b.rows_dram[0:1, :].partition_broadcast(128)),
           waits=[(k.dmah, v_r2)], inc=k.dmah, inc_n=16)
    v_bc = k.step("scalar",
                  lambda e: e.dma_start(out=b.mr_b[:],
                                        in_=b.rows_dram[1:2, :].partition_broadcast(128)),
                  inc=k.dmah, inc_n=16)

    v_vblk = {}
    for f in range(HB):
        par = f % 2
        xl = k.xl0 if par == 0 else k.xl1
        vx = k.step(
            "sync",
            lambda e, f=f, par=par: e.dma_start(out=b.xblk[par][:],
                                                in_=b.x_fm[ts(f), :]),
            waits=[(k.vec, v_vblk.get(f - 2, 0)), (k.mm, v_sums)],
            inc=xl, inc_n=16)
        k.step("vector",
               lambda e, f=f, par=par: e.tensor_tensor(
                   out=b.xn[:, f * TH:(f + 1) * TH], in0=b.xblk[par][:],
                   in1=b.rstd_b[:, 0:TH], op=OP.mult),
               waits=[(xl, vx), (k.dmah, v_bc)], drain=True)
        k.step("vector",
               lambda e, f=f: e.tensor_tensor(
                   out=b.xn[:, f * TH:(f + 1) * TH],
                   in0=b.xn[:, f * TH:(f + 1) * TH],
                   in1=b.mr_b[:, 0:TH], op=OP.add),
               drain=True)
        ssem = k.st0
        vadd = k.step("vector",
                      lambda e, f=f, par=par: e.tensor_tensor(
                          out=b.sq[:, 0:T], in0=b.xblk[par][:, 3:TH],
                          in1=b.xn[:, f * TH + 3:(f + 1) * TH], op=OP.add),
                      waits=[(ssem, ssem.v)], drain=True, inc=k.vec)
        v_vblk[f] = vadd
        k.step(
            "sync",
            lambda e, f=f: e.dma_start(out=b.out_fm[ts(f), :], in_=b.sq[:, 0:T]),
            waits=[(k.vec, vadd)], inc=ssem, inc_n=16)
    b.v_xn = k.vec.v
    b.v_out_init_st0 = k.st0.v


def emit_r2(k):
    b = k.b
    tslices = [(0, 512), (512, 512), (1024, 3)]
    prev_mm = 0
    for tsi, (t0, tw) in enumerate(tslices):
        bank = 6 if tsi != 1 else 7
        coff = 0 if tsi != 2 else 256
        for f in range(HB):
            vsq = k.step(
                "scalar",
                lambda e, f=f, t0=t0, tw=tw: e.activation(
                    out=b.sq[:, t0:t0 + tw],
                    in_=b.xn[:, f * TH + t0: f * TH + t0 + tw], func=AF.Square),
                waits=[(k.vec, b.v_xn), (k.mm, prev_mm), (k.st0, k.st0.v)],
                inc=k.scl)
            prev_mm = k.step(
                "tensor",
                lambda e, f=f, t0=t0, tw=tw, bank=bank, coff=coff: e.matmul(
                    b.pb[bank][0:1, coff:coff + tw], b.ones[:, 0:1],
                    b.sq[:, t0:t0 + tw],
                    start=(f == 0), stop=(f == HB - 1)),
                waits=[(k.scl, vsq)], inc=k.mm)
    v_sums = prev_mm
    for tsi, (t0, tw) in enumerate(tslices):
        bank = 6 if tsi != 1 else 7
        coff = 0 if tsi != 2 else 256
        k.step("vector",
               lambda e, tsi=tsi, t0=t0, tw=tw, bank=bank, coff=coff:
               e.tensor_scalar(out=b.row_var[0:1, t0:t0 + tw],
                               in0=b.pb[bank][0:1, coff:coff + tw],
                               scalar1=1.0 / H, scalar2=EPS,
                               op0=OP.mult, op1=OP.add),
               waits=[(k.mm, v_sums)])
    v = k.step("vector",
               lambda e: e.reciprocal(b.row_rstd[0:1, :], b.row_var[0:1, :]),
               drain=True, inc=k.vec)
    b.v_r2_psum_done = v
    v = k.step("scalar",
               lambda e: e.activation(out=b.row_rstd[0:1, :], in_=b.row_rstd[0:1, :],
                                      func=AF.Sqrt),
               waits=[(k.vec, v)], inc=k.scl)
    v_rd = k.step("scalar",
                  lambda e: e.dma_start(out=b.rows_dram[2:3, :], in_=b.row_rstd[0:1, :]),
                  drain=True, inc=k.dmah, inc_n=16)
    b.v_r2b = k.step("scalar",
                     lambda e: e.dma_start(
                         out=b.r2_b[:],
                         in_=b.rows_dram[2:3, :].partition_broadcast(128)),
                     waits=[(k.dmah, v_rd)], inc=k.dmah, inc_n=16)


IP_PHASES = [(i * 128, 128) for i in range(32)]


def emit_inproj(k):
    b = k.b
    tslices = [(0, 343), (343, 342), (685, 342)]
    v_mm_phase = {}
    v_scale_phase = {}
    for ph, (o0, W) in enumerate(IP_PHASES):
        par = ph % 2
        wl = k.wload0 if par == 0 else k.wload1
        grp = b.frq % 2
        b.frq += 1
        base = 4 * grp
        vw = k.step(
            "gpsimd",
            lambda e, o0=o0, W=W, par=par: e.dma_start(
                out=b.wslab[par][:, 0:16 * W].rearrange("p (kb w) -> p kb w", kb=16),
                in_=b.inw_t.rearrange("(kb p) o -> p kb o", p=128)[:, :, o0:o0 + W]),
            waits=[(k.mm, v_mm_phase.get(ph - 2, 0))], inc=wl, inc_n=16)
        for kb in range(16):
            k.step(
                "vector",
                lambda e, kb=kb, W=W, par=par: e.tensor_scalar_mul(
                    b.wslab[par][:, kb * W:(kb + 1) * W],
                    b.wslab[par][:, kb * W:(kb + 1) * W],
                    b.nw_col[:, kb:kb + 1]),
                waits=([(wl, vw), (k.mm, v_mm_phase.get(ph - 2, 0)),
                        (k.xload, b.v_consts)] if kb == 0 else ()),
                drain=(kb == 0), inc=(k.vec if kb == 15 else None))
        v_scale_phase[ph] = k.vec.v
        for tsi, (t0, tw) in enumerate(tslices):
            bank = base + tsi
            for kb in range(16):
                wts = []
                if tsi == 0 and kb == 0:
                    wts = [(k.vec, v_scale_phase[ph]),
                           (k.vec, b.fr_prev_vec[grp]),
                           (k.scl, b.fr_prev_scl[grp])]
                    if ph <= 1:
                        wts.append((k.vec, b.v_r2_psum_done))
                is_last = (tsi == 2 and kb == 15)
                k.step(
                    "tensor",
                    lambda e, tsi=tsi, kb=kb, t0=t0, tw=tw, W=W,
                           par=par, bank=bank: e.matmul(
                        b.pb[bank][0:128, 0:tw],
                        b.wslab[par][:, kb * W: kb * W + 128],
                        b.xn[:, kb * TH + t0: kb * TH + t0 + tw],
                        start=(kb == 0), stop=(kb == 15)),
                    waits=wts, inc=(k.mm if is_last else None))
        v_mm_phase[ph] = k.mm.v
        dest = b.xc_pre if o0 < H else b.z
        f = (o0 % H) // 128
        for tsi, (t0, tw) in enumerate(tslices):
            bank = base + tsi
            wts = [(k.mm, v_mm_phase[ph])]
            if tsi == 0:
                wts.append((k.dmah, b.v_r2b))
            k.step(
                "vector",
                lambda e, dest=dest, f=f, t0=t0, tw=tw, bank=bank:
                e.tensor_tensor(
                    out=dest[:, f * TH + t0: f * TH + t0 + tw],
                    in0=b.pb[bank][0:128, 0:tw],
                    in1=b.r2_b[:, t0:t0 + tw], op=OP.mult),
                waits=wts, drain=(tsi == 0),
                inc=(k.vec if tsi == 2 else None))
        b.fr_prev_vec[grp] = k.vec.v
    b.v_inproj_done = k.vec.v
    b.v_inproj_mm = k.mm.v
    for f in range(HB):
        k.step("scalar",
               lambda e, f=f: e.activation(
                   out=b.z[:, f * TH:(f + 1) * TH],
                   in_=b.z[:, f * TH:(f + 1) * TH], func=AF.Silu),
               waits=[(k.vec, b.v_inproj_done)] if f == 0 else (),
               inc=(k.scl if f == HB - 1 else None))
    b.v_sz = k.scl.v


def emit_conv(k):
    b = k.b
    tslices = [(0, 512), (512, 512)]
    v_mm_half = {}
    for ph in range(HB):
        b.conv_grp = b.frq % 2
        b.frq += 1
        b.conv_base = 4 * b.conv_grp
        for half in range(2):
            q = 2 * ph + half
            par = q % 2
            wl = k.wload0 if par == 0 else k.wload1
            vw = k.step(
                "gpsimd",
                lambda e, ph=ph, half=half, par=par: e.dma_start(
                    out=b.wslab[par][:, 0:4096].rearrange("p (c kb w) -> p c kb w", c=2, kb=16),
                    in_=b.convw.rearrange("c (kb p) o -> p c kb o", p=128)[
                        :, 2 * half:2 * half + 2, :, ph * 128:(ph + 1) * 128]),
                waits=[(k.mm, v_mm_half.get(q - 2, b.v_inproj_mm if q < 2 else 0))],
                inc=wl, inc_n=16)
            first = True
            for tsi, (t0, tw) in enumerate(tslices):
                bank = b.conv_base + tsi
                for cl in range(2):
                    c = 2 * half + cl
                    for kb in range(16):
                        wts = []
                        if first:
                            wts = [(wl, vw)]
                            if half == 0:
                                wts += [(k.scl, b.fr_prev_scl[b.conv_grp]),
                                        (k.vec, b.fr_prev_vec[b.conv_grp])]
                            if q == 0:
                                wts.append((k.vec, b.v_inproj_done))
                            first = False
                        is_last = (tsi == 1 and cl == 1 and kb == 15)
                        k.step(
                            "tensor",
                            lambda e, c=c, cl=cl, kb=kb, t0=t0, tw=tw, par=par,
                                   bank=bank: e.matmul(
                                b.pb[bank][:, 0:tw],
                                b.wslab[par][:, (cl * 16 + kb) * 128:(cl * 16 + kb + 1) * 128],
                                b.xc_pre[:, kb * TH + t0 + c: kb * TH + t0 + c + tw],
                                start=(c == 0 and kb == 0), stop=(c == 3 and kb == 15)),
                            waits=wts, inc=(k.mm if is_last else None))
            v_mm_half[q] = k.mm.v
        for tsi, (t0, tw) in enumerate(tslices):
            bank = b.conv_base + tsi
            k.step(
                "scalar",
                lambda e, ph=ph, t0=t0, tw=tw, bank=bank: e.activation(
                    out=b.xc[:, ph * T + t0: ph * T + t0 + tw],
                    in_=b.pb[bank][:, 0:tw], func=AF.Silu,
                    bias=b.convb_col[:, ph:ph + 1], scale=1.0),
                waits=[(k.mm, v_mm_half[2 * ph + 1])],
                inc=(k.scl if tsi == 1 else None))
        b.fr_prev_scl[b.conv_grp] = k.scl.v
    b.v_conv_mm = k.mm.v
    b.v_conv_done = k.scl.v


def emit_xproj(k):
    b = k.b
    vw = k.step(
        "gpsimd",
        lambda e: e.dma_start(
            out=b.wslab[0][:, 0:16 * 96].rearrange("p (kb j) -> p kb j", kb=16),
            in_=b.xpw_t.rearrange("(kb p) j -> p kb j", p=128)),
        waits=[(k.mm, b.v_conv_mm)], inc=k.wload0, inc_n=16)
    tslices = [(0, 512), (512, 512)]
    for tsi, (t0, tw) in enumerate(tslices):
        for kb in range(16):
            wts = []
            if tsi == 0 and kb == 0:
                wts = [(k.wload0, vw), (k.scl, b.v_conv_done),
                       (k.vec, max(b.fr_prev_vec[0], b.fr_prev_vec[1]))]
            k.step(
                "tensor",
                lambda e, kb=kb, t0=t0, tw=tw, tsi=tsi: e.matmul(
                    b.pb[2 + tsi][0:96, 0:tw],
                    b.wslab[0][:, kb * 96:(kb + 1) * 96],
                    b.xc[:, kb * T + t0: kb * T + t0 + tw],
                    start=(kb == 0), stop=(kb == 15)),
                waits=wts, inc=(k.mm if (tsi == 1 and kb == 15) else None))
    v_mm = k.mm.v
    b.v_xproj_mm = v_mm
    for tsi, (t0, tw) in enumerate(tslices):
        k.step("vector",
               lambda e, t0=t0, tw=tw, tsi=tsi: e.tensor_copy(
                   out=b.dbc[0:96, t0:t0 + tw], in_=b.pb[2 + tsi][0:96, 0:tw]),
               waits=[(k.mm, v_mm)], drain=(tsi == 0),
               inc=(k.vec if tsi == 1 else None))
    b.v_dbc = k.vec.v
    b.v_dbc_dram = k.step(
        "sync",
        lambda e: e.dma_start(out=b.dbc_dram.ap(), in_=b.dbc[0:96, :]),
        waits=[(k.vec, b.v_dbc)], inc=k.dmas, inc_n=16)
    k.step("gpsimd",
           lambda e: e.dma_start(out=b.dtb_row_s, in_=b.dtb_row),
           waits=[(k.vec, k.vec.v), (k.scl, k.scl.v)], inc=k.wdtw, inc_n=16)
    b.v_ones_row = k.step("vector",
                          lambda e: e.memset(b.ones_row, 1.0),
                          waits=[(k.scl, k.scl.v), (k.mm, k.mm.v),
                                 (k.st0, k.st0.v)],
                          drain=True, inc=k.vec)



def emit_scan_pass(k, is_c):
    b = k.b
    for sub in range(NSUB):
        s0 = sub * SUB
        v_bm = k.step(
            "scalar",
            lambda e, s0=s0: e.dma_start(
                out=b.bm_rep[:].rearrange("p (n t) -> p n t", n=NST),
                in_=b.dbc_dram.ap()[64:80, s0:s0 + SUB].partition_broadcast(128)),
            waits=[(k.dmas, k.dmas.v), (k.vec, b.vv_bmfree)],
            inc=k.dmah, inc_n=16)
        if is_c:
            v_bm = k.step(
                "scalar",
                lambda e, s0=s0: e.dma_start(
                    out=b.c_rep[:].rearrange("p (n t) -> p n t", n=NST),
                    in_=b.dbc_dram.ap()[80:96, s0:s0 + SUB].partition_broadcast(128)),
                inc=k.dmah, inc_n=16)
        v_bmc = v_bm

        vdt = {}
        vsp = {}
        for hp in range(HB // 2):
            bank = 6 + hp % 2
            wts = [(k.scl, vsp.get(hp - 2, b.v_dtpb_free)), (k.vec, b.v_dbc)]
            if hp < 2:
                wts.append((k.vec, b.v_oppb_free))
            if sub == 0 and hp == 0:
                wts.append((k.wdtw, k.wdtw.v))
                wts.append((k.vec, b.v_ones_row))
            for j in range(2):
                hh = 2 * hp + j
                k.step(
                    "tensor",
                    lambda e, hh=hh, j=j, s0=s0, bank=bank: e.matmul(
                        b.pb[bank][:, j * SUB:(j + 1) * SUB],
                        b.dtw_slab[0:64, hh * 128:(hh + 1) * 128],
                        b.dbc[0:64, s0:s0 + SUB],
                        start=True, stop=False),
                    waits=(wts if j == 0 else ()))
                vdt[hp] = k.step(
                    "tensor",
                    lambda e, hh=hh, j=j, bank=bank: e.matmul(
                        b.pb[bank][:, j * SUB:(j + 1) * SUB],
                        b.dtb_row_s[0:1, hh * 128:(hh + 1) * 128],
                        b.ones_row[0:1, :],
                        start=False, stop=True),
                    inc=(k.mm if j == 1 else None))
            t1 = b.t1p[hp % 2]
            k.step("scalar",
                   lambda e, bank=bank, t1=t1: e.activation(
                       out=t1, in_=b.pb[bank][:, 0:2 * SUB], func=AF.Abs),
                   waits=[(k.mm, vdt[hp]), (k.vec, b.v_tscfree.get(hp % 2, 0))],
                   drain=True)
            k.step("scalar",
                   lambda e, t1=t1: e.activation(out=t1, in_=t1,
                                                 func=AF.Exp, scale=-1.0),
                   drain=True)
            k.step("scalar",
                   lambda e, t1=t1: e.activation(out=t1, in_=t1,
                                                 func=AF.Ln, bias=1.0),
                   drain=True)
            vsp[hp] = k.step("scalar",
                             lambda e, hp=hp, bank=bank: e.activation(
                                 out=b.d_sub[:, 2 * hp * SUB:2 * (hp + 1) * SUB],
                                 in_=b.pb[bank][:, 0:2 * SUB], func=AF.Relu),
                             waits=[(k.vec, b.v_dsub_free_vec)],
                             drain=True, inc=k.scl)
            vd = k.step("vector",
                        lambda e, hp=hp, t1=t1: e.tensor_tensor(
                            out=b.d_sub[:, 2 * hp * SUB:2 * (hp + 1) * SUB],
                            in0=b.d_sub[:, 2 * hp * SUB:2 * (hp + 1) * SUB],
                            in1=t1, op=OP.add),
                        waits=[(k.scl, vsp[hp])], drain=True, inc=k.vec)
            b.v_tscfree[hp % 2] = vd
        b.v_dtpb_free = k.scl.v
        v_delta = k.vec.v

        k.step("vector",
               lambda e, s0=s0: e.tensor_tensor(
                   out=b.utmp_all[:].rearrange("p (h t) -> p h t", h=HB),
                   in0=b.d_sub[:].rearrange("p (h t) -> p h t", h=HB),
                   in1=b.xc[:].rearrange("p (h t) -> p h t", h=HB)[:, :, s0:s0 + SUB],
                   op=OP.mult),
               drain=True, inc=k.vec)
        v_delta = k.vec.v

        if not is_c:
            k.step("vector",
                   lambda e: e.tensor_reduce(
                       out=b.aprod[:, 0:16],
                       in_=b.d_sub[:].rearrange("p (h t) -> p h t", h=HB),
                       axis=AX.X, op=OP.add),
                   waits=[(k.vec, v_delta)], drain=True)
            if sub == 0:
                k.step("vector",
                       lambda e: e.tensor_copy(out=b.sumd[:], in_=b.aprod[:, 0:16]),
                       drain=True, inc=k.vec)
            else:
                k.step("vector",
                       lambda e: e.tensor_tensor(out=b.sumd[:], in0=b.sumd[:],
                                                 in1=b.aprod[:, 0:16], op=OP.add),
                       drain=True, inc=k.vec)

        b.v_dsub_free_vec = k.vec.v
        for hh in range(HB):
            par = hh % 2
            ab = b.ab[par]
            for n in range(NST):
                b.v_abprod[par] = k.step(
                    "scalar",
                    lambda e, hh=hh, n=n, ab=ab: e.activation(
                        out=ab[:, n * SUB:(n + 1) * SUB],
                        in_=b.d_sub[:, hh * SUB:(hh + 1) * SUB], func=AF.Exp,
                        scale=b.A_sbuf[:, hh * NST + n: hh * NST + n + 1]),
                    waits=([(k.vec, v_delta), (k.vec, b.v_abfree.get(par, 0)),
                            (k.vec, b.v_A)] if n == 0 else ()),
                    inc=(k.scl if n == NST - 1 else None))
            k.step(
                "vector",
                lambda e, hh=hh: e.tensor_tensor(
                    out=b.bb[:].rearrange("p (n t) -> p n t", n=NST),
                    in0=b.utmp_all[:, hh * SUB:(hh + 1) * SUB]
                        .unsqueeze(1).broadcast_to([128, NST, SUB]),
                    in1=b.bm_rep[:].rearrange("p (n t) -> p n t", n=NST),
                    op=OP.mult),
                waits=[(k.dmah, v_bmc), (k.vec, b.v_bbfree)], drain=True)
            for n in range(NST):
                k.step(
                    "vector",
                    lambda e, hh=hh, n=n, ab=ab: e.tensor_tensor_scan(
                        out=b.bb[:, n * SUB:(n + 1) * SUB],
                        data0=ab[:, n * SUB:(n + 1) * SUB],
                        data1=b.bb[:, n * SUB:(n + 1) * SUB],
                        initial=b.state[:, hh * NST + n: hh * NST + n + 1],
                        op0=OP.mult, op1=OP.add),
                    waits=[(k.scl, b.v_abprod[par])] if n == 0 else (),
                    drain=(n == 0))
            vst = k.step("vector",
                         lambda e, hh=hh: e.tensor_copy(
                             out=b.state[:, hh * NST:(hh + 1) * NST],
                             in_=b.bb[:].rearrange("p (n t) -> p n t", n=NST)[:, :, SUB - 1]),
                         drain=True, inc=k.vec)
            b.v_abfree[par] = vst
            b.vv_bmfree = vst
            b.v_bbfree = vst
            if is_c:
                k.step("vector",
                       lambda e: e.tensor_tensor(
                           out=b.bb[:], in0=b.bb[:], in1=b.c_rep[:], op=OP.mult),
                       drain=True)
                k.step("vector",
                       lambda e: e.tensor_tensor(
                           out=b.bb[:, 0:8 * SUB], in0=b.bb[:, 0:8 * SUB],
                           in1=b.bb[:, 8 * SUB:16 * SUB], op=OP.add),
                       drain=True)
                k.step("vector",
                       lambda e: e.tensor_tensor(
                           out=b.bb[:, 0:4 * SUB], in0=b.bb[:, 0:4 * SUB],
                           in1=b.bb[:, 4 * SUB:8 * SUB], op=OP.add),
                       drain=True)
                k.step("vector",
                       lambda e: e.tensor_tensor(
                           out=b.bb[:, 0:2 * SUB], in0=b.bb[:, 0:2 * SUB],
                           in1=b.bb[:, 2 * SUB:4 * SUB], op=OP.add),
                       drain=True)
                vrhs = k.step("vector",
                              lambda e, hh=hh: e.tensor_tensor(
                                  out=b.rhs[:, hh * SUB:(hh + 1) * SUB],
                                  in0=b.bb[:, 0:SUB], in1=b.bb[:, SUB:2 * SUB],
                                  op=OP.add),
                              drain=True, inc=k.vec)
                b.v_bbfree = vrhs
                b.vv_bmfree = vrhs
        if is_c:
            k.step("vector",
                   lambda e, s0=s0: e.tensor_tensor(
                       out=b.dxc[:].rearrange("p (h t) -> p h t", h=HB),
                       in0=b.xc[:].rearrange("p (h t) -> p h t", h=HB)[:, :, s0:s0 + SUB],
                       in1=b.d_col[:].unsqueeze(2).broadcast_to([128, HB, SUB]),
                       op=OP.mult),
                   drain=True)
            k.step("vector",
                   lambda e: e.tensor_tensor(out=b.rhs[:], in0=b.rhs[:],
                                             in1=b.dxc[:], op=OP.add),
                   drain=True)
            vrhs = k.step("vector",
                          lambda e, s0=s0: e.tensor_tensor(
                              out=b.rhs[:].rearrange("p (h t) -> p h t", h=HB),
                              in0=b.rhs[:].rearrange("p (h t) -> p h t", h=HB),
                              in1=b.z[:].rearrange("p (h t) -> p h t", h=HB)[
                                  :, :, 3 + s0:3 + s0 + SUB],
                              op=OP.mult),
                          drain=True, inc=k.vec)
            emit_outproj_sub(k, sub)


def emit_outproj_sub(k, sub):
    b = k.b
    v_rhs = k.vec.v
    for ph in range(8):
        o0 = ph * 256
        q, par, wl, vw = _slab(
            k, b.outw_t.rearrange("(kb p) o -> p kb o", p=128)[:, :, o0:o0 + 256]
                .rearrange("p kb o -> p (kb o)") if False else
            b.outw_t.rearrange("(kb p) o -> p kb o", p=128)[:, :, o0:o0 + 256],
            4096)
        for ob in range(2):
            bank = 6 + ob
            for kb in range(16):
                wts = []
                if ob == 0 and kb == 0:
                    wts = [(wl, vw), (k.vec, v_rhs),
                           (k.vec, b.op_evac.get(q - 1, 0)),
                           (k.scl, b.v_dtpb_free)]
                k.step(
                    "tensor",
                    lambda e, ob=ob, kb=kb, par=par, bank=bank: e.matmul(
                        b.pb[bank][:, 0:SUB],
                        b.wslab[par][:, kb * 256 + ob * 128: kb * 256 + ob * 128 + 128],
                        b.rhs[:, kb * SUB:(kb + 1) * SUB],
                        start=(kb == 0), stop=(kb == 15)),
                    waits=wts,
                    inc=(k.mm if (ob == 1 and kb == 15) else None))
        b.op_mm[q] = k.mm.v
        b.slab_mm[q] = k.mm.v
        for ob in range(2):
            bank = 6 + ob
            o = o0 + ob * 128
            f = o // 128
            ssem = k.acc0 if ob == 0 else k.acc1
            stg = b.sq[:, ob * SUB:(ob + 1) * SUB]
            vev = k.step(
                "vector",
                lambda e, bank=bank, stg=stg: e.tensor_copy(out=stg,
                                                            in_=b.pb[bank][:, 0:SUB]),
                waits=[(k.mm, b.op_mm[q]), (ssem, ssem.v)], drain=True,
                inc=k.vec)
            k.step(
                "gpsimd",
                lambda e, f=f, sub=sub, stg=stg: e.dma_start(
                    out=b.out_fm[ts(f), sub * SUB:(sub + 1) * SUB], in_=stg,
                    accum_op=OP.add),
                waits=[(k.vec, vev), (k.st0, b.v_out_init_st0)],
                inc=ssem, inc_n=16)
        b.op_evac[q] = k.vec.v
    b.v_oppb_free = k.vec.v



def emit_gather(k, n_cores):
    b = k.b
    v = k.step("vector",
               lambda e: e.tensor_tensor(
                   out=b.aprod[:].rearrange("p (h n) -> p h n", n=NST),
                   in0=b.sumd[:].unsqueeze(2).broadcast_to([128, HB, NST]),
                   in1=b.A_sbuf[:].rearrange("p (h n) -> p h n", n=NST),
                   op=OP.mult),
               drain=True, inc=k.vec)
    v = k.step("scalar",
               lambda e: e.activation(out=b.aprod[:], in_=b.aprod[:], func=AF.Exp),
               waits=[(k.vec, v)], drain=True, inc=k.scl)
    k.step("sync",
           lambda e: e.dma_start(out=b.ag_in.ap()[:, 0:256], in_=b.aprod[:]),
           waits=[(k.scl, v)], inc=k.dmas, inc_n=16)
    v_ag = k.step("sync",
                  lambda e: e.dma_start(out=b.ag_in.ap()[:, 256:512], in_=b.state[:]),
                  waits=[(k.vec, k.vec.v)], inc=k.dmas, inc_n=16)
    v_cc = k.step("gpsimd",
                  lambda e, n_cores=n_cores: e.collective_compute(
                      "AllGather", OP.bypass,
                      replica_groups=[list(range(n_cores))],
                      ins=[b.ag_in.ap().opt()],
                      outs=[b.ag_out.ap().opt()]),
                  waits=[(k.dmas, v_ag)], inc=k.cc)
    v_g = k.step("gpsimd",
                 lambda e, n_cores=n_cores: e.dma_start(
                     out=b.gath[:, 0:n_cores * 512].rearrange("p (r c) -> p r c", r=n_cores),
                     in_=b.ag_out.ap().rearrange("(r p) c -> p r c", p=128)),
                 waits=[(k.cc, v_cc)], inc=k.wgath, inc_n=16)
    v = k.step("vector", lambda e: e.memset(b.hin[:], 0.0),
               waits=[(k.wgath, v_g)], drain=True, inc=k.vec)
    for j in range(n_cores):
        k.step("vector",
               lambda e, j=j: e.scalar_tensor_tensor(
                   out=b.tsc1[0], in0=b.gath[:, j * 512: j * 512 + 256],
                   scalar=b.m_col[:, j:j + 1],
                   in1=b.im_col[:, j:j + 1].to_broadcast([128, 256]),
                   op0=OP.mult, op1=OP.add),
               drain=True)
        k.step("vector",
               lambda e: e.tensor_tensor(out=b.hin[:], in0=b.hin[:],
                                         in1=b.tsc1[0], op=OP.mult),
               drain=True)
        k.step("vector",
               lambda e, j=j: e.tensor_scalar_mul(
                   b.tsc1[0], b.gath[:, j * 512 + 256:(j + 1) * 512],
                   b.m_col[:, j:j + 1]),
               drain=True)
        k.step("vector",
               lambda e: e.tensor_tensor(out=b.hin[:], in0=b.hin[:],
                                         in1=b.tsc1[0], op=OP.add),
               drain=True)
    b.v_hin = k.step("vector",
                     lambda e: e.tensor_copy(out=b.state[:], in_=b.hin[:]),
                     drain=True, inc=k.vec)
    b.v_tscfree = {0: b.v_hin, 1: b.v_hin}



def _slab(k, src_ap, width):
    b = k.b
    q = b.wq
    b.wq += 1
    par = q % 2
    wl = k.wload0 if par == 0 else k.wload1
    nd = len(src_ap.shape)
    dst = b.wslab[par][:, 0:width]
    if nd == 3:
        dst = dst.rearrange("p (a c) -> p a c", a=src_ap.shape[1])
    vw = k.step(
        "gpsimd",
        lambda e, src_ap=src_ap, dst=dst: e.dma_start(out=dst, in_=src_ap),
        waits=[(k.mm, b.slab_mm.get(q - 2, b.v_xproj_mm if q < 2 else 0))],
        inc=wl, inc_n=16)
    return q, par, wl, vw


def emit_gate(k):
    b = k.b
    ptile = b.tsc1[0][:, 0:16]
    negmax = b.tsc1[0][:, 32:33]
    rcp = b.tsc1[0][:, 34:35]
    m8 = b.tsc1[0][:, 40:48]
    mask = b.tsc1[0][:, 64:80]
    ctile = b.ctile[:]
    cstage = b.tsc2[0][0:16, 0:128]
    k.step("gpsimd",
           lambda e: e.dma_start(
               out=b.gslab[:].rearrange("p (kb e) -> p kb e", kb=16),
               in_=b.gatew_t.rearrange("(kb p) e -> p kb e", p=128)),
           inc=k.wdtw, inc_n=16)
    v_id = k.step("gpsimd",
                  lambda e: e.dma_start(out=b.ident_bf[:], in_=b.ident),
                  inc=k.wdtw, inc_n=16)
    v_gw = v_id
    v_cdram = b.v_cfm_dram = 0
    for tt in range(8):
        t0 = 3 + tt * 128
        vmm = k.step(
            "tensor",
            lambda e, tt=tt, t0=t0: [
                e.matmul(b.pb[4][:, 0:16],
                         b.xn[:, kb * TH + t0: kb * TH + t0 + 128],
                         b.gslab[:, kb * 16:(kb + 1) * 16],
                         start=(kb == 0), stop=(kb == 15))
                for kb in range(16)][-1],
            waits=([(k.wdtw, v_gw), (k.vec, b.v_dbc)]
                   if tt == 0 else [(k.vec, k.vec.v)]),
            inc=k.mm)
        k.step("vector",
               lambda e: e.tensor_reduce(out=negmax[:, :], in_=b.pb[4][:, 0:16],
                                         axis=AX.X, op=OP.max),
               waits=[(k.mm, vmm)], drain=True)
        k.step("vector",
               lambda e: e.tensor_scalar_mul(negmax, negmax, -1.0), drain=True)
        vneg = k.step("vector", lambda e: e.engine_nop(), drain=True, inc=k.vec)
        vexp = k.step("scalar",
                      lambda e: e.activation(out=ptile, in_=b.pb[4][:, 0:16],
                                             func=AF.Exp, bias=negmax),
                      waits=[(k.vec, vneg)], drain=True, inc=k.scl)
        k.step("vector",
               lambda e: e.tensor_reduce(out=rcp, in_=ptile, axis=AX.X, op=OP.add),
               waits=[(k.scl, vexp)], drain=True)
        k.step("vector", lambda e: e.reciprocal(rcp, rcp), drain=True)
        k.step("vector",
               lambda e: e.tensor_scalar_mul(ptile, ptile, rcp), drain=True)
        k.step("vector", lambda e: e.max(out=m8, in_=ptile), drain=True)
        k.step("vector",
               lambda e: e.tensor_scalar(out=mask, in0=ptile, scalar1=m8[:, 5:6],
                                         scalar2=None, op0=OP.is_ge),
               drain=True)
        vct = k.step("vector",
                     lambda e: e.tensor_tensor(out=ctile, in0=ptile, in1=mask,
                                               op=OP.mult),
                     drain=True, inc=k.vec)
        vtr = k.step("tensor",
                     lambda e: e.transpose(b.pb[5][0:16, 0:64].bitcast(BF16), ctile,
                                           b.ident_bf[:]),
                     waits=[(k.vec, vct), (k.wdtw, v_id),
                            (k.vec, b.v_ctr_prev)],
                     inc=k.mm)
        b.v_ctr_prev = k.step(
            "vector",
            lambda e: e.tensor_copy(out=cstage,
                                    in_=b.pb[5][0:16, 0:64].bitcast(BF16)),
            waits=[(k.mm, vtr), (k.dmas, v_cdram)], drain=True, inc=k.vec)
        v_cdram = k.step(
            "sync",
            lambda e, tt=tt: e.dma_start(
                out=b.combine_dram.ap()[:, tt * 128:(tt + 1) * 128], in_=cstage),
            waits=[(k.vec, b.v_ctr_prev)], inc=k.dmas, inc_n=16)
    b.v_cfm_dram = v_cdram


def emit_shared_up_phase(k, ph):
    b = k.b
    o0 = ph * 256
    grp = b.bgrp % 2
    b.bgrp += 1
    base = 4 * grp
    q, par, wl, vw = _slab(
        k, b.shw1_t.rearrange("(kb p) o -> p kb o", p=128)[:, :, o0:o0 + 256], 4096)
    for ob in range(2):
        for tti, tw0 in enumerate((3, 515)):
            bank = base + 2 * ob + tti
            for kb in range(16):
                wts = []
                if ob == 0 and tti == 0 and kb == 0:
                    wts = [(wl, vw),
                           (k.vec, max(b.grp_prev_vec[grp], b.v_acc_zero)),
                           (k.scl, b.grp_prev_scl[grp])]
                k.step(
                    "tensor",
                    lambda e, ob=ob, kb=kb, tw0=tw0, par=par, bank=bank: e.matmul(
                        b.pb[bank][:, 0:512],
                        b.wslab[par][:, kb * 256 + ob * 128: kb * 256 + ob * 128 + 128],
                        b.xn[:, kb * TH + tw0: kb * TH + tw0 + 512],
                        start=(kb == 0), stop=(kb == 15)),
                    waits=wts,
                    inc=(k.mm if (ob == 1 and tti == 1 and kb == 15) else None))
    b.moe_mm[q] = k.mm.v
    b.slab_mm[q] = k.mm.v
    for ob in range(2):
        o = o0 + ob * 128
        for tti in range(2):
            bank = base + 2 * ob + tti
            if o < IS:
                ib = o // 128
                k.step("scalar",
                       lambda e, ib=ib, tti=tti, bank=bank: e.activation(
                           out=b.sg_s[:, ib * 1024 + tti * 512: ib * 1024 + tti * 512 + 512],
                           in_=b.pb[bank][:, 0:512], func=AF.Silu),
                       waits=[(k.mm, b.moe_mm[q])],
                       inc=(k.scl if (tti == 1) else None))
            else:
                ib = (o - IS) // 128
                k.step("vector",
                       lambda e, ib=ib, tti=tti, bank=bank: e.tensor_tensor(
                           out=b.sg_s[:, ib * 1024 + tti * 512: ib * 1024 + tti * 512 + 512],
                           in0=b.sg_s[:, ib * 1024 + tti * 512: ib * 1024 + tti * 512 + 512],
                           in1=b.pb[bank][:, 0:512], op=OP.mult),
                       waits=[(k.mm, b.moe_mm[q]), (k.scl, b.v_shared_gate_done)],
                       drain=True,
                       inc=(k.vec if (tti == 1) else None))
    if o0 + 256 == IS:
        b.v_shared_gate_done = k.scl.v
    b.grp_prev_vec[grp] = k.vec.v
    b.grp_prev_scl[grp] = k.scl.v
    if ph == 2 * IS // 256 - 1:
        b.v_shared_act = k.vec.v


def emit_shared_down(k):
    b = k.b
    ISB = IS // 128
    for ph in range(HB):
        grp = b.bgrp % 2
        b.bgrp += 1
        base = 4 * grp
        q, par, wl, vw = _slab(
            k, b.shw2_t.rearrange("(kb p) o -> p kb o", p=128)[:, :, ph * 128:(ph + 1) * 128],
            ISB * 128)
        for tti in range(2):
            bank = base + tti
            for kb in range(ISB):
                wts = []
                if tti == 0 and kb == 0:
                    wts = [(wl, vw),
                           (k.vec, max(b.v_shared_act, b.grp_prev_vec[grp])),
                           (k.scl, b.grp_prev_scl[grp])]
                k.step(
                    "tensor",
                    lambda e, kb=kb, tti=tti, par=par, bank=bank: e.matmul(
                        b.pb[bank][:, 0:512],
                        b.wslab[par][:, kb * 128:(kb + 1) * 128],
                        b.sg_s[:, kb * 1024 + tti * 512: kb * 1024 + tti * 512 + 512],
                        start=(kb == 0), stop=(kb == ISB - 1)),
                    waits=wts,
                    inc=(k.mm if (tti == 1 and kb == ISB - 1) else None))
        b.moe_mm[q] = k.mm.v
        b.slab_mm[q] = k.mm.v
        for tti in range(2):
            bank = base + tti
            k.step("vector",
                   lambda e, ph=ph, tti=tti, bank=bank: e.tensor_tensor(
                       out=b.acc[:, ph * 1024 + tti * 512: ph * 1024 + tti * 512 + 512],
                       in0=b.acc[:, ph * 1024 + tti * 512: ph * 1024 + tti * 512 + 512],
                       in1=b.pb[bank][:, 0:512], op=OP.add),
                   waits=[(k.mm, b.moe_mm[q]), (k.vec, b.v_acc_zero)],
                   drain=True, inc=(k.vec if tti == 1 else None))
        b.grp_prev_vec[grp] = k.vec.v


def emit_experts(k):
    b = k.b
    IB = I_FF // 128
    for ex in range(E):
        par_e = ex % 2
        act = b.act[par_e]
        cb = b.cb[par_e]
        wcb = k.wcb0 if par_e == 0 else k.wcb1
        v_cb = k.step(
            "gpsimd",
            lambda e, ex=ex, cb=cb: e.dma_start(
                out=cb, in_=b.combine_dram.ap()[ex:ex + 1, :].partition_broadcast(128)),
            waits=[(k.dmas, b.v_cfm_dram), (k.vec, b.cb_free.get(par_e, 0))],
            inc=wcb, inc_n=16)
        for ph in range(IB):
            o0 = ph * 256
            grp = b.bgrp % 2
            b.bgrp += 1
            base = 4 * grp
            q, par, wl, vw = _slab(
                k, b.w1_t.rearrange("e (kb p) o -> e p kb o", p=128)[ex][:, :, o0:o0 + 256],
                4096)
            for ob in range(2):
                for tti, tw0 in enumerate((3, 515)):
                    bank = base + 2 * ob + tti
                    for kb in range(16):
                        wts = []
                        if ob == 0 and tti == 0 and kb == 0:
                            wts = [(wl, vw),
                                   (k.vec, b.grp_prev_vec[grp]),
                                   (k.scl, b.grp_prev_scl[grp])]
                        k.step(
                            "tensor",
                            lambda e, ob=ob, kb=kb, tw0=tw0, par=par, bank=bank: e.matmul(
                                b.pb[bank][:, 0:512],
                                b.wslab[par][:, kb * 256 + ob * 128: kb * 256 + ob * 128 + 128],
                                b.xn[:, kb * TH + tw0: kb * TH + tw0 + 512],
                                start=(kb == 0), stop=(kb == 15)),
                            waits=wts,
                            inc=(k.mm if (ob == 1 and tti == 1 and kb == 15) else None))
            b.moe_mm[q] = k.mm.v
            b.slab_mm[q] = k.mm.v
            for ob in range(2):
                o = o0 + ob * 128
                for tti in range(2):
                    bank = base + 2 * ob + tti
                    if o < I_FF:
                        ib = o // 128
                        k.step("scalar",
                               lambda e, ib=ib, tti=tti, bank=bank, act=act: e.activation(
                                   out=act[:, ib * 1024 + tti * 512: ib * 1024 + tti * 512 + 512],
                                   in_=b.pb[bank][:, 0:512], func=AF.Silu),
                               waits=[(k.mm, b.moe_mm[q]),
                                      (k.vec, b.act_free.get(par_e, 0))],
                               inc=(k.scl if tti == 1 else None))
                    else:
                        ib = (o - I_FF) // 128
                        sl = lambda act, ib, tti: act[:, ib * 1024 + tti * 512: ib * 1024 + tti * 512 + 512]
                        k.step("vector",
                               lambda e, ib=ib, tti=tti, bank=bank, act=act, sl=sl: e.tensor_tensor(
                                   out=sl(act, ib, tti), in0=sl(act, ib, tti),
                                   in1=b.pb[bank][:, 0:512], op=OP.mult),
                               waits=[(k.mm, b.moe_mm[q])],
                               drain=True)
                        k.step("vector",
                               lambda e, ib=ib, tti=tti, act=act, cb=cb, sl=sl: e.tensor_tensor(
                                   out=sl(act, ib, tti), in0=sl(act, ib, tti),
                                   in1=cb[:, tti * 512:(tti + 1) * 512], op=OP.mult),
                               waits=[(wcb, v_cb)], drain=True,
                               inc=(k.vec if (tti == 1 and ob == 1) else None))
            b.grp_prev_vec[grp] = k.vec.v
            b.grp_prev_scl[grp] = k.scl.v
        v_act = k.vec.v
        b.cb_free[par_e] = v_act
        for ph in range(HB):
            grp = b.bgrp % 2
            b.bgrp += 1
            base = 4 * grp
            q, par, wl, vw = _slab(
                k, b.w2_t.rearrange("e (kb p) o -> e p kb o", p=128)[ex][:, :, ph * 128:(ph + 1) * 128],
                IB * 128)
            for tti in range(2):
                bank = base + tti
                for kb in range(IB):
                    wts = []
                    if tti == 0 and kb == 0:
                        wts = [(wl, vw),
                               (k.vec, max(v_act, b.grp_prev_vec[grp])),
                               (k.scl, b.grp_prev_scl[grp])]
                    k.step(
                        "tensor",
                        lambda e, kb=kb, tti=tti, par=par, bank=bank, act=act: e.matmul(
                            b.pb[bank][:, 0:512],
                            b.wslab[par][:, kb * 128:(kb + 1) * 128],
                            act[:, kb * 1024 + tti * 512: kb * 1024 + tti * 512 + 512],
                            start=(kb == 0), stop=(kb == IB - 1)),
                        waits=wts,
                        inc=(k.mm if (tti == 1 and kb == IB - 1) else None))
            b.moe_mm[q] = k.mm.v
            b.slab_mm[q] = k.mm.v
            for tti in range(2):
                bank = base + tti
                k.step("vector",
                       lambda e, ph=ph, tti=tti, bank=bank: e.tensor_tensor(
                           out=b.acc[:, ph * 1024 + tti * 512: ph * 1024 + tti * 512 + 512],
                           in0=b.acc[:, ph * 1024 + tti * 512: ph * 1024 + tti * 512 + 512],
                           in1=b.pb[bank][:, 0:512], op=OP.add),
                       waits=[(k.mm, b.moe_mm[q])], drain=True,
                       inc=(k.vec if tti == 1 else None))
            b.grp_prev_vec[grp] = k.vec.v
        b.act_free[par_e] = k.vec.v


def emit_moe_tail(k):
    b = k.b
    b.v_acc_zero = k.step(
        "vector", lambda e: e.memset(b.acc[:], 0.0),
        waits=[(k.vec, k.vec.v), (k.acc0, k.acc0.v), (k.acc1, k.acc1.v)],
        drain=True, inc=k.vec)
    for ph in range(2 * IS // 256):
        emit_shared_up_phase(k, ph)
    emit_shared_down(k)
    emit_experts(k)
    vflush_gate = [(k.vec, k.vec.v), (k.acc0, k.acc0.v), (k.acc1, k.acc1.v),
                   (k.st0, b.v_out_init_st0)]
    for f in range(HB):
        ssem = k.acc0 if f % 2 == 0 else k.acc1
        k.step("gpsimd",
               lambda e, f=f: e.dma_start(
                   out=b.out_fm[ts(f), :], in_=b.acc[:, f * 1024:(f + 1) * 1024],
                   accum_op=OP.add),
               waits=(vflush_gate if f == 0 else ()), inc=ssem, inc_n=16)


def emit_debug(k):
    if not DEBUG:
        return
    b = k.b
    if not DEBUG_FRONT:
        k.step("sync",
               lambda e: e.dma_start(
                   out=b.xn_dbg.rearrange("(f p) t -> p f t", p=128),
                   in_=b.xn[:].rearrange("p (f t) -> p f t", f=HB)),
               waits=[(k.vec, b.v_xn)], inc=k.st2, inc_n=16)
        return
    k.step("sync",
           lambda e: e.dma_start(
               out=b.xn_dbg.rearrange("(f p) t -> p f t", p=128),
               in_=b.xn[:].rearrange("p (f t) -> p f t", f=HB)),
           waits=[(k.vec, b.v_xn)], inc=k.st2, inc_n=16)
    k.step("sync",
           lambda e: e.dma_start(
               out=b.z_dbg.rearrange("(f p) t -> p f t", p=128),
               in_=b.z[:].rearrange("p (f t) -> p f t", f=HB)),
           waits=[(k.vec, b.v_inproj_done)], inc=k.st2, inc_n=16)
    k.step("sync",
           lambda e: e.dma_start(
               out=b.xc_dbg.rearrange("(f p) t -> p f t", p=128),
               in_=b.xc[:].rearrange("p (f t) -> p f t", f=HB)),
           waits=[(k.scl, b.v_conv_done)], inc=k.st2, inc_n=16)
    k.step("sync",
           lambda e: e.dma_start(out=b.dbc_dbg, in_=b.dbc[0:96, :]),
           waits=[(k.vec, b.v_dbc)], inc=k.st2, inc_n=16)
    k.step("sync",
           lambda e: e.dma_start(out=b.st_dbg[:, 0:512], in_=b.rhs[:, 0:1024]
                                 .bitcast(F32)),
           waits=[(k.vec, k.vec.v)], inc=k.st2, inc_n=16)


def program(k, n_cores=8):
    b = k.b
    b.vscl_bank = {}
    b.vscl_exp = 0
    b.vv_bmfree = 0
    b.v_tscfree = 0
    b.op_mm = {}
    b.op_evac = {}
    b.opq = 0
    b.wq = 0
    b.slab_mm = {}
    b.shared_ph_iter = None
    b.v_ctr_prev = 0
    b.v_shared_act = 0
    b.shared_fed = 0
    b.frq = 0
    b.fr_prev_vec = {0: 0, 1: 0}
    b.fr_prev_scl = {0: 0, 1: 0}
    b.bgrp = 0
    b.grp_prev_vec = {0: 0, 1: 0}
    b.grp_prev_scl = {0: 0, 1: 0}
    b.moe_mm = {}
    b.moe_evac = {}
    b.moe_sevac = {}
    b.cb_free = {}
    b.act_free = {}
    b.v_scans = 0
    b.v_dtpb_free = 0
    b.v_oppb_free = 0
    b.v_abprod = {}
    b.v_abfree = {}
    b.v_bbfree = 0
    b.v_dsub_free_vec = 0
    b.v_tscfree = {}
    b.eg_gate_done = {}
    b.v_shared_gate_done = 0
    emit_consts(k)
    emit_ln(k)
    emit_r2(k)
    emit_inproj(k)
    emit_conv(k)
    emit_xproj(k)
    emit_gate(k)
    emit_scan_pass(k, is_c=False)
    emit_gather(k, n_cores)
    emit_scan_pass(k, is_c=True)
    emit_moe_tail(k)
    emit_debug(k)
    waits = [(s, s.v) for s in (k.st0, k.st1, k.st2, k.st3, k.acc0, k.acc1)]
    k.step("sync", lambda e: e.nop(), waits=waits)



class Bufs:
    pass


def build_nc(n_cores=8):
    nc = bass.Bass("TRN2", target_bir_lowering=False)
    b = Bufs()
    b.x_fm = nc.dram_tensor("x_fm", [H, TH], F32, kind="ExternalInput").ap()
    b.inw_t = nc.dram_tensor("inw_t", [H, 2 * H], F32, kind="ExternalInput").ap()
    b.convw = nc.dram_tensor("convw", [4, H, H], F32, kind="ExternalInput").ap()
    b.xpw_t = nc.dram_tensor("xpw_t", [H, 96], F32, kind="ExternalInput").ap()
    b.dtw_t = nc.dram_tensor("dtw_t", [DTR, H], F32, kind="ExternalInput").ap()
    b.dtb_row = nc.dram_tensor("dtb_row", [1, H], F32, kind="ExternalInput").ap()
    b.outw_t = nc.dram_tensor("outw_t", [H, H], F32, kind="ExternalInput").ap()
    b.nw_in = nc.dram_tensor("nw_in", [128, HB], F32, kind="ExternalInput").ap()
    b.convb_in = nc.dram_tensor("convb_in", [128, HB], F32, kind="ExternalInput").ap()
    b.dtb_in = nc.dram_tensor("dtb_in", [128, HB], F32, kind="ExternalInput").ap()
    b.d_in = nc.dram_tensor("d_in", [128, HB], F32, kind="ExternalInput").ap()
    b.alog_in = nc.dram_tensor("alog_in", [128, HB * NST], F32, kind="ExternalInput").ap()
    b.mask_in = nc.dram_tensor("mask_in", [128, 8], F32, kind="ExternalInput").ap()
    b.imask_in = nc.dram_tensor("imask_in", [128, 8], F32, kind="ExternalInput").ap()
    b.gatew_t = nc.dram_tensor("gatew_t", [H, E], F32, kind="ExternalInput").ap()
    b.shw1_t = nc.dram_tensor("shw1_t", [H, 2 * IS], F32, kind="ExternalInput").ap()
    b.shw2_t = nc.dram_tensor("shw2_t", [IS, H], F32, kind="ExternalInput").ap()
    b.w1_t = nc.dram_tensor("w1_t", [E, H, 2 * I_FF], F32, kind="ExternalInput").ap()
    b.w2_t = nc.dram_tensor("w2_t", [E, I_FF, H], F32, kind="ExternalInput").ap()
    b.ident = nc.dram_tensor("ident", [128, 128], F32, kind="ExternalInput").ap()
    b.combine_dram = nc.dram_tensor("combine_dram", [E, T], F32)
    b.out_fm = nc.dram_tensor("out_fm", [H, T], F32, kind="ExternalOutput").ap()
    b.rows_dram = nc.dram_tensor("rows_dram", [4, TH], F32).ap()
    b.dbc_dram = nc.dram_tensor("dbc_dram", [96, T], BF16)
    b.ag_in = nc.dram_tensor("ag_in", [128, 512], F32)
    b.ag_out = nc.dram_tensor("ag_out", [n_cores * 128, 512], F32,
                              addr_space="Shared" if n_cores > 1 else "Local")
    if DEBUG:
        b.xn_dbg = nc.dram_tensor("xn_dbg", [H, TH], BF16, kind="ExternalOutput").ap()
        b.z_dbg = nc.dram_tensor("z_dbg", [H, TH], BF16, kind="ExternalOutput").ap()
        b.xc_dbg = nc.dram_tensor("xc_dbg", [H, T], BF16, kind="ExternalOutput").ap()
        b.dbc_dbg = nc.dram_tensor("dbc_dbg", [96, T], BF16, kind="ExternalOutput").ap()
        b.st_dbg = nc.dram_tensor("st_dbg", [128, 1024], F32, kind="ExternalOutput").ap()

    from contextlib import ExitStack
    ctx = ExitStack()
    S = lambda name, shape, dt: ctx.enter_context(nc.sbuf_tensor(name, shape, dt))
    b.xblk = [S(f"xblk{i}", [128, TH], F32) for i in range(2)]
    b.xn = S("xn", [128, HB * TH], BF16)
    b.sq = S("sq", [128, TH], F32)
    b.ones = S("ones", [128, 4], F32)
    b.dbc = S("dbc", [128, T], BF16)
    b.wslab = [S(f"wslab{i}", [128, 4096], BF16) for i in range(2)]
    b.dtw_slab = S("dtw_slab", [64, H], BF16)
    b.A_sbuf = S("A_sbuf", [128, HB * NST], F32)
    b.nw_col = S("nw_col", [128, HB], F32)
    b.convb_col = S("convb_col", [128, HB], F32)
    b.dtb_col = S("dtb_col", [128, HB], F32)
    b.d_col = S("d_col", [128, HB], F32)
    b.m_col = S("m_col", [128, 8], F32)
    b.im_col = S("im_col", [128, 8], F32)

    b.state = b.xblk[0][:, 0:256]
    b.hin = b.xblk[0][:, 256:512]
    b.y_hh = b.xblk[0][:, 512:768]
    b.aprod = b.xblk[0][:, 768:1024]
    b.sumd = b.y_hh[:, 0:16]
    b.tsc1 = [b.xblk[1][:, 0:256], b.xblk[1][:, 512:768]]
    b.tsc2 = [b.xblk[1][:, 256:512], b.xblk[1][:, 768:1024]]
    b.t1p = [b.xblk[1][:, 0:512], b.xblk[1][:, 512:1024]]
    b.ones_row = b.sq[:, 896:1024].bitcast(BF16)[0:1, :]

    NA = 35907
    b.arena = S("arena", [128, NA], F32)
    A = b.arena
    b.rstd_b = A[:, 0:1027]
    b.mr_b = A[:, 1027:2054]
    b.row_mean = A[0:1, 2054:3081]
    b.row_var = A[0:1, 3081:4108]
    b.row_rstd = A[0:1, 4108:5135]
    b.row_mr = A[0:1, 5135:6162]
    b.xc_pre = A[:, 0:8224].bitcast(BF16)[:, 0:HB * TH]
    b.z = A[:, 8224:16448].bitcast(BF16)[:, 0:HB * TH]
    b.xc = A[:, 16448:24640].bitcast(BF16)
    b.r2_b = A[:, 24640:25667]
    b.d_sub = A[:, 0:2048].bitcast(BF16)
    b.ab = [A[:, 2048:4096].bitcast(BF16), A[:, 25667:27715].bitcast(BF16)]
    b.dxc = A[:, 4096:6144].bitcast(BF16)
    b.gath = A[:, 2048:6144]
    b.bb = A[:, 6144:8192].bitcast(BF16)
    b.utmp_all = A[:, 27715:29763].bitcast(BF16)
    b.dtb_row_s = A[:, 24640:25664].bitcast(BF16)[0:1, :]
    b.bm_rep = A[:, 29763:31811].bitcast(BF16)
    b.c_rep = A[:, 31811:33859].bitcast(BF16)
    b.rhs = A[:, 33859:35907].bitcast(BF16)
    b.acc = A[:, 0:16384]
    b.act = [A[:, 16384:22016].bitcast(BF16),
             A[:, 22016:27648].bitcast(BF16)]
    b.sg_s = A[:, 16384:27648].bitcast(BF16)[:, 0:22 * 1024]
    b.cb = [A[:, 27648:28160].bitcast(BF16),
            A[:, 28160:28672].bitcast(BF16)]
    b.gslab = A[:, 28672:28800].bitcast(BF16)
    b.ident_bf = A[:, 28800:28864].bitcast(BF16)
    b.ctile = A[:, 28864:28872].bitcast(BF16)
    b.cfm = A[0:16, 28872:29896]

    b.pb = [ctx.enter_context(nc.psum_tensor(f"pb{i}", [128, 512], F32))
            for i in range(8)]

    sem_names = ["xload", "xl0", "xl1", "wload0", "wload1", "mm", "vec", "scl",
                 "dmah", "dmas", "cc", "st0", "st1", "st2", "st3", "acc0", "acc1", "wdtw", "wgath", "wcb0", "wcb1", "chain_vector", "chain_scalar"]
    sems = {}
    for name in sem_names:
        sems[name] = ctx.enter_context(nc.semaphore(name))

    k = K(nc, b, sems)
    k.start_scan_pass()
    program(k, n_cores=n_cores)
    k.end_pass()
    k.plan()
    with ctx, nc.Block() as block:
        for eng_name in ENGS:
            def section(eng, eng_name=eng_name):
                k.start_pass(eng_name, eng)
                program(k, n_cores=n_cores)
                k.end_pass()
            getattr(block, eng_name)(section)
    return nc, k


def _colify(v):
    return np.ascontiguousarray(v.reshape(HB, 128).T)


def host_inputs(inputs):
    w = {}
    w["inw_t"] = np.ascontiguousarray(np.asarray(inputs["L0_attn_in_proj_w"]).T)
    w["convw"] = np.ascontiguousarray(np.asarray(inputs["L0_attn_conv_w"]))
    w["xpw_t"] = np.ascontiguousarray(np.asarray(inputs["L0_attn_x_proj_w"]).T)
    w["nw_in"] = _colify(np.asarray(inputs["L0_attn_norm_w"]))
    w["convb_in"] = _colify(np.asarray(inputs["L0_attn_conv_b"]))
    w["dtw_t"] = np.ascontiguousarray(np.asarray(inputs["L0_attn_dt_proj_w"]).T)
    w["dtb_row"] = np.ascontiguousarray(
        np.asarray(inputs["L0_attn_dt_proj_b"])[None, :])
    w["outw_t"] = np.ascontiguousarray(np.asarray(inputs["L0_attn_out_proj_w"]).T)
    w["dtb_in"] = _colify(np.asarray(inputs["L0_attn_dt_proj_b"]))
    w["d_in"] = _colify(np.asarray(inputs["L0_attn_D"]))
    al = np.asarray(inputs["L0_attn_A_log"])
    w["alog_in"] = np.ascontiguousarray(
        al.reshape(HB, 128, NST).transpose(1, 0, 2).reshape(128, HB * NST))
    w["gatew_t"] = np.ascontiguousarray(np.asarray(inputs["L0_ffn_gate_w"]).T)
    w["shw1_t"] = np.ascontiguousarray(np.asarray(inputs["L0_ffn_shared_w1"]).T)
    w["shw2_t"] = np.ascontiguousarray(np.asarray(inputs["L0_ffn_shared_w2"]).T)
    w["w1_t"] = np.ascontiguousarray(np.asarray(inputs["L0_ffn_w1"]).transpose(0, 2, 1))
    w["w2_t"] = np.ascontiguousarray(np.asarray(inputs["L0_ffn_w2"]).transpose(0, 2, 1))
    w["ident"] = np.eye(128, dtype=np.float32)
    return w


def kernel(**inputs):
    x = np.asarray(inputs["L0_attn_x"])
    nc, _k = build_nc()
    shared = host_inputs(inputs)
    in_maps = []
    for c in range(8):
        seq, pos = divmod(c, 4)
        t0 = pos * T
        xc = x[seq, t0:t0 + T, :]
        halo = x[seq, t0 - 3:t0, :] if pos > 0 else np.zeros((3, H), np.float32)
        xf = np.ascontiguousarray(np.concatenate([halo, xc], axis=0).T)
        m = np.zeros(8, np.float32)
        for j in range(8):
            if j // 4 == seq and j % 4 < pos:
                m[j] = 1.0
        mask = np.repeat(m[None, :], 128, axis=0).astype(np.float32)
        in_maps.append({"x_fm": xf, "mask_in": mask,
                        "imask_in": np.ascontiguousarray(1.0 - mask), **shared})
    res = run_bass_kernel_spmd(nc, in_maps, core_ids=list(range(8)))
    outs = [res.results[c]["out_fm"].T for c in range(8)]
    full = np.concatenate(outs, axis=0).reshape(2, 4096, H)
    if DEBUG:
        kernel._dbg = res.results
    return full.astype(np.float32)


if __name__ == "__main__":
    rng = np.random.default_rng(0)
    x = rng.standard_normal((2, 4096, H), dtype=np.float32)
    out = kernel(L0_attn_x=x)
    m = x.mean(-1, keepdims=True)
    v = x.var(-1, keepdims=True)
    xn = (x - m) / np.sqrt(v + EPS)
    ref = x + xn
    err = np.abs(out - ref).max()
    print("out err vs x+xn:", err)
    if DEBUG:
        xn_dbg = kernel._dbg[0]["xn_dbg"].T
        ref_xn = xn[0, :T]
        got = xn_dbg[3:, :]
        print("xn err:", np.abs(got - ref_xn).max())


# version 14 (final)
# speedup vs baseline: 1.0602x; 1.0062x over previous
import numpy as np
import concourse.bass as bass
import concourse.mybir as mybir
from concourse.bass_utils import run_bass_kernel_spmd

F32 = mybir.dt.float32
BF16 = mybir.dt.bfloat16
AF = mybir.ActivationFunctionType
OP = mybir.AluOpType
AX = mybir.AxisListType

H = 2048
HB = 16
T = 1024
TH = T + 3
SUB = 256
NSUB = T // SUB
NST = 16
DTR = 64
E = 16
I_FF = 1408
IS = 2816
EPS = 1e-5

RACE_SAFE = False
DEBUG = False
DEBUG_FRONT = False
ENGS = ["sync", "gpsimd", "tensor", "vector", "scalar"]


class Sem:
    __slots__ = ("h", "v", "name")

    def __init__(self, handle):
        self.h = handle
        self.v = 0

    def inc(self, n=1):
        self.v += n
        return self.v


class K:

    def __init__(self, nc, bufs, sem_handles):
        self.nc = nc
        self.b = bufs
        self._sh = sem_handles
        self.cur = None
        self.e = None
        self.nsteps_ref = None
        self.nsteps = 0
        self.mode = "emit"
        self.rec = None
        self.plan_chain = {}
        self.plan_wait = {}

    def start_scan_pass(self):
        self.mode = "scan"
        self.cur = "__scan__"
        self.e = None
        self.nsteps = 0
        self.eidx = {}
        self.rec = {e: [] for e in ENGS}
        for name, h in self._sh.items():
            sem = Sem(h)
            sem.name = name
            setattr(self, name, sem)
        self.chains = {e: Sem(self._sh["chain_" + e]) for e in ("vector", "scalar")}
        for e in ("vector", "scalar"):
            self.chains[e].name = "chain_" + e
            setattr(self, "chain_" + e, self.chains[e])

    def start_pass(self, eng_name, eng):
        self.mode = "emit"
        self.cur = eng_name
        self.e = eng
        self.nsteps = 0
        self.eidx = {}
        for name, h in self._sh.items():
            sem = Sem(h)
            sem.name = name
            setattr(self, name, sem)
        self.chains = {e: Sem(self._sh["chain_" + e]) for e in ("vector", "scalar")}
        for e in ("vector", "scalar"):
            self.chains[e].name = "chain_" + e
            setattr(self, "chain_" + e, self.chains[e])

    def end_pass(self):
        if self.nsteps_ref is None:
            self.nsteps_ref = self.nsteps
        else:
            assert self.nsteps == self.nsteps_ref, \
                f"emission diverged on pass {self.cur}: {self.nsteps} != {self.nsteps_ref}"

    def step(self, eng, emit, waits=(), inc=None, inc_n=1, drain=False):
        self.nsteps += 1
        i = self.eidx.get(eng, 0)
        self.eidx[eng] = i + 1
        if self.mode == "scan":
            self.rec[eng].append([bool(drain), inc.name if inc else None,
                                  inc.v + inc_n if inc else None, False])
            v = inc.inc(inc_n) if inc is not None else None
            return v
        chain = self.chains.get(eng)
        needs_chain = self.plan_chain.get(eng, {}).get(i, False)
        v = inc.inc(inc_n) if inc is not None else None
        if needs_chain:
            chain.inc(1)
        if self.cur == eng:
            for s, val in waits:
                if val > 0:
                    self.e.wait_ge(s.h, val)
            if drain:
                dw = self.plan_wait.get(eng, {}).get(i)
                if dw is None:
                    self.e.drain()
                else:
                    sname, sval = dw
                    if sval > 0:
                        self.e.wait_ge(self.sem_by_name(sname).h, sval)
            ins = emit(self.e)
            if inc is not None:
                assert ins is not None, "step with inc= must emit an instruction"
                ins.then_inc(inc.h, inc_n)
            if needs_chain:
                assert ins is not None and inc is None
                ins.then_inc(chain.h, 1)
        return v

    def sem_by_name(self, name):
        return getattr(self, name)

    def plan(self):
        self.plan_chain = {}
        self.plan_wait = {}
        ENGINE_SEMS = {"vec", "scl", "mm"}
        for eng, recs in self.rec.items():
            chain_marked = {}
            waits = {}
            for i, r in enumerate(recs):
                if r[0] and i > 0:
                    p = i - 1
                    while p >= 0 and recs[p][1] is not None \
                            and recs[p][1] not in ENGINE_SEMS:
                        p -= 1
                    if p < 0:
                        continue
                    if recs[p][1] in ENGINE_SEMS:
                        waits[i] = (recs[p][1], recs[p][2])
                    else:
                        chain_marked[p] = True
            cval = 0
            cvals = {}
            for p in sorted(chain_marked):
                cval += 1
                cvals[p] = cval
            for i, r in enumerate(recs):
                if r[0] and i > 0 and i not in waits:
                    p = i - 1
                    if p in cvals:
                        waits[i] = ("chain_" + eng, cvals[p])
            self.plan_chain[eng] = chain_marked
            self.plan_wait[eng] = waits


def ts(i, n=128):
    return slice(i * n, (i + 1) * n)



def emit_consts(k):
    b = k.b
    k.step("vector", lambda e: e.memset(b.ones[:], 1.0), inc=k.vec)
    b.v_consts_vec = k.vec.v
    k.step("sync", lambda e: e.dma_start(out=b.nw_col[:], in_=b.nw_in),
           inc=k.xload, inc_n=16)
    k.step("sync", lambda e: e.dma_start(out=b.convb_col[:], in_=b.convb_in),
           inc=k.xload, inc_n=16)
    k.step("sync", lambda e: e.dma_start(out=b.dtb_col[:], in_=b.dtb_in),
           inc=k.xload, inc_n=16)
    k.step("sync", lambda e: e.dma_start(out=b.d_col[:], in_=b.d_in),
           inc=k.xload, inc_n=16)
    k.step("sync", lambda e: e.dma_start(out=b.m_col[:], in_=b.mask_in),
           inc=k.xload, inc_n=16)
    k.step("sync", lambda e: e.dma_start(out=b.im_col[:], in_=b.imask_in),
           inc=k.xload, inc_n=16)
    k.step("sync", lambda e: e.dma_start(out=b.A_sbuf[:], in_=b.alog_in),
           inc=k.xload, inc_n=16)
    b.v_consts = k.xload.v
    b.v_dtw = k.step("gpsimd",
                     lambda e: e.dma_start(out=b.dtw_slab[:], in_=b.dtw_t),
                     inc=k.wdtw, inc_n=16)
    v = k.step("scalar",
               lambda e: e.activation(out=b.A_sbuf[:], in_=b.A_sbuf[:], func=AF.Exp),
               waits=[(k.xload, b.v_consts)], inc=k.scl)
    b.v_A = k.step("vector",
                   lambda e: e.tensor_scalar_mul(b.A_sbuf[:], b.A_sbuf[:], -1.0),
                   waits=[(k.scl, v)], inc=k.vec)


def emit_ln(k):
    b = k.b
    tslices = [(0, 512), (512, 512), (1024, 3)]

    v_mmblk = {}
    for f in range(HB):
        par = f % 2
        xl = k.xl0 if par == 0 else k.xl1
        vx = k.step(
            "sync",
            lambda e, f=f, par=par: e.dma_start(out=b.xblk[par][:],
                                                in_=b.x_fm[ts(f), :]),
            waits=[(k.mm, v_mmblk.get(f - 2, 0))], inc=xl, inc_n=16)
        sqb = b.sq if par == 0 else b.sq2
        vsq = k.step(
            "scalar",
            lambda e, f=f, par=par, sqb=sqb: e.activation(
                out=sqb[:], in_=b.xblk[par][:], func=AF.Square),
            waits=[(xl, vx), (k.mm, v_mmblk.get(f - 2, 0))], inc=k.scl)
        for tsi, (t0, tw) in enumerate(tslices):
            k.step(
                "tensor",
                lambda e, f=f, par=par, tsi=tsi, t0=t0, tw=tw: e.matmul(
                    b.pb[2 * tsi][0:1, 0:tw], b.ones[:, 0:1],
                    b.xblk[par][:, t0:t0 + tw],
                    start=(f == 0), stop=(f == HB - 1)),
                waits=[(xl, vx), (k.vec, b.v_consts_vec)] if tsi == 0 else ())
        for tsi, (t0, tw) in enumerate(tslices):
            vmb = k.step(
                "tensor",
                lambda e, f=f, tsi=tsi, t0=t0, tw=tw, sqb=sqb: e.matmul(
                    b.pb[2 * tsi + 1][0:1, 0:tw], b.ones[:, 0:1],
                    sqb[:, t0:t0 + tw],
                    start=(f == 0), stop=(f == HB - 1)),
                waits=[(k.scl, vsq)] if tsi == 0 else (),
                inc=(k.mm if tsi == 2 else None))
        v_mmblk[f] = vmb
    v_sums = k.mm.v

    for tsi, (t0, tw) in enumerate(tslices):
        k.step("vector",
               lambda e, tsi=tsi, t0=t0, tw=tw: e.tensor_scalar_mul(
                   b.row_mean[0:1, t0:t0 + tw], b.pb[2 * tsi][0:1, 0:tw], 1.0 / H),
               waits=[(k.mm, v_sums)])
    k.step("vector",
           lambda e: e.scalar_tensor_tensor(
               out=b.row_var[0:1, :], in0=b.row_mean[0:1, :], scalar=-1.0,
               in1=b.row_mean[0:1, :], op0=OP.mult, op1=OP.mult),
           drain=True)
    for tsi, (t0, tw) in enumerate(tslices):
        k.step("vector",
               lambda e, tsi=tsi, t0=t0, tw=tw: e.scalar_tensor_tensor(
                   out=b.row_var[0:1, t0:t0 + tw], in0=b.pb[2 * tsi + 1][0:1, 0:tw],
                   scalar=1.0 / H, in1=b.row_var[0:1, t0:t0 + tw],
                   op0=OP.mult, op1=OP.add),
               drain=(tsi == 0))
    k.step("vector",
           lambda e: e.tensor_scalar_add(b.row_var[0:1, :], b.row_var[0:1, :], EPS),
           drain=True)
    v = k.step("vector",
               lambda e: e.reciprocal(b.row_rstd[0:1, :], b.row_var[0:1, :]),
               drain=True, inc=k.vec)
    b.v_ln_psum_done = v
    v = k.step("scalar",
               lambda e: e.activation(out=b.row_rstd[0:1, :], in_=b.row_rstd[0:1, :],
                                      func=AF.Sqrt),
               waits=[(k.vec, v)], inc=k.scl)
    v_mr = k.step("vector",
                  lambda e: e.scalar_tensor_tensor(
                      out=b.row_mr[0:1, :], in0=b.row_mean[0:1, :], scalar=-1.0,
                      in1=b.row_rstd[0:1, :], op0=OP.mult, op1=OP.mult),
                  waits=[(k.scl, v)], inc=k.vec)
    k.step("scalar",
           lambda e: e.dma_start(out=b.rows_dram[0:1, :], in_=b.row_rstd[0:1, :]),
           waits=[(k.vec, v_mr)], drain=True, inc=k.dmah, inc_n=16)
    v_r2 = k.step("scalar",
                  lambda e: e.dma_start(out=b.rows_dram[1:2, :], in_=b.row_mr[0:1, :]),
                  inc=k.dmah, inc_n=16)
    k.step("scalar",
           lambda e: e.dma_start(out=b.rstd_b[:],
                                 in_=b.rows_dram[0:1, :].partition_broadcast(128)),
           waits=[(k.dmah, v_r2)], inc=k.dmah, inc_n=16)
    v_bc = k.step("scalar",
                  lambda e: e.dma_start(out=b.mr_b[:],
                                        in_=b.rows_dram[1:2, :].partition_broadcast(128)),
                  inc=k.dmah, inc_n=16)

    v_vblk = {}
    for f in range(HB):
        par = f % 2
        xl = k.xl0 if par == 0 else k.xl1
        vx = k.step(
            "sync",
            lambda e, f=f, par=par: e.dma_start(out=b.xblk[par][:],
                                                in_=b.x_fm[ts(f), :]),
            waits=[(k.vec, v_vblk.get(f - 2, 0)), (k.mm, v_sums)],
            inc=xl, inc_n=16)
        k.step("vector",
               lambda e, f=f, par=par: e.tensor_tensor(
                   out=b.xn[:, f * TH:(f + 1) * TH], in0=b.xblk[par][:],
                   in1=b.rstd_b[:, 0:TH], op=OP.mult),
               waits=[(xl, vx), (k.dmah, v_bc)], drain=True)
        k.step("vector",
               lambda e, f=f: e.tensor_tensor(
                   out=b.xn[:, f * TH:(f + 1) * TH],
                   in0=b.xn[:, f * TH:(f + 1) * TH],
                   in1=b.mr_b[:, 0:TH], op=OP.add),
               drain=True)
        ssem = k.st0
        vadd = k.step("vector",
                      lambda e, f=f, par=par: e.tensor_tensor(
                          out=b.sq[:, 0:T], in0=b.xblk[par][:, 3:TH],
                          in1=b.xn[:, f * TH + 3:(f + 1) * TH], op=OP.add),
                      waits=[(ssem, ssem.v)], drain=True, inc=k.vec)
        v_vblk[f] = vadd
        k.step(
            "sync",
            lambda e, f=f: e.dma_start(out=b.out_fm[ts(f), :], in_=b.sq[:, 0:T]),
            waits=[(k.vec, vadd)], inc=ssem, inc_n=16)
    b.v_xn = k.vec.v
    b.v_out_init_st0 = k.st0.v


def emit_r2(k):
    b = k.b
    tslices = [(0, 512), (512, 512), (1024, 3)]
    prev_mm = 0
    for tsi, (t0, tw) in enumerate(tslices):
        bank = 6 if tsi != 1 else 7
        coff = 0 if tsi != 2 else 256
        for f in range(HB):
            vsq = k.step(
                "scalar",
                lambda e, f=f, t0=t0, tw=tw: e.activation(
                    out=b.sq[:, t0:t0 + tw],
                    in_=b.xn[:, f * TH + t0: f * TH + t0 + tw], func=AF.Square),
                waits=[(k.vec, b.v_xn), (k.mm, prev_mm), (k.st0, k.st0.v)],
                inc=k.scl)
            prev_mm = k.step(
                "tensor",
                lambda e, f=f, t0=t0, tw=tw, bank=bank, coff=coff: e.matmul(
                    b.pb[bank][0:1, coff:coff + tw], b.ones[:, 0:1],
                    b.sq[:, t0:t0 + tw],
                    start=(f == 0), stop=(f == HB - 1)),
                waits=[(k.scl, vsq)], inc=k.mm)
    v_sums = prev_mm
    for tsi, (t0, tw) in enumerate(tslices):
        bank = 6 if tsi != 1 else 7
        coff = 0 if tsi != 2 else 256
        k.step("vector",
               lambda e, tsi=tsi, t0=t0, tw=tw, bank=bank, coff=coff:
               e.tensor_scalar(out=b.row_var[0:1, t0:t0 + tw],
                               in0=b.pb[bank][0:1, coff:coff + tw],
                               scalar1=1.0 / H, scalar2=EPS,
                               op0=OP.mult, op1=OP.add),
               waits=[(k.mm, v_sums)])
    v = k.step("vector",
               lambda e: e.reciprocal(b.row_rstd[0:1, :], b.row_var[0:1, :]),
               drain=True, inc=k.vec)
    b.v_r2_psum_done = v
    v = k.step("scalar",
               lambda e: e.activation(out=b.row_rstd[0:1, :], in_=b.row_rstd[0:1, :],
                                      func=AF.Sqrt),
               waits=[(k.vec, v)], inc=k.scl)
    v_rd = k.step("scalar",
                  lambda e: e.dma_start(out=b.rows_dram[2:3, :], in_=b.row_rstd[0:1, :]),
                  drain=True, inc=k.dmah, inc_n=16)
    b.v_r2b = k.step("scalar",
                     lambda e: e.dma_start(
                         out=b.r2_b[:],
                         in_=b.rows_dram[2:3, :].partition_broadcast(128)),
                     waits=[(k.dmah, v_rd)], inc=k.dmah, inc_n=16)


IP_PHASES = [(i * 128, 128) for i in range(32)]


def emit_inproj(k):
    b = k.b
    tslices = [(0, 343), (343, 342), (685, 342)]
    v_mm_phase = {}
    v_scale_phase = {}
    for ph, (o0, W) in enumerate(IP_PHASES):
        par = ph % 2
        wl = k.wload0 if par == 0 else k.wload1
        grp = b.frq % 2
        b.frq += 1
        base = 4 * grp
        vw = k.step(
            "gpsimd",
            lambda e, o0=o0, W=W, par=par: e.dma_start(
                out=b.wslab[par][:, 0:16 * W].rearrange("p (kb w) -> p kb w", kb=16),
                in_=b.inw_t.rearrange("(kb p) o -> p kb o", p=128)[:, :, o0:o0 + W]),
            waits=[(k.mm, v_mm_phase.get(ph - 2, 0))], inc=wl, inc_n=16)
        for kb in range(16):
            k.step(
                "vector",
                lambda e, kb=kb, W=W, par=par: e.tensor_scalar_mul(
                    b.wslab[par][:, kb * W:(kb + 1) * W],
                    b.wslab[par][:, kb * W:(kb + 1) * W],
                    b.nw_col[:, kb:kb + 1]),
                waits=([(wl, vw), (k.mm, v_mm_phase.get(ph - 2, 0)),
                        (k.xload, b.v_consts)] if kb == 0 else ()),
                drain=(kb == 0), inc=(k.vec if kb == 15 else None))
        v_scale_phase[ph] = k.vec.v
        for tsi, (t0, tw) in enumerate(tslices):
            bank = base + tsi
            for kb in range(16):
                wts = []
                if tsi == 0 and kb == 0:
                    wts = [(k.vec, v_scale_phase[ph]),
                           (k.vec, b.fr_prev_vec[grp]),
                           (k.scl, b.fr_prev_scl[grp])]
                    if ph <= 1:
                        wts.append((k.vec, b.v_r2_psum_done))
                is_last = (tsi == 2 and kb == 15)
                k.step(
                    "tensor",
                    lambda e, tsi=tsi, kb=kb, t0=t0, tw=tw, W=W,
                           par=par, bank=bank: e.matmul(
                        b.pb[bank][0:128, 0:tw],
                        b.wslab[par][:, kb * W: kb * W + 128],
                        b.xn[:, kb * TH + t0: kb * TH + t0 + tw],
                        start=(kb == 0), stop=(kb == 15)),
                    waits=wts, inc=(k.mm if is_last else None))
        v_mm_phase[ph] = k.mm.v
        dest = b.xc_pre if o0 < H else b.z
        f = (o0 % H) // 128
        for tsi, (t0, tw) in enumerate(tslices):
            bank = base + tsi
            wts = [(k.mm, v_mm_phase[ph])]
            if tsi == 0:
                wts.append((k.dmah, b.v_r2b))
            k.step(
                "vector",
                lambda e, dest=dest, f=f, t0=t0, tw=tw, bank=bank:
                e.tensor_tensor(
                    out=dest[:, f * TH + t0: f * TH + t0 + tw],
                    in0=b.pb[bank][0:128, 0:tw],
                    in1=b.r2_b[:, t0:t0 + tw], op=OP.mult),
                waits=wts, drain=(tsi == 0),
                inc=(k.vec if tsi == 2 else None))
        b.fr_prev_vec[grp] = k.vec.v
    b.v_inproj_done = k.vec.v
    b.v_inproj_mm = k.mm.v
    for f in range(HB):
        k.step("scalar",
               lambda e, f=f: e.activation(
                   out=b.z[:, f * TH:(f + 1) * TH],
                   in_=b.z[:, f * TH:(f + 1) * TH], func=AF.Silu),
               waits=[(k.vec, b.v_inproj_done)] if f == 0 else (),
               inc=(k.scl if f == HB - 1 else None))
    b.v_sz = k.scl.v


def emit_conv(k):
    b = k.b
    tslices = [(0, 512), (512, 512)]
    v_mm_half = {}
    for ph in range(HB):
        b.conv_grp = b.frq % 2
        b.frq += 1
        b.conv_base = 4 * b.conv_grp
        for half in range(2):
            q = 2 * ph + half
            par = q % 2
            wl = k.wload0 if par == 0 else k.wload1
            vw = k.step(
                "gpsimd",
                lambda e, ph=ph, half=half, par=par: e.dma_start(
                    out=b.wslab[par][:, 0:4096].rearrange("p (c kb w) -> p c kb w", c=2, kb=16),
                    in_=b.convw.rearrange("c (kb p) o -> p c kb o", p=128)[
                        :, 2 * half:2 * half + 2, :, ph * 128:(ph + 1) * 128]),
                waits=[(k.mm, v_mm_half.get(q - 2, b.v_inproj_mm if q < 2 else 0))],
                inc=wl, inc_n=16)
            first = True
            for tsi, (t0, tw) in enumerate(tslices):
                bank = b.conv_base + tsi
                for cl in range(2):
                    c = 2 * half + cl
                    for kb in range(16):
                        wts = []
                        if first:
                            wts = [(wl, vw)]
                            if half == 0:
                                wts += [(k.scl, b.fr_prev_scl[b.conv_grp]),
                                        (k.vec, b.fr_prev_vec[b.conv_grp])]
                            if q == 0:
                                wts.append((k.vec, b.v_inproj_done))
                            first = False
                        is_last = (tsi == 1 and cl == 1 and kb == 15)
                        k.step(
                            "tensor",
                            lambda e, c=c, cl=cl, kb=kb, t0=t0, tw=tw, par=par,
                                   bank=bank: e.matmul(
                                b.pb[bank][:, 0:tw],
                                b.wslab[par][:, (cl * 16 + kb) * 128:(cl * 16 + kb + 1) * 128],
                                b.xc_pre[:, kb * TH + t0 + c: kb * TH + t0 + c + tw],
                                start=(c == 0 and kb == 0), stop=(c == 3 and kb == 15)),
                            waits=wts, inc=(k.mm if is_last else None))
            v_mm_half[q] = k.mm.v
        for tsi, (t0, tw) in enumerate(tslices):
            bank = b.conv_base + tsi
            k.step(
                "scalar",
                lambda e, ph=ph, t0=t0, tw=tw, bank=bank: e.activation(
                    out=b.xc[:, ph * T + t0: ph * T + t0 + tw],
                    in_=b.pb[bank][:, 0:tw], func=AF.Silu,
                    bias=b.convb_col[:, ph:ph + 1], scale=1.0),
                waits=[(k.mm, v_mm_half[2 * ph + 1])],
                inc=(k.scl if tsi == 1 else None))
        b.fr_prev_scl[b.conv_grp] = k.scl.v
    b.v_conv_mm = k.mm.v
    b.v_conv_done = k.scl.v


def emit_xproj(k):
    b = k.b
    vw = k.step(
        "gpsimd",
        lambda e: e.dma_start(
            out=b.wslab[0][:, 0:16 * 96].rearrange("p (kb j) -> p kb j", kb=16),
            in_=b.xpw_t.rearrange("(kb p) j -> p kb j", p=128)),
        waits=[(k.mm, b.v_conv_mm)], inc=k.wload0, inc_n=16)
    tslices = [(0, 512), (512, 512)]
    for tsi, (t0, tw) in enumerate(tslices):
        for kb in range(16):
            wts = []
            if tsi == 0 and kb == 0:
                wts = [(k.wload0, vw), (k.scl, b.v_conv_done),
                       (k.vec, max(b.fr_prev_vec[0], b.fr_prev_vec[1]))]
            k.step(
                "tensor",
                lambda e, kb=kb, t0=t0, tw=tw, tsi=tsi: e.matmul(
                    b.pb[2 + tsi][0:96, 0:tw],
                    b.wslab[0][:, kb * 96:(kb + 1) * 96],
                    b.xc[:, kb * T + t0: kb * T + t0 + tw],
                    start=(kb == 0), stop=(kb == 15)),
                waits=wts, inc=(k.mm if (tsi == 1 and kb == 15) else None))
    v_mm = k.mm.v
    b.v_xproj_mm = v_mm
    for tsi, (t0, tw) in enumerate(tslices):
        k.step("vector",
               lambda e, t0=t0, tw=tw, tsi=tsi: e.tensor_copy(
                   out=b.dbc[0:96, t0:t0 + tw], in_=b.pb[2 + tsi][0:96, 0:tw]),
               waits=[(k.mm, v_mm)], drain=(tsi == 0),
               inc=(k.vec if tsi == 1 else None))
    b.v_dbc = k.vec.v
    b.v_dbc_dram = k.step(
        "sync",
        lambda e: e.dma_start(out=b.dbc_dram.ap(), in_=b.dbc[0:96, :]),
        waits=[(k.vec, b.v_dbc)], inc=k.dmas, inc_n=16)
    k.step("gpsimd",
           lambda e: e.dma_start(out=b.dtb_row_s, in_=b.dtb_row),
           waits=[(k.vec, k.vec.v), (k.scl, k.scl.v)], inc=k.wdtw, inc_n=16)
    b.v_ones_row = k.step("vector",
                          lambda e: e.memset(b.ones_row, 1.0),
                          waits=[(k.scl, k.scl.v), (k.mm, k.mm.v),
                                 (k.st0, k.st0.v)],
                          drain=True, inc=k.vec)



def emit_scan_pass(k, is_c):
    b = k.b
    for sub in range(NSUB):
        s0 = sub * SUB
        v_bm = k.step(
            "scalar",
            lambda e, s0=s0: e.dma_start(
                out=b.bm_rep[:].rearrange("p (n t) -> p n t", n=NST),
                in_=b.dbc_dram.ap()[64:80, s0:s0 + SUB].partition_broadcast(128)),
            waits=[(k.dmas, k.dmas.v), (k.vec, b.vv_bmfree)],
            inc=k.dmah, inc_n=16)
        if is_c:
            v_bm = k.step(
                "scalar",
                lambda e, s0=s0: e.dma_start(
                    out=b.c_rep[:].rearrange("p (n t) -> p n t", n=NST),
                    in_=b.dbc_dram.ap()[80:96, s0:s0 + SUB].partition_broadcast(128)),
                inc=k.dmah, inc_n=16)
        v_bmc = v_bm

        vdt = {}
        vsp = {}
        for hp in range(HB // 2):
            bank = 6 + hp % 2
            wts = [(k.scl, vsp.get(hp - 2, b.v_dtpb_free)), (k.vec, b.v_dbc)]
            if hp < 2:
                wts.append((k.vec, b.v_oppb_free))
            if sub == 0 and hp == 0:
                wts.append((k.wdtw, k.wdtw.v))
                wts.append((k.vec, b.v_ones_row))
            for j in range(2):
                hh = 2 * hp + j
                k.step(
                    "tensor",
                    lambda e, hh=hh, j=j, s0=s0, bank=bank: e.matmul(
                        b.pb[bank][:, j * SUB:(j + 1) * SUB],
                        b.dtw_slab[0:64, hh * 128:(hh + 1) * 128],
                        b.dbc[0:64, s0:s0 + SUB],
                        start=True, stop=False),
                    waits=(wts if j == 0 else ()))
                vdt[hp] = k.step(
                    "tensor",
                    lambda e, hh=hh, j=j, bank=bank: e.matmul(
                        b.pb[bank][:, j * SUB:(j + 1) * SUB],
                        b.dtb_row_s[0:1, hh * 128:(hh + 1) * 128],
                        b.ones_row[0:1, :],
                        start=False, stop=True),
                    inc=(k.mm if j == 1 else None))
            t1 = b.t1p[hp % 2]
            k.step("scalar",
                   lambda e, bank=bank, t1=t1: e.activation(
                       out=t1, in_=b.pb[bank][:, 0:2 * SUB], func=AF.Abs),
                   waits=[(k.mm, vdt[hp]), (k.vec, b.v_tscfree.get(hp % 2, 0))],
                   drain=True)
            k.step("scalar",
                   lambda e, t1=t1: e.activation(out=t1, in_=t1,
                                                 func=AF.Exp, scale=-1.0),
                   drain=True)
            k.step("scalar",
                   lambda e, t1=t1: e.activation(out=t1, in_=t1,
                                                 func=AF.Ln, bias=1.0),
                   drain=True)
            vsp[hp] = k.step("scalar",
                             lambda e, hp=hp, bank=bank: e.activation(
                                 out=b.d_sub[:, 2 * hp * SUB:2 * (hp + 1) * SUB],
                                 in_=b.pb[bank][:, 0:2 * SUB], func=AF.Relu),
                             waits=[(k.vec, b.v_dsub_free_vec)],
                             drain=True, inc=k.scl)
            vd = k.step("vector",
                        lambda e, hp=hp, t1=t1: e.tensor_tensor(
                            out=b.d_sub[:, 2 * hp * SUB:2 * (hp + 1) * SUB],
                            in0=b.d_sub[:, 2 * hp * SUB:2 * (hp + 1) * SUB],
                            in1=t1, op=OP.add),
                        waits=[(k.scl, vsp[hp])], drain=True, inc=k.vec)
            b.v_tscfree[hp % 2] = vd
        b.v_dtpb_free = k.scl.v
        v_delta = k.vec.v

        k.step("vector",
               lambda e, s0=s0: e.tensor_tensor(
                   out=b.utmp_all[:].rearrange("p (h t) -> p h t", h=HB),
                   in0=b.d_sub[:].rearrange("p (h t) -> p h t", h=HB),
                   in1=b.xc[:].rearrange("p (h t) -> p h t", h=HB)[:, :, s0:s0 + SUB],
                   op=OP.mult),
               drain=True, inc=k.vec)
        v_delta = k.vec.v

        if not is_c:
            k.step("vector",
                   lambda e: e.tensor_reduce(
                       out=b.aprod[:, 0:16],
                       in_=b.d_sub[:].rearrange("p (h t) -> p h t", h=HB),
                       axis=AX.X, op=OP.add),
                   waits=[(k.vec, v_delta)], drain=True)
            if sub == 0:
                k.step("vector",
                       lambda e: e.tensor_copy(out=b.sumd[:], in_=b.aprod[:, 0:16]),
                       drain=True, inc=k.vec)
            else:
                k.step("vector",
                       lambda e: e.tensor_tensor(out=b.sumd[:], in0=b.sumd[:],
                                                 in1=b.aprod[:, 0:16], op=OP.add),
                       drain=True, inc=k.vec)

        b.v_dsub_free_vec = k.vec.v
        for hh in range(HB):
            par = hh % 2
            ab = b.ab[par]
            for n in range(NST):
                b.v_abprod[par] = k.step(
                    "scalar",
                    lambda e, hh=hh, n=n, ab=ab: e.activation(
                        out=ab[:, n * SUB:(n + 1) * SUB],
                        in_=b.d_sub[:, hh * SUB:(hh + 1) * SUB], func=AF.Exp,
                        scale=b.A_sbuf[:, hh * NST + n: hh * NST + n + 1]),
                    waits=([(k.vec, v_delta), (k.vec, b.v_abfree.get(par, 0)),
                            (k.vec, b.v_A)] if n == 0 else ()),
                    inc=(k.scl if n == NST - 1 else None))
            k.step(
                "vector",
                lambda e, hh=hh: e.tensor_tensor(
                    out=b.bb[:].rearrange("p (n t) -> p n t", n=NST),
                    in0=b.utmp_all[:, hh * SUB:(hh + 1) * SUB]
                        .unsqueeze(1).broadcast_to([128, NST, SUB]),
                    in1=b.bm_rep[:].rearrange("p (n t) -> p n t", n=NST),
                    op=OP.mult),
                waits=[(k.dmah, v_bmc), (k.vec, b.v_bbfree)], drain=True)
            for n in range(NST):
                k.step(
                    "vector",
                    lambda e, hh=hh, n=n, ab=ab: e.tensor_tensor_scan(
                        out=b.bb[:, n * SUB:(n + 1) * SUB],
                        data0=ab[:, n * SUB:(n + 1) * SUB],
                        data1=b.bb[:, n * SUB:(n + 1) * SUB],
                        initial=b.state[:, hh * NST + n: hh * NST + n + 1],
                        op0=OP.mult, op1=OP.add),
                    waits=[(k.scl, b.v_abprod[par])] if n == 0 else (),
                    drain=(n == 0))
            vst = k.step("vector",
                         lambda e, hh=hh: e.tensor_copy(
                             out=b.state[:, hh * NST:(hh + 1) * NST],
                             in_=b.bb[:].rearrange("p (n t) -> p n t", n=NST)[:, :, SUB - 1]),
                         drain=True, inc=k.vec)
            b.v_abfree[par] = vst
            b.vv_bmfree = vst
            b.v_bbfree = vst
            if is_c:
                k.step("vector",
                       lambda e: e.tensor_tensor(
                           out=b.bb[:], in0=b.bb[:], in1=b.c_rep[:], op=OP.mult),
                       drain=True)
                k.step("vector",
                       lambda e: e.tensor_tensor(
                           out=b.bb[:, 0:8 * SUB], in0=b.bb[:, 0:8 * SUB],
                           in1=b.bb[:, 8 * SUB:16 * SUB], op=OP.add),
                       drain=True)
                k.step("vector",
                       lambda e: e.tensor_tensor(
                           out=b.bb[:, 0:4 * SUB], in0=b.bb[:, 0:4 * SUB],
                           in1=b.bb[:, 4 * SUB:8 * SUB], op=OP.add),
                       drain=True)
                k.step("vector",
                       lambda e: e.tensor_tensor(
                           out=b.bb[:, 0:2 * SUB], in0=b.bb[:, 0:2 * SUB],
                           in1=b.bb[:, 2 * SUB:4 * SUB], op=OP.add),
                       drain=True)
                vrhs = k.step("vector",
                              lambda e, hh=hh: e.tensor_tensor(
                                  out=b.rhs[:, hh * SUB:(hh + 1) * SUB],
                                  in0=b.bb[:, 0:SUB], in1=b.bb[:, SUB:2 * SUB],
                                  op=OP.add),
                              drain=True, inc=k.vec)
                b.v_bbfree = vrhs
                b.vv_bmfree = vrhs
        if is_c:
            k.step("vector",
                   lambda e, s0=s0: e.tensor_tensor(
                       out=b.dxc[:].rearrange("p (h t) -> p h t", h=HB),
                       in0=b.xc[:].rearrange("p (h t) -> p h t", h=HB)[:, :, s0:s0 + SUB],
                       in1=b.d_col[:].unsqueeze(2).broadcast_to([128, HB, SUB]),
                       op=OP.mult),
                   drain=True)
            k.step("vector",
                   lambda e: e.tensor_tensor(out=b.rhs[:], in0=b.rhs[:],
                                             in1=b.dxc[:], op=OP.add),
                   drain=True)
            vrhs = k.step("vector",
                          lambda e, s0=s0: e.tensor_tensor(
                              out=b.rhs[:].rearrange("p (h t) -> p h t", h=HB),
                              in0=b.rhs[:].rearrange("p (h t) -> p h t", h=HB),
                              in1=b.z[:].rearrange("p (h t) -> p h t", h=HB)[
                                  :, :, 3 + s0:3 + s0 + SUB],
                              op=OP.mult),
                          drain=True, inc=k.vec)
            emit_outproj_sub(k, sub)


def emit_outproj_sub(k, sub):
    b = k.b
    v_rhs = k.vec.v
    for ph in range(8):
        o0 = ph * 256
        q, par, wl, vw = _slab(
            k, b.outw_t.rearrange("(kb p) o -> p kb o", p=128)[:, :, o0:o0 + 256]
                .rearrange("p kb o -> p (kb o)") if False else
            b.outw_t.rearrange("(kb p) o -> p kb o", p=128)[:, :, o0:o0 + 256],
            4096)
        for ob in range(2):
            bank = 6 + ob
            for kb in range(16):
                wts = []
                if ob == 0 and kb == 0:
                    wts = [(wl, vw), (k.vec, v_rhs),
                           (k.vec, b.op_evac.get(q - 1, 0)),
                           (k.scl, b.v_dtpb_free)]
                k.step(
                    "tensor",
                    lambda e, ob=ob, kb=kb, par=par, bank=bank: e.matmul(
                        b.pb[bank][:, 0:SUB],
                        b.wslab[par][:, kb * 256 + ob * 128: kb * 256 + ob * 128 + 128],
                        b.rhs[:, kb * SUB:(kb + 1) * SUB],
                        start=(kb == 0), stop=(kb == 15)),
                    waits=wts,
                    inc=(k.mm if (ob == 1 and kb == 15) else None))
        b.op_mm[q] = k.mm.v
        b.slab_mm[q] = k.mm.v
        for ob in range(2):
            bank = 6 + ob
            o = o0 + ob * 128
            f = o // 128
            ssem = k.acc0 if ob == 0 else k.acc1
            stg = b.sq[:, ob * SUB:(ob + 1) * SUB]
            vev = k.step(
                "vector",
                lambda e, bank=bank, stg=stg: e.tensor_copy(out=stg,
                                                            in_=b.pb[bank][:, 0:SUB]),
                waits=[(k.mm, b.op_mm[q]), (ssem, ssem.v)], drain=True,
                inc=k.vec)
            k.step(
                "gpsimd",
                lambda e, f=f, sub=sub, stg=stg: e.dma_start(
                    out=b.out_fm[ts(f), sub * SUB:(sub + 1) * SUB], in_=stg,
                    accum_op=OP.add),
                waits=[(k.vec, vev), (k.st0, b.v_out_init_st0)],
                inc=ssem, inc_n=16)
        b.op_evac[q] = k.vec.v
    b.v_oppb_free = k.vec.v



def emit_gather(k, n_cores):
    b = k.b
    v = k.step("vector",
               lambda e: e.tensor_tensor(
                   out=b.aprod[:].rearrange("p (h n) -> p h n", n=NST),
                   in0=b.sumd[:].unsqueeze(2).broadcast_to([128, HB, NST]),
                   in1=b.A_sbuf[:].rearrange("p (h n) -> p h n", n=NST),
                   op=OP.mult),
               drain=True, inc=k.vec)
    v = k.step("scalar",
               lambda e: e.activation(out=b.aprod[:], in_=b.aprod[:], func=AF.Exp),
               waits=[(k.vec, v)], drain=True, inc=k.scl)
    k.step("sync",
           lambda e: e.dma_start(out=b.ag_in.ap()[:, 0:256], in_=b.aprod[:]),
           waits=[(k.scl, v)], inc=k.dmas, inc_n=16)
    v_ag = k.step("sync",
                  lambda e: e.dma_start(out=b.ag_in.ap()[:, 256:512], in_=b.state[:]),
                  waits=[(k.vec, k.vec.v)], inc=k.dmas, inc_n=16)
    v_cc = k.step("gpsimd",
                  lambda e, n_cores=n_cores: e.collective_compute(
                      "AllGather", OP.bypass,
                      replica_groups=[list(range(n_cores))],
                      ins=[b.ag_in.ap().opt()],
                      outs=[b.ag_out.ap().opt()]),
                  waits=[(k.dmas, v_ag)], inc=k.cc)
    v_g = k.step("gpsimd",
                 lambda e, n_cores=n_cores: e.dma_start(
                     out=b.gath[:, 0:n_cores * 512].rearrange("p (r c) -> p r c", r=n_cores),
                     in_=b.ag_out.ap().rearrange("(r p) c -> p r c", p=128)),
                 waits=[(k.cc, v_cc)], inc=k.wgath, inc_n=16)
    v = k.step("vector", lambda e: e.memset(b.hin[:], 0.0),
               waits=[(k.wgath, v_g)], drain=True, inc=k.vec)
    for j in range(n_cores):
        k.step("vector",
               lambda e, j=j: e.scalar_tensor_tensor(
                   out=b.tsc1[0], in0=b.gath[:, j * 512: j * 512 + 256],
                   scalar=b.m_col[:, j:j + 1],
                   in1=b.im_col[:, j:j + 1].to_broadcast([128, 256]),
                   op0=OP.mult, op1=OP.add),
               drain=True)
        k.step("vector",
               lambda e: e.tensor_tensor(out=b.hin[:], in0=b.hin[:],
                                         in1=b.tsc1[0], op=OP.mult),
               drain=True)
        k.step("vector",
               lambda e, j=j: e.tensor_scalar_mul(
                   b.tsc1[0], b.gath[:, j * 512 + 256:(j + 1) * 512],
                   b.m_col[:, j:j + 1]),
               drain=True)
        k.step("vector",
               lambda e: e.tensor_tensor(out=b.hin[:], in0=b.hin[:],
                                         in1=b.tsc1[0], op=OP.add),
               drain=True)
    b.v_hin = k.step("vector",
                     lambda e: e.tensor_copy(out=b.state[:], in_=b.hin[:]),
                     drain=True, inc=k.vec)
    b.v_tscfree = {0: b.v_hin, 1: b.v_hin}



def _slab(k, src_ap, width):
    b = k.b
    q = b.wq
    b.wq += 1
    par = q % 2
    wl = k.wload0 if par == 0 else k.wload1
    nd = len(src_ap.shape)
    dst = b.wslab[par][:, 0:width]
    if nd == 3:
        dst = dst.rearrange("p (a c) -> p a c", a=src_ap.shape[1])
    vw = k.step(
        "gpsimd",
        lambda e, src_ap=src_ap, dst=dst: e.dma_start(out=dst, in_=src_ap),
        waits=[(k.mm, b.slab_mm.get(q - 2, b.v_xproj_mm if q < 2 else 0))],
        inc=wl, inc_n=16)
    return q, par, wl, vw


def emit_gate(k):
    b = k.b
    ptile = b.tsc1[0][:, 0:16]
    negmax = b.tsc1[0][:, 32:33]
    rcp = b.tsc1[0][:, 34:35]
    m8 = b.tsc1[0][:, 40:48]
    mask = b.tsc1[0][:, 64:80]
    ctile = b.ctile[:]
    cstage = b.tsc2[0][0:16, 0:128]
    k.step("gpsimd",
           lambda e: e.dma_start(
               out=b.gslab[:].rearrange("p (kb e) -> p kb e", kb=16),
               in_=b.gatew_t.rearrange("(kb p) e -> p kb e", p=128)),
           inc=k.wdtw, inc_n=16)
    v_id = k.step("gpsimd",
                  lambda e: e.dma_start(out=b.ident_bf[:], in_=b.ident),
                  inc=k.wdtw, inc_n=16)
    v_gw = v_id
    v_cdram = b.v_cfm_dram = 0
    for tt in range(8):
        t0 = 3 + tt * 128
        vmm = k.step(
            "tensor",
            lambda e, tt=tt, t0=t0: [
                e.matmul(b.pb[4][:, 0:16],
                         b.xn[:, kb * TH + t0: kb * TH + t0 + 128],
                         b.gslab[:, kb * 16:(kb + 1) * 16],
                         start=(kb == 0), stop=(kb == 15))
                for kb in range(16)][-1],
            waits=([(k.wdtw, v_gw), (k.vec, b.v_dbc)]
                   if tt == 0 else [(k.vec, k.vec.v)]),
            inc=k.mm)
        k.step("vector",
               lambda e: e.tensor_reduce(out=negmax[:, :], in_=b.pb[4][:, 0:16],
                                         axis=AX.X, op=OP.max),
               waits=[(k.mm, vmm)], drain=True)
        k.step("vector",
               lambda e: e.tensor_scalar_mul(negmax, negmax, -1.0), drain=True)
        vneg = k.step("vector", lambda e: e.engine_nop(), drain=True, inc=k.vec)
        vexp = k.step("scalar",
                      lambda e: e.activation(out=ptile, in_=b.pb[4][:, 0:16],
                                             func=AF.Exp, bias=negmax),
                      waits=[(k.vec, vneg)], drain=True, inc=k.scl)
        k.step("vector",
               lambda e: e.tensor_reduce(out=rcp, in_=ptile, axis=AX.X, op=OP.add),
               waits=[(k.scl, vexp)], drain=True)
        k.step("vector", lambda e: e.reciprocal(rcp, rcp), drain=True)
        k.step("vector",
               lambda e: e.tensor_scalar_mul(ptile, ptile, rcp), drain=True)
        k.step("vector", lambda e: e.max(out=m8, in_=ptile), drain=True)
        k.step("vector",
               lambda e: e.tensor_scalar(out=mask, in0=ptile, scalar1=m8[:, 5:6],
                                         scalar2=None, op0=OP.is_ge),
               drain=True)
        vct = k.step("vector",
                     lambda e: e.tensor_tensor(out=ctile, in0=ptile, in1=mask,
                                               op=OP.mult),
                     drain=True, inc=k.vec)
        vtr = k.step("tensor",
                     lambda e: e.transpose(b.pb[5][0:16, 0:64].bitcast(BF16), ctile,
                                           b.ident_bf[:]),
                     waits=[(k.vec, vct), (k.wdtw, v_id),
                            (k.vec, b.v_ctr_prev)],
                     inc=k.mm)
        b.v_ctr_prev = k.step(
            "vector",
            lambda e: e.tensor_copy(out=cstage,
                                    in_=b.pb[5][0:16, 0:64].bitcast(BF16)),
            waits=[(k.mm, vtr), (k.dmas, v_cdram)], drain=True, inc=k.vec)
        v_cdram = k.step(
            "sync",
            lambda e, tt=tt: e.dma_start(
                out=b.combine_dram.ap()[:, tt * 128:(tt + 1) * 128], in_=cstage),
            waits=[(k.vec, b.v_ctr_prev)], inc=k.dmas, inc_n=16)
    b.v_cfm_dram = v_cdram


def emit_shared_up_phase(k, ph):
    b = k.b
    o0 = ph * 256
    grp = b.bgrp % 2
    b.bgrp += 1
    base = 4 * grp
    q, par, wl, vw = _slab(
        k, b.shw1_t.rearrange("(kb p) o -> p kb o", p=128)[:, :, o0:o0 + 256], 4096)
    for ob in range(2):
        for tti, tw0 in enumerate((3, 515)):
            bank = base + 2 * ob + tti
            for kb in range(16):
                wts = []
                if ob == 0 and tti == 0 and kb == 0:
                    wts = [(wl, vw),
                           (k.vec, max(b.grp_prev_vec[grp], b.v_acc_zero)),
                           (k.scl, b.grp_prev_scl[grp])]
                k.step(
                    "tensor",
                    lambda e, ob=ob, kb=kb, tw0=tw0, par=par, bank=bank: e.matmul(
                        b.pb[bank][:, 0:512],
                        b.wslab[par][:, kb * 256 + ob * 128: kb * 256 + ob * 128 + 128],
                        b.xn[:, kb * TH + tw0: kb * TH + tw0 + 512],
                        start=(kb == 0), stop=(kb == 15)),
                    waits=wts,
                    inc=(k.mm if (ob == 1 and tti == 1 and kb == 15) else None))
    b.moe_mm[q] = k.mm.v
    b.slab_mm[q] = k.mm.v
    for ob in range(2):
        o = o0 + ob * 128
        for tti in range(2):
            bank = base + 2 * ob + tti
            if o < IS:
                ib = o // 128
                k.step("scalar",
                       lambda e, ib=ib, tti=tti, bank=bank: e.activation(
                           out=b.sg_s[:, ib * 1024 + tti * 512: ib * 1024 + tti * 512 + 512],
                           in_=b.pb[bank][:, 0:512], func=AF.Silu),
                       waits=[(k.mm, b.moe_mm[q])],
                       inc=(k.scl if (tti == 1) else None))
            else:
                ib = (o - IS) // 128
                k.step("vector",
                       lambda e, ib=ib, tti=tti, bank=bank: e.tensor_tensor(
                           out=b.sg_s[:, ib * 1024 + tti * 512: ib * 1024 + tti * 512 + 512],
                           in0=b.sg_s[:, ib * 1024 + tti * 512: ib * 1024 + tti * 512 + 512],
                           in1=b.pb[bank][:, 0:512], op=OP.mult),
                       waits=[(k.mm, b.moe_mm[q]), (k.scl, b.v_shared_gate_done)],
                       drain=True,
                       inc=(k.vec if (tti == 1) else None))
    if o0 + 256 == IS:
        b.v_shared_gate_done = k.scl.v
    b.grp_prev_vec[grp] = k.vec.v
    b.grp_prev_scl[grp] = k.scl.v
    if ph == 2 * IS // 256 - 1:
        b.v_shared_act = k.vec.v


def emit_shared_down(k):
    b = k.b
    ISB = IS // 128
    for ph in range(HB):
        grp = b.bgrp % 2
        b.bgrp += 1
        base = 4 * grp
        q, par, wl, vw = _slab(
            k, b.shw2_t.rearrange("(kb p) o -> p kb o", p=128)[:, :, ph * 128:(ph + 1) * 128],
            ISB * 128)
        for tti in range(2):
            bank = base + tti
            for kb in range(ISB):
                wts = []
                if tti == 0 and kb == 0:
                    wts = [(wl, vw),
                           (k.vec, max(b.v_shared_act, b.grp_prev_vec[grp])),
                           (k.scl, b.grp_prev_scl[grp])]
                k.step(
                    "tensor",
                    lambda e, kb=kb, tti=tti, par=par, bank=bank: e.matmul(
                        b.pb[bank][:, 0:512],
                        b.wslab[par][:, kb * 128:(kb + 1) * 128],
                        b.sg_s[:, kb * 1024 + tti * 512: kb * 1024 + tti * 512 + 512],
                        start=(kb == 0), stop=(kb == ISB - 1)),
                    waits=wts,
                    inc=(k.mm if (tti == 1 and kb == ISB - 1) else None))
        b.moe_mm[q] = k.mm.v
        b.slab_mm[q] = k.mm.v
        for tti in range(2):
            bank = base + tti
            k.step("vector",
                   lambda e, ph=ph, tti=tti, bank=bank: e.tensor_tensor(
                       out=b.acc[:, ph * 1024 + tti * 512: ph * 1024 + tti * 512 + 512],
                       in0=b.acc[:, ph * 1024 + tti * 512: ph * 1024 + tti * 512 + 512],
                       in1=b.pb[bank][:, 0:512], op=OP.add),
                   waits=[(k.mm, b.moe_mm[q]), (k.vec, b.v_acc_zero)],
                   drain=True, inc=(k.vec if tti == 1 else None))
        b.grp_prev_vec[grp] = k.vec.v


def emit_experts(k):
    b = k.b
    IB = I_FF // 128
    for ex in range(E):
        par_e = ex % 2
        act = b.act[par_e]
        cb = b.cb[par_e]
        wcb = k.wcb0 if par_e == 0 else k.wcb1
        v_cb = k.step(
            "gpsimd",
            lambda e, ex=ex, cb=cb: e.dma_start(
                out=cb, in_=b.combine_dram.ap()[ex:ex + 1, :].partition_broadcast(128)),
            waits=[(k.dmas, b.v_cfm_dram), (k.vec, b.cb_free.get(par_e, 0))],
            inc=wcb, inc_n=16)
        for ph in range(IB):
            o0 = ph * 256
            grp = b.bgrp % 2
            b.bgrp += 1
            base = 4 * grp
            q, par, wl, vw = _slab(
                k, b.w1_t.rearrange("e (kb p) o -> e p kb o", p=128)[ex][:, :, o0:o0 + 256],
                4096)
            for ob in range(2):
                for tti, tw0 in enumerate((3, 515)):
                    bank = base + 2 * ob + tti
                    for kb in range(16):
                        wts = []
                        if ob == 0 and tti == 0 and kb == 0:
                            wts = [(wl, vw),
                                   (k.vec, b.grp_prev_vec[grp]),
                                   (k.scl, b.grp_prev_scl[grp])]
                        k.step(
                            "tensor",
                            lambda e, ob=ob, kb=kb, tw0=tw0, par=par, bank=bank: e.matmul(
                                b.pb[bank][:, 0:512],
                                b.wslab[par][:, kb * 256 + ob * 128: kb * 256 + ob * 128 + 128],
                                b.xn[:, kb * TH + tw0: kb * TH + tw0 + 512],
                                start=(kb == 0), stop=(kb == 15)),
                            waits=wts,
                            inc=(k.mm if (ob == 1 and tti == 1 and kb == 15) else None))
            b.moe_mm[q] = k.mm.v
            b.slab_mm[q] = k.mm.v
            for ob in range(2):
                o = o0 + ob * 128
                for tti in range(2):
                    bank = base + 2 * ob + tti
                    if o < I_FF:
                        ib = o // 128
                        k.step("scalar",
                               lambda e, ib=ib, tti=tti, bank=bank, act=act: e.activation(
                                   out=act[:, ib * 1024 + tti * 512: ib * 1024 + tti * 512 + 512],
                                   in_=b.pb[bank][:, 0:512], func=AF.Silu),
                               waits=[(k.mm, b.moe_mm[q]),
                                      (k.vec, b.act_free.get(par_e, 0))],
                               inc=(k.scl if tti == 1 else None))
                    else:
                        ib = (o - I_FF) // 128
                        sl = lambda act, ib, tti: act[:, ib * 1024 + tti * 512: ib * 1024 + tti * 512 + 512]
                        k.step("vector",
                               lambda e, ib=ib, tti=tti, bank=bank, act=act, sl=sl: e.tensor_tensor(
                                   out=sl(act, ib, tti), in0=sl(act, ib, tti),
                                   in1=b.pb[bank][:, 0:512], op=OP.mult),
                               waits=[(k.mm, b.moe_mm[q])],
                               drain=True)
                        k.step("vector",
                               lambda e, ib=ib, tti=tti, act=act, cb=cb, sl=sl: e.tensor_tensor(
                                   out=sl(act, ib, tti), in0=sl(act, ib, tti),
                                   in1=cb[:, tti * 512:(tti + 1) * 512], op=OP.mult),
                               waits=[(wcb, v_cb)], drain=True,
                               inc=(k.vec if (tti == 1 and ob == 1) else None))
            b.grp_prev_vec[grp] = k.vec.v
            b.grp_prev_scl[grp] = k.scl.v
        v_act = k.vec.v
        b.cb_free[par_e] = v_act
        for ph in range(HB):
            grp = b.bgrp % 2
            b.bgrp += 1
            base = 4 * grp
            q, par, wl, vw = _slab(
                k, b.w2_t.rearrange("e (kb p) o -> e p kb o", p=128)[ex][:, :, ph * 128:(ph + 1) * 128],
                IB * 128)
            for tti in range(2):
                bank = base + tti
                for kb in range(IB):
                    wts = []
                    if tti == 0 and kb == 0:
                        wts = [(wl, vw),
                               (k.vec, max(v_act, b.grp_prev_vec[grp])),
                               (k.scl, b.grp_prev_scl[grp])]
                    k.step(
                        "tensor",
                        lambda e, kb=kb, tti=tti, par=par, bank=bank, act=act: e.matmul(
                            b.pb[bank][:, 0:512],
                            b.wslab[par][:, kb * 128:(kb + 1) * 128],
                            act[:, kb * 1024 + tti * 512: kb * 1024 + tti * 512 + 512],
                            start=(kb == 0), stop=(kb == IB - 1)),
                        waits=wts,
                        inc=(k.mm if (tti == 1 and kb == IB - 1) else None))
            b.moe_mm[q] = k.mm.v
            b.slab_mm[q] = k.mm.v
            for tti in range(2):
                bank = base + tti
                k.step("vector",
                       lambda e, ph=ph, tti=tti, bank=bank: e.tensor_tensor(
                           out=b.acc[:, ph * 1024 + tti * 512: ph * 1024 + tti * 512 + 512],
                           in0=b.acc[:, ph * 1024 + tti * 512: ph * 1024 + tti * 512 + 512],
                           in1=b.pb[bank][:, 0:512], op=OP.add),
                       waits=[(k.mm, b.moe_mm[q])], drain=True,
                       inc=(k.vec if tti == 1 else None))
            b.grp_prev_vec[grp] = k.vec.v
        b.act_free[par_e] = k.vec.v


def emit_moe_tail(k):
    b = k.b
    b.v_acc_zero = k.step(
        "vector", lambda e: e.memset(b.acc[:], 0.0),
        waits=[(k.vec, k.vec.v), (k.acc0, k.acc0.v), (k.acc1, k.acc1.v)],
        drain=True, inc=k.vec)
    for ph in range(2 * IS // 256):
        emit_shared_up_phase(k, ph)
    emit_shared_down(k)
    emit_experts(k)
    vflush_gate = [(k.vec, k.vec.v), (k.acc0, k.acc0.v), (k.acc1, k.acc1.v),
                   (k.st0, b.v_out_init_st0)]
    for f in range(HB):
        ssem = k.acc0 if f % 2 == 0 else k.acc1
        k.step("gpsimd",
               lambda e, f=f: e.dma_start(
                   out=b.out_fm[ts(f), :], in_=b.acc[:, f * 1024:(f + 1) * 1024],
                   accum_op=OP.add),
               waits=(vflush_gate if f == 0 else ()), inc=ssem, inc_n=16)


def emit_debug(k):
    if not DEBUG:
        return
    b = k.b
    if not DEBUG_FRONT:
        k.step("sync",
               lambda e: e.dma_start(
                   out=b.xn_dbg.rearrange("(f p) t -> p f t", p=128),
                   in_=b.xn[:].rearrange("p (f t) -> p f t", f=HB)),
               waits=[(k.vec, b.v_xn)], inc=k.st2, inc_n=16)
        return
    k.step("sync",
           lambda e: e.dma_start(
               out=b.xn_dbg.rearrange("(f p) t -> p f t", p=128),
               in_=b.xn[:].rearrange("p (f t) -> p f t", f=HB)),
           waits=[(k.vec, b.v_xn)], inc=k.st2, inc_n=16)
    k.step("sync",
           lambda e: e.dma_start(
               out=b.z_dbg.rearrange("(f p) t -> p f t", p=128),
               in_=b.z[:].rearrange("p (f t) -> p f t", f=HB)),
           waits=[(k.vec, b.v_inproj_done)], inc=k.st2, inc_n=16)
    k.step("sync",
           lambda e: e.dma_start(
               out=b.xc_dbg.rearrange("(f p) t -> p f t", p=128),
               in_=b.xc[:].rearrange("p (f t) -> p f t", f=HB)),
           waits=[(k.scl, b.v_conv_done)], inc=k.st2, inc_n=16)
    k.step("sync",
           lambda e: e.dma_start(out=b.dbc_dbg, in_=b.dbc[0:96, :]),
           waits=[(k.vec, b.v_dbc)], inc=k.st2, inc_n=16)
    k.step("sync",
           lambda e: e.dma_start(out=b.st_dbg[:, 0:512], in_=b.rhs[:, 0:1024]
                                 .bitcast(F32)),
           waits=[(k.vec, k.vec.v)], inc=k.st2, inc_n=16)


def program(k, n_cores=8):
    b = k.b
    b.vscl_bank = {}
    b.vscl_exp = 0
    b.vv_bmfree = 0
    b.v_tscfree = 0
    b.op_mm = {}
    b.op_evac = {}
    b.opq = 0
    b.wq = 0
    b.slab_mm = {}
    b.shared_ph_iter = None
    b.v_ctr_prev = 0
    b.v_shared_act = 0
    b.shared_fed = 0
    b.frq = 0
    b.fr_prev_vec = {0: 0, 1: 0}
    b.fr_prev_scl = {0: 0, 1: 0}
    b.bgrp = 0
    b.grp_prev_vec = {0: 0, 1: 0}
    b.grp_prev_scl = {0: 0, 1: 0}
    b.moe_mm = {}
    b.moe_evac = {}
    b.moe_sevac = {}
    b.cb_free = {}
    b.act_free = {}
    b.v_scans = 0
    b.v_dtpb_free = 0
    b.v_oppb_free = 0
    b.v_abprod = {}
    b.v_abfree = {}
    b.v_bbfree = 0
    b.v_dsub_free_vec = 0
    b.v_tscfree = {}
    b.eg_gate_done = {}
    b.v_shared_gate_done = 0
    emit_consts(k)
    emit_ln(k)
    emit_r2(k)
    emit_inproj(k)
    emit_conv(k)
    emit_xproj(k)
    emit_gate(k)
    emit_scan_pass(k, is_c=False)
    emit_gather(k, n_cores)
    emit_scan_pass(k, is_c=True)
    emit_moe_tail(k)
    emit_debug(k)
    waits = [(s, s.v) for s in (k.st0, k.st1, k.st2, k.st3, k.acc0, k.acc1)]
    k.step("sync", lambda e: e.nop(), waits=waits)



class Bufs:
    pass


def build_nc(n_cores=8):
    nc = bass.Bass("TRN2", target_bir_lowering=False)
    b = Bufs()
    b.x_fm = nc.dram_tensor("x_fm", [H, TH], F32, kind="ExternalInput").ap()
    b.inw_t = nc.dram_tensor("inw_t", [H, 2 * H], F32, kind="ExternalInput").ap()
    b.convw = nc.dram_tensor("convw", [4, H, H], F32, kind="ExternalInput").ap()
    b.xpw_t = nc.dram_tensor("xpw_t", [H, 96], F32, kind="ExternalInput").ap()
    b.dtw_t = nc.dram_tensor("dtw_t", [DTR, H], F32, kind="ExternalInput").ap()
    b.dtb_row = nc.dram_tensor("dtb_row", [1, H], F32, kind="ExternalInput").ap()
    b.outw_t = nc.dram_tensor("outw_t", [H, H], F32, kind="ExternalInput").ap()
    b.nw_in = nc.dram_tensor("nw_in", [128, HB], F32, kind="ExternalInput").ap()
    b.convb_in = nc.dram_tensor("convb_in", [128, HB], F32, kind="ExternalInput").ap()
    b.dtb_in = nc.dram_tensor("dtb_in", [128, HB], F32, kind="ExternalInput").ap()
    b.d_in = nc.dram_tensor("d_in", [128, HB], F32, kind="ExternalInput").ap()
    b.alog_in = nc.dram_tensor("alog_in", [128, HB * NST], F32, kind="ExternalInput").ap()
    b.mask_in = nc.dram_tensor("mask_in", [128, 8], F32, kind="ExternalInput").ap()
    b.imask_in = nc.dram_tensor("imask_in", [128, 8], F32, kind="ExternalInput").ap()
    b.gatew_t = nc.dram_tensor("gatew_t", [H, E], F32, kind="ExternalInput").ap()
    b.shw1_t = nc.dram_tensor("shw1_t", [H, 2 * IS], F32, kind="ExternalInput").ap()
    b.shw2_t = nc.dram_tensor("shw2_t", [IS, H], F32, kind="ExternalInput").ap()
    b.w1_t = nc.dram_tensor("w1_t", [E, H, 2 * I_FF], F32, kind="ExternalInput").ap()
    b.w2_t = nc.dram_tensor("w2_t", [E, I_FF, H], F32, kind="ExternalInput").ap()
    b.ident = nc.dram_tensor("ident", [128, 128], F32, kind="ExternalInput").ap()
    b.combine_dram = nc.dram_tensor("combine_dram", [E, T], F32)
    b.out_fm = nc.dram_tensor("out_fm", [H, T], F32, kind="ExternalOutput").ap()
    b.rows_dram = nc.dram_tensor("rows_dram", [4, TH], F32).ap()
    b.dbc_dram = nc.dram_tensor("dbc_dram", [96, T], BF16)
    b.ag_in = nc.dram_tensor("ag_in", [128, 512], F32)
    b.ag_out = nc.dram_tensor("ag_out", [n_cores * 128, 512], F32,
                              addr_space="Shared" if n_cores > 1 else "Local")
    if DEBUG:
        b.xn_dbg = nc.dram_tensor("xn_dbg", [H, TH], BF16, kind="ExternalOutput").ap()
        b.z_dbg = nc.dram_tensor("z_dbg", [H, TH], BF16, kind="ExternalOutput").ap()
        b.xc_dbg = nc.dram_tensor("xc_dbg", [H, T], BF16, kind="ExternalOutput").ap()
        b.dbc_dbg = nc.dram_tensor("dbc_dbg", [96, T], BF16, kind="ExternalOutput").ap()
        b.st_dbg = nc.dram_tensor("st_dbg", [128, 1024], F32, kind="ExternalOutput").ap()

    from contextlib import ExitStack
    ctx = ExitStack()
    S = lambda name, shape, dt: ctx.enter_context(nc.sbuf_tensor(name, shape, dt))
    b.xblk = [S(f"xblk{i}", [128, TH], F32) for i in range(2)]
    b.xn = S("xn", [128, HB * TH], BF16)
    b.sq = S("sq", [128, TH], F32)
    b.ones = S("ones", [128, 4], F32)
    b.dbc = S("dbc", [128, T], BF16)
    b.wslab = [S(f"wslab{i}", [128, 4096], BF16) for i in range(2)]
    b.dtw_slab = S("dtw_slab", [64, H], BF16)
    b.A_sbuf = S("A_sbuf", [128, HB * NST], F32)
    b.nw_col = S("nw_col", [128, HB], F32)
    b.convb_col = S("convb_col", [128, HB], F32)
    b.dtb_col = S("dtb_col", [128, HB], F32)
    b.d_col = S("d_col", [128, HB], F32)
    b.m_col = S("m_col", [128, 8], F32)
    b.im_col = S("im_col", [128, 8], F32)

    b.state = b.xblk[0][:, 0:256]
    b.hin = b.xblk[0][:, 256:512]
    b.y_hh = b.xblk[0][:, 512:768]
    b.aprod = b.xblk[0][:, 768:1024]
    b.sumd = b.y_hh[:, 0:16]
    b.tsc1 = [b.xblk[1][:, 0:256], b.xblk[1][:, 512:768]]
    b.tsc2 = [b.xblk[1][:, 256:512], b.xblk[1][:, 768:1024]]
    b.t1p = [b.xblk[1][:, 0:512], b.xblk[1][:, 512:1024]]
    b.ones_row = b.sq[:, 896:1024].bitcast(BF16)[0:1, :]

    NA = 35907
    b.arena = S("arena", [128, NA], F32)
    A = b.arena
    b.rstd_b = A[:, 0:1027]
    b.mr_b = A[:, 1027:2054]
    b.row_mean = A[0:1, 2054:3081]
    b.row_var = A[0:1, 3081:4108]
    b.row_rstd = A[0:1, 4108:5135]
    b.row_mr = A[0:1, 5135:6162]
    b.xc_pre = A[:, 0:8224].bitcast(BF16)[:, 0:HB * TH]
    b.sq2 = A[:, 6162:7189]
    b.z = A[:, 8224:16448].bitcast(BF16)[:, 0:HB * TH]
    b.xc = A[:, 16448:24640].bitcast(BF16)
    b.r2_b = A[:, 24640:25667]
    b.d_sub = A[:, 0:2048].bitcast(BF16)
    b.ab = [A[:, 2048:4096].bitcast(BF16), A[:, 25667:27715].bitcast(BF16)]
    b.dxc = A[:, 4096:6144].bitcast(BF16)
    b.gath = A[:, 2048:6144]
    b.bb = A[:, 6144:8192].bitcast(BF16)
    b.utmp_all = A[:, 27715:29763].bitcast(BF16)
    b.dtb_row_s = A[:, 24640:25664].bitcast(BF16)[0:1, :]
    b.bm_rep = A[:, 29763:31811].bitcast(BF16)
    b.c_rep = A[:, 31811:33859].bitcast(BF16)
    b.rhs = A[:, 33859:35907].bitcast(BF16)
    b.acc = A[:, 0:16384]
    b.act = [A[:, 16384:22016].bitcast(BF16),
             A[:, 22016:27648].bitcast(BF16)]
    b.sg_s = A[:, 16384:27648].bitcast(BF16)[:, 0:22 * 1024]
    b.cb = [A[:, 27648:28160].bitcast(BF16),
            A[:, 28160:28672].bitcast(BF16)]
    b.gslab = A[:, 28672:28800].bitcast(BF16)
    b.ident_bf = A[:, 28800:28864].bitcast(BF16)
    b.ctile = A[:, 28864:28872].bitcast(BF16)
    b.cfm = A[0:16, 28872:29896]

    b.pb = [ctx.enter_context(nc.psum_tensor(f"pb{i}", [128, 512], F32))
            for i in range(8)]

    sem_names = ["xload", "xl0", "xl1", "wload0", "wload1", "mm", "vec", "scl",
                 "dmah", "dmas", "cc", "st0", "st1", "st2", "st3", "acc0", "acc1", "wdtw", "wgath", "wcb0", "wcb1", "chain_vector", "chain_scalar"]
    sems = {}
    for name in sem_names:
        sems[name] = ctx.enter_context(nc.semaphore(name))

    k = K(nc, b, sems)
    k.start_scan_pass()
    program(k, n_cores=n_cores)
    k.end_pass()
    k.plan()
    with ctx, nc.Block() as block:
        for eng_name in ENGS:
            def section(eng, eng_name=eng_name):
                k.start_pass(eng_name, eng)
                program(k, n_cores=n_cores)
                k.end_pass()
            getattr(block, eng_name)(section)
    return nc, k


def _colify(v):
    return np.ascontiguousarray(v.reshape(HB, 128).T)


def host_inputs(inputs):
    w = {}
    w["inw_t"] = np.ascontiguousarray(np.asarray(inputs["L0_attn_in_proj_w"]).T)
    w["convw"] = np.ascontiguousarray(np.asarray(inputs["L0_attn_conv_w"]))
    w["xpw_t"] = np.ascontiguousarray(np.asarray(inputs["L0_attn_x_proj_w"]).T)
    w["nw_in"] = _colify(np.asarray(inputs["L0_attn_norm_w"]))
    w["convb_in"] = _colify(np.asarray(inputs["L0_attn_conv_b"]))
    w["dtw_t"] = np.ascontiguousarray(np.asarray(inputs["L0_attn_dt_proj_w"]).T)
    w["dtb_row"] = np.ascontiguousarray(
        np.asarray(inputs["L0_attn_dt_proj_b"])[None, :])
    w["outw_t"] = np.ascontiguousarray(np.asarray(inputs["L0_attn_out_proj_w"]).T)
    w["dtb_in"] = _colify(np.asarray(inputs["L0_attn_dt_proj_b"]))
    w["d_in"] = _colify(np.asarray(inputs["L0_attn_D"]))
    al = np.asarray(inputs["L0_attn_A_log"])
    w["alog_in"] = np.ascontiguousarray(
        al.reshape(HB, 128, NST).transpose(1, 0, 2).reshape(128, HB * NST))
    w["gatew_t"] = np.ascontiguousarray(np.asarray(inputs["L0_ffn_gate_w"]).T)
    w["shw1_t"] = np.ascontiguousarray(np.asarray(inputs["L0_ffn_shared_w1"]).T)
    w["shw2_t"] = np.ascontiguousarray(np.asarray(inputs["L0_ffn_shared_w2"]).T)
    w["w1_t"] = np.ascontiguousarray(np.asarray(inputs["L0_ffn_w1"]).transpose(0, 2, 1))
    w["w2_t"] = np.ascontiguousarray(np.asarray(inputs["L0_ffn_w2"]).transpose(0, 2, 1))
    w["ident"] = np.eye(128, dtype=np.float32)
    return w


def kernel(**inputs):
    x = np.asarray(inputs["L0_attn_x"])
    nc, _k = build_nc()
    shared = host_inputs(inputs)
    in_maps = []
    for c in range(8):
        seq, pos = divmod(c, 4)
        t0 = pos * T
        xc = x[seq, t0:t0 + T, :]
        halo = x[seq, t0 - 3:t0, :] if pos > 0 else np.zeros((3, H), np.float32)
        xf = np.ascontiguousarray(np.concatenate([halo, xc], axis=0).T)
        m = np.zeros(8, np.float32)
        for j in range(8):
            if j // 4 == seq and j % 4 < pos:
                m[j] = 1.0
        mask = np.repeat(m[None, :], 128, axis=0).astype(np.float32)
        in_maps.append({"x_fm": xf, "mask_in": mask,
                        "imask_in": np.ascontiguousarray(1.0 - mask), **shared})
    res = run_bass_kernel_spmd(nc, in_maps, core_ids=list(range(8)))
    outs = [res.results[c]["out_fm"].T for c in range(8)]
    full = np.concatenate(outs, axis=0).reshape(2, 4096, H)
    if DEBUG:
        kernel._dbg = res.results
    return full.astype(np.float32)


if __name__ == "__main__":
    rng = np.random.default_rng(0)
    x = rng.standard_normal((2, 4096, H), dtype=np.float32)
    out = kernel(L0_attn_x=x)
    m = x.mean(-1, keepdims=True)
    v = x.var(-1, keepdims=True)
    xn = (x - m) / np.sqrt(v + EPS)
    ref = x + xn
    err = np.abs(out - ref).max()
    print("out err vs x+xn:", err)
    if DEBUG:
        xn_dbg = kernel._dbg[0]["xn_dbg"].T
        ref_xn = xn[0, :T]
        got = xn_dbg[3:, :]
        print("xn err:", np.abs(got - ref_xn).max())
